# version 11
# baseline (speedup 1.0000x reference)
import numpy as np

import math
from contextlib import ExitStack

import concourse.bass as bass
import concourse.mybir as mybir
from concourse.aot_env import allow_ici_all_to_all

F32 = mybir.dt.float32
BF16 = mybir.dt.bfloat16
U8 = mybir.dt.uint8
AF = mybir.ActivationFunctionType
ALU = mybir.AluOpType
AX = mybir.AxisListType

ENGS = ("sp", "act", "dve", "pe", "pool")
SELF_CHAIN = ("act", "dve", "pool")


class Cfg:
    def __init__(self, d_model=4096, seq=4096, batch=4, ffn_int=8192,
                 ffn_lat=2048, debug_taps=False, stop_after=None,
                 relax=True):
        self.stop_after = stop_after
        self.relax = relax
        self.d_model = d_model
        self.seq = seq
        self.batch = batch
        self.ffn_int = ffn_int
        self.ffn_lat = ffn_lat
        self.debug_taps = debug_taps
        self.d_state = 16
        self.dt_rank = 128
        self.heads = 32
        self.dn = 128
        self.dv = 128
        self.dq = 192
        self.n_exp = 8
        self.ln_eps = 1e-5
        assert batch == 4
        self.tokens = batch * seq
        self.tpc = self.tokens // 8
        self.dsl = d_model // 8
        self.ndt = self.dsl // 128
        self.ndc = d_model // 128
        self.th = seq // 2
        assert self.th == self.tpc
        self.hpc = self.heads // 2
        self.hbc = self.heads // 8
        self.nec = self.heads * self.dv // 128
        self.sblk = min(512, self.tpc)
        self.nsblk = self.tpc // self.sblk
        self.sw = min(512, self.tpc)
        self.nst = self.tpc // self.sw
        self.qmac = min(512, seq // 2)
        self.nqm = seq // self.qmac
        self.ktpq = self.qmac // 128
        self.tb = min(512, self.th)
        self.ntb = seq // self.tb
        self.nic = ffn_int // 128
        self.nlc = ffn_lat // 128
        self.nxp = self.dt_rank + 2 * self.d_state
        self.nrc = math.ceil(self.nxp / 128)


class Sem:
    def __init__(self, nc, es, name):
        self.h = es.enter_context(nc.semaphore(name))
        self.n = 0


class Sched:
    def __init__(self, nc, es):
        self.nc = nc
        self.es = es
        self.ops = {e: [] for e in ENGS}
        self.sems = {}
        self.waited = {}
        self.last_wide = {}
        self.relax = False
        for e in SELF_CHAIN:
            self.sems["self_" + e] = Sem(nc, es, "self_" + e)

    def sem(self, name):
        if name not in self.sems:
            self.sems[name] = Sem(self.nc, self.es, name)
        return self.sems[name]

    def wait(self, eng, sem, val=None):
        v = sem.n if val is None else val
        key = (eng, id(sem))
        if v <= 0 or self.waited.get(key, -1) >= v:
            return
        self.waited[key] = v
        self.ops[eng].append(lambda e, s=sem.h, v=v: e.wait_ge(s, v))

    def op(self, eng, fn, incs=(), chain=True, wide=False):
        incs = list(incs)
        if eng in SELF_CHAIN and not chain:
            ss = self.sems["self_" + eng]
            self.wait(eng, ss, ss.n)
        if eng in SELF_CHAIN and chain:
            ss = self.sems["self_" + eng]
            if not (self.relax and eng == "dve" and wide
                    and self.last_wide.get(eng, False)):
                self.wait(eng, ss, ss.n)
            self.last_wide[eng] = wide
            incs.append((ss, 1))
        for s, k in incs:
            s.n += k

        def wrapped(e, fn=fn, incs=tuple(incs)):
            ins = fn(e)
            for s, k in incs:
                ins = ins.then_inc(s.h, k)
            return ins
        self.ops[eng].append(wrapped)

    def dma(self, eng, out, in_, key):
        if eng in SELF_CHAIN:
            self.last_wide[eng] = False
        sem = self.sem("dma_" + key)
        sem.n += 16

        def wrapped(e, out=out, in_=in_, s=sem.h):
            return e.dma_start(out=out, in_=in_).then_inc(s, 16)
        self.ops[eng].append(wrapped)

    def wdma(self, eng, *keys):
        for key in keys:
            if ("dma_" + key) in self.sems:
                self.wait(eng, self.sems["dma_" + key])

    def barrier(self):
        bar = self.sem("bar")
        for e in ENGS:
            bar.n += 1
            self.ops[e].append(lambda en, s=bar.h: en.nop().then_inc(s, 1))
        tgt = bar.n
        for e in ENGS:
            self.ops[e].append(lambda en, s=bar.h, v=tgt: en.wait_ge(s, v))

    def replay(self, block):
        sched = self

        @block.sync
        def _(e):
            for f in sched.ops["sp"]:
                f(e)

        @block.scalar
        def _(e):
            for f in sched.ops["act"]:
                f(e)

        @block.vector
        def _(e):
            for f in sched.ops["dve"]:
                f(e)

        @block.tensor
        def _(e):
            for f in sched.ops["pe"]:
                f(e)

        @block.gpsimd
        def _(e):
            for f in sched.ops["pool"]:
                f(e)


class Arena:
    def __init__(self, nc, es, cols_bf16):
        self.guard_h = es.enter_context(
            nc.sbuf_tensor("arena", [128, cols_bf16], BF16))
        self.hb = self.guard_h
        self.h32 = self.hb.bitcast(F32)
        self.cols = cols_bf16
        self.off = 0
        self.marks = []

    def alloc(self, dtype, cols, rows=128):
        e2 = cols * (2 if dtype == F32 else 1)
        self.off = (self.off + 31) & ~31
        assert self.off + e2 <= self.cols, \
            f"arena overflow {self.off}+{e2}>{self.cols}"
        o = self.off
        self.off += e2
        if dtype == BF16:
            return self.hb[:rows, o:o + cols]
        return self.h32[:rows, o // 2:o // 2 + cols]

    def mark(self):
        self.marks.append(self.off)

    def reset(self):
        self.off = self.marks.pop()


def build(cfg: Cfg):
    c = cfg
    nc = bass.Bass("TRN2", target_bir_lowering=False)

    def din(name, shape, dt=F32):
        return nc.dram_tensor(name, shape, dt, kind="ExternalInput").ap()

    xT_t = din("xT_t", [c.d_model, c.tpc])
    xT_d = din("xT_d", [c.dsl, c.tokens])
    A_log_sl = din("A_log_sl", [128, c.ndt * c.d_state])
    D_sl = din("D_sl", [128, c.ndt])
    xpw = din("xpw", [c.nxp, c.d_model])
    dtw_sl = din("dtw_sl", [128, c.ndt * 128])
    dtb_sl = din("dtb_sl", [128, c.ndt])
    wqkv_u = din("wqkv_u", [c.d_model, 3 * c.heads * 128])
    wo = din("wo", [c.heads * c.dv, c.d_model])
    gate_wT = din("gate_wT", [c.n_exp, c.d_model])
    up_w = din("up_w", [c.d_model, c.ffn_int])
    down_w = din("down_w", [c.ffn_int, c.d_model])
    fc1_w = din("fc1_w", [c.d_model, c.ffn_lat])
    fc2_w = din("fc2_w", [c.ffn_lat, c.d_model])
    exp_w = din("exp_w", [c.ffn_lat, c.ffn_lat])
    ln_g = din("ln_g", [128, c.ndc])
    ln_b = din("ln_b", [128, c.ndc])
    ident = din("ident", [128, 128], F32)
    ones_b = din("ones_b", [128, 1], BF16)
    ones_f = din("ones_f", [128, 1], F32)
    onesr_f = din("onesr_f", [1, 128], F32)
    masks = din("masks", [128, c.ktpq * c.qmac], BF16)

    out_xT = nc.dram_tensor("out_xT", [c.d_model, c.tpc], F32,
                            kind="ExternalOutput").ap()

    def scr(name, shape, dt):
        return nc.dram_tensor(name, shape, dt).ap()

    nxs = c.nxp + 2
    xp_in = scr("xp_in", [nxs, c.tpc], F32)
    xp_out = scr("xp_out", [8 * nxs, c.tpc], F32)
    bcrep_d = scr("bcrep_d", [8, 128, 2 * c.d_state * c.th], BF16)
    y_in = scr("y_in", [c.d_model, c.tpc], F32)
    y_out = scr("y_out", [c.d_model, c.tpc], F32)
    z3_in = scr("z3_in", [c.nsblk, 128, c.ndc * c.sblk], BF16)
    qkv_in = scr("qkv_in", [8 * 3 * c.hbc * 128, c.tpc], BF16)
    qkv_out = scr("qkv_out", [8 * 3 * c.hbc * 128, c.tpc], BF16)
    at_in = scr("at_in", [8 * c.hbc * 128, c.tpc], BF16)
    at_out = scr("at_out", [8 * c.hbc * 128, c.tpc], BF16)
    x3_d = scr("x3_d", [c.d_model, c.tpc], F32)
    x4_d = scr("x4_d", [c.d_model, c.tpc], F32)
    facc_d = scr("facc_d", [c.d_model, 2 * c.sblk], F32)
    up_bf = scr("up_bf", [c.d_model, c.ffn_int], BF16)
    down_bf = scr("down_bf", [c.ffn_int, c.d_model], BF16)
    fc1_bf = scr("fc1_bf", [c.d_model, c.ffn_lat], BF16)
    fc2_bf = scr("fc2_bf", [c.ffn_lat, c.d_model], BF16)
    exp_bf = scr("exp_bf", [c.ffn_lat, c.ffn_lat], BF16)

    def pb_meta(rows, cols):
        kc, ncc = rows // 128, cols // 128
        oct_ = min(8, kc)
        return kc, ncc, oct_, ncc // 2, kc // oct_

    PBW = {}
    for nm, (wbf, rows, cols) in {
        "up": (up_bf, c.d_model, c.ffn_int),
        "down": (down_bf, c.ffn_int, c.d_model),
        "fc1": (fc1_bf, c.d_model, c.ffn_lat),
        "fc2": (fc2_bf, c.ffn_lat, c.d_model),
        "exp": (exp_bf, c.ffn_lat, c.ffn_lat),
    }.items():
        kc, ncc, oct_, npj, nkq = pb_meta(rows, cols)
        pb = scr(nm + "_pb", [npj * nkq, 128, oct_ * 256], BF16)
        PBW[nm] = (wbf, pb, kc, ncc, oct_, npj, nkq, cols)

    taps = {}
    if c.debug_taps:
        for nm, shape, dt in [
            ("t_xp", [8 * nxs, c.tpc], F32),
            ("t_x2", [c.d_model, c.tpc], F32),
            ("t_x3", [c.d_model, c.tpc], F32),
            ("t_z3", [c.d_model, c.tpc], BF16),
            ("t_at", [8 * c.hbc * 128, c.tpc], BF16),
            ("t_x4", [c.d_model, c.tpc], F32),
        ]:
            taps[nm] = nc.dram_tensor(nm, shape, dt,
                                      kind="ExternalOutput").ap()

    with ExitStack() as es:
        sc = Sched(nc, es)
        sc.relax = getattr(cfg, "relax", False)
        E = lambda *a: es.enter_context(nc.sbuf_tensor(*a))

        id32 = E("id32", [128, 128], F32)
        onesb_sb = E("onesb_sb", [128, 1], BF16)
        onesf_sb = E("onesf_sb", [128, 1], F32)
        onesr_sb = E("onesr_sb", [1, 128], F32)
        onesrb_sb = E("onesrb_sb", [1, 128], BF16)
        idb_sb = E("idb_sb", [128, 128], BF16)
        masks_sb = E("masks_sb", [128, c.ktpq * c.qmac], BF16)
        A_sb = E("A_sb", [128, c.ndt * c.d_state], F32)
        D_sb = E("D_sb", [128, c.ndt], F32)
        dtb_sb = E("dtb_sb", [128, c.ndt], F32)
        dtwT_sb = E("dtwT_sb", [128, c.ndt * 128], BF16)
        xpwT_sb = E("xpwT_sb", [128, c.ndc * c.nrc * 128], BF16)
        ncs_sb = E("ncs_sb", [128, c.nrc], F32)
        gwT_sb = E("gwT_sb", [128, c.ndc * c.n_exp], BF16)
        lng_sb = E("lng_sb", [128, c.ndc], F32)
        lnb_sb = E("lnb_sb", [128, c.ndc], F32)
        carry_sb = E("carry_sb", [128, max(c.ndt * c.d_state, 16)], F32)
        eps1_sb = E("eps1_sb", [128, 1], F32)
        one1_sb = E("one1_sb", [128, 1], F32)
        mu_row = E("mu_row", [1, c.sw], F32)
        rs_row = E("rs_row", [1, c.sw], F32)
        ms_row = E("ms_row", [1, c.sw], F32)
        z_row = E("z_row", [1, c.qmac], F32)
        m_row = E("m_row", [1, c.sblk], F32)
        msct = E("msct", [max(c.sblk // 128, 1), 128], F32)
        eqm = E("eqm", [128, c.n_exp], U8)

        AR = Arena(nc, es, 88064)

        PS = es.enter_context(nc.psum_tensor("PS", [128, 4096], F32))

        def bank(i, rows=128, w=512):
            return PS[:rows, 512 * i:512 * i + w]

        LD = sc.sem("ld")
        CV = sc.sem("cv")
        PE = sc.sem("pe")
        ST = sc.sem("st")
        CONV = sc.sem("conv")
        CC0, CC1, CC2, CC3 = (sc.sem(f"cc{i}") for i in range(4))
        SDVE = sc.sems["self_dve"]
        SACT = sc.sems["self_act"]
        SPOOL = sc.sems["self_pool"]

        sc.dma("sp", id32[:], ident, "p0")
        sc.dma("sp", onesb_sb[:], ones_b, "p0")
        sc.dma("sp", onesf_sb[:], ones_f, "p0")
        sc.dma("sp", onesr_sb[:], onesr_f, "p0")
        sc.dma("sp", masks_sb[:], masks, "p0")
        sc.dma("sp", D_sb[:], D_sl, "p0")
        sc.dma("sp", dtb_sb[:], dtb_sl, "p0")
        sc.dma("sp", lng_sb[:], ln_g, "p0")
        sc.dma("sp", lnb_sb[:], ln_b, "p0")

        AR.mark()
        tmpA = AR.alloc(F32, c.ndt * c.d_state)
        stage = AR.alloc(F32, max(c.ndt * 128, c.d_model))

        sc.dma("sp", tmpA, A_log_sl, "p0")
        sc.op("dve", lambda e: e.memset(eps1_sb[:], float(c.ln_eps)))
        sc.wait("dve", LD, LD.n) if False else None
        sc.op("dve", lambda e: e.memset(onesrb_sb[:], 1.0))
        sc.wdma("dve", "p0")
        sc.op("dve", lambda e: e.tensor_copy(idb_sb[:], id32[:]))
        sc.op("dve", lambda e: e.memset(one1_sb[:], 1.0))
        sc.wdma("act", "p0")
        sc.op("act", lambda e: e.activation(tmpA, tmpA, AF.Exp))
        sc.wait("dve", SACT, SACT.n)
        sc.op("dve", lambda e: e.tensor_scalar_mul(A_sb[:], tmpA, -1.0))

        ntr = [0]

        def tr128(src, dst, ocols):
            bi = 6 + ntr[0] % 2
            sc.wait("pe", SACT, SACT.n)
            sc.op("pe", lambda e, s=src, bi=bi, oc=ocols: e.transpose(
                PS[:, bi * 512:bi * 512 + oc], s, id32[:oc, :oc]
                if False else id32[:s.shape[0], :s.shape[0]]),
                incs=((PE, 1),))
            sc.wait("act", PE, PE.n)
            sc.op("act", lambda e, bi=bi, d=dst, oc=ocols: e.activation(
                d, PS[:d.shape[0], bi * 512:bi * 512 + oc], AF.Copy))
            ntr[0] += 1

        sc.wdma("pe", "p0")
        sc.dma("sp", stage[:, :c.ndt * 128], dtw_sl, "p0s")
        sc.wdma("pe", "p0s")
        for a in range(c.ndt):
            tr128(stage[:, a * 128:(a + 1) * 128],
                  dtwT_sb[:, a * 128:(a + 1) * 128], 128)

        for j in range(c.nrc):
            r0 = 128 * j
            rw = min(128 * (j + 1), c.nxp) - r0
            sc.wait("sp", SACT, SACT.n)
            sc.dma("sp", stage[:rw, :c.d_model], xpw[r0:r0 + rw, :], "p0s")
            sc.wdma("pe", "p0s")
            for k in range(c.ndc):
                tr128(stage[:rw, k * 128:(k + 1) * 128],
                      xpwT_sb[:, (k * c.nrc + j) * 128:
                              (k * c.nrc + j) * 128 + rw], rw)
            sc.wait("pe", SACT, SACT.n)
            for k in range(c.ndc):
                o = (k * c.nrc + j) * 128
                sc.op("pe", lambda e, o=o, rw=rw, k=k: e.matmul(
                    PS[:rw, :1], xpwT_sb[:, o:o + rw], onesb_sb[:],
                    start=(k == 0), stop=(k == c.ndc - 1)),
                    incs=((PE, 1),) if k == c.ndc - 1 else ())
            sc.wait("dve", PE, PE.n)
            sc.op("dve", lambda e, j=j, rw=rw: e.tensor_scalar_mul(
                ncs_sb[:rw, j:j + 1], PS[:rw, :1], -1.0))
            sc.wait("pe", SDVE, SDVE.n)

        sc.wait("sp", SACT, SACT.n)
        sc.dma("sp", stage[:c.n_exp, :c.d_model], gate_wT, "p0s")
        sc.wdma("pe", "p0s")
        for k in range(c.ndc):
            tr128(stage[:c.n_exp, k * 128:(k + 1) * 128],
                  gwT_sb[:, k * c.n_exp:(k + 1) * c.n_exp], c.n_exp)

        def conv(dst, src, rows, rchunk):
            for r0 in range(0, rows, rchunk):
                r1 = min(r0 + rchunk, rows)
                sc.dma("pool", dst[r0:r1, :], src[r0:r1, :], "conv")

        conv(up_bf, up_w, c.d_model, 512)
        conv(down_bf, down_w, c.ffn_int, 512)
        conv(fc1_bf, fc1_w, c.d_model, 1024)
        conv(fc2_bf, fc2_w, c.ffn_lat, 1024)
        conv(exp_bf, exp_w, c.ffn_lat, 1024)

        def _mk_reblock():
            thunks = []
            for nm, (wbf, pb, kc, ncc, oct_, npj, nkq, cols) in PBW.items():
                for j in range(npj):
                    for kq in range(nkq):
                        for a in range(oct_):
                            def th(j=j, kq=kq, a=a, wbf=wbf, pb=pb,
                                   oct_=oct_, nkq=nkq, cols=cols):
                                r0 = (kq * oct_ + a) * 128
                                sc.dma("act",
                                       pb[j * nkq + kq,
                                          :, a * 256:(a + 1) * 256],
                                       wbf[r0:r0 + 128,
                                           j * 256:(j + 1) * 256], "rbk")
                            thunks.append(th)
            return thunks

        reblock_thunks = _mk_reblock()
        rb_i = [0]

        def emit_reblocks(k):
            if rb_i[0] == 0 and reblock_thunks:
                sc.wdma("act", "conv")
            n = min(k, len(reblock_thunks) - rb_i[0])
            for _ in range(n):
                reblock_thunks[rb_i[0]]()
                rb_i[0] += 1

        AR.reset()
        sc.barrier()

        def ln_finalize(mu_o, rs_o, ms_o, w, d_total, bs, bq):
            sc.wait("dve", PE, PE.n)
            inv = 1.0 / d_total
            sc.op("dve", lambda e: e.tensor_scalar_mul(
                mu_o, bank(bs, rows=1, w=w), inv))
            sc.op("dve", lambda e: e.tensor_scalar_mul(
                rs_o, bank(bq, rows=1, w=w), inv))
            sc.op("dve", lambda e: e.tensor_tensor(ms_o, mu_o, mu_o, ALU.mult))
            sc.op("dve", lambda e: e.tensor_tensor(
                rs_o, rs_o, ms_o, ALU.subtract))
            sc.wait("act", SDVE, SDVE.n)
            sc.op("act", lambda e: e.activation(
                rs_o, rs_o, AF.Ln, bias=eps1_sb[:1, :]))
            sc.op("act", lambda e: e.activation(
                rs_o, rs_o, AF.Exp, scale=-0.5))
            sc.wait("dve", SACT, SACT.n)
            sc.op("dve", lambda e: e.tensor_tensor(ms_o, mu_o, rs_o, ALU.mult))

        def replicate(row_ap, dst, pbank):
            w = row_ap.shape[-1]
            sc.wait("pe", SDVE, SDVE.n)
            sc.wait("pe", SACT, SACT.n)
            sc.op("pe", lambda e, r=row_ap, w=w: e.matmul(
                bank(pbank, w=w), onesr_sb[:], r, start=True, stop=True),
                incs=((PE, 1),))
            sc.wait("act", PE, PE.n)
            sc.op("act", lambda e, d=dst, w=w: e.activation(
                d, bank(pbank, w=w), AF.Copy))

        AR.mark()
        xchk = [AR.alloc(F32, c.sw) for _ in range(2)]
        sqchb = AR.alloc(BF16, c.sw)
        xbfs = [AR.alloc(BF16, c.sw) for _ in range(c.ndc)]
        rep_r = AR.alloc(F32, c.sw)
        rep_m = AR.alloc(F32, c.sw)
        t1b = AR.alloc(F32, c.sw)
        xpo = [AR.alloc(F32, c.sw) for _ in range(c.nrc)]

        for w in range(c.nst):
            cw = slice(c.sw * w, c.sw * (w + 1))
            pe_chunk = []
            sc.wdma("dve", "st_xp")
            for k in range(c.ndc):
                buf = xchk[k % 2]
                key = f"xc{k % 2}"
                if k >= 2:
                    sc.wait("sp", PE, pe_chunk[k - 2])
                    sc.wait("sp", SDVE, SDVE.n)
                sc.dma("sp", buf, xT_t[k * 128:(k + 1) * 128, cw], key)
                sc.wdma("dve", key)
                if w > 0 and k == 0:
                    sc.wait("dve", PE, PE.n)
                sc.op("dve", lambda e, b=buf, k=k: e.tensor_copy(xbfs[k], b))
                sc.wait("act", SDVE, SDVE.n)
                sc.wait("act", PE, PE.n)
                sc.op("act", lambda e, k=k: e.activation(
                    sqchb, xbfs[k], AF.Square))
                sc.wait("pe", SACT, SACT.n)
                sc.op("pe", lambda e, k=k: e.matmul(
                    bank(0, rows=1, w=c.sw), onesb_sb[:], xbfs[k],
                    start=(k == 0), stop=(k == c.ndc - 1)))
                sc.op("pe", lambda e, k=k: e.matmul(
                    bank(1, rows=1, w=c.sw), onesb_sb[:], sqchb,
                    start=(k == 0), stop=(k == c.ndc - 1)), incs=((PE, 1),))
                pe_chunk.append(PE.n)
            ln_finalize(mu_row[:], rs_row[:], ms_row[:], c.sw, c.d_model, 0, 1)
            replicate(rs_row[:], rep_r, 2)
            replicate(ms_row[:], rep_m, 3)
            sc.wait("pe", SDVE, SDVE.n)
            for j in range(c.nrc):
                rw = min(128 * (j + 1), c.nxp) - 128 * j
                for k in range(c.ndc):
                    o = (k * c.nrc + j) * 128
                    sc.op("pe", lambda e, o=o, rw=rw, k=k, j=j: e.matmul(
                        bank(4 + j, rows=rw, w=c.sw), xpwT_sb[:, o:o + rw],
                        xbfs[k], start=(k == 0), stop=(k == c.ndc - 1)),
                        incs=((PE, 1),) if k == c.ndc - 1 else ())
            sc.wait("dve", PE, PE.n)
            sc.wait("dve", SACT, SACT.n)
            for j in range(c.nrc):
                rw = min(128 * (j + 1), c.nxp) - 128 * j
                sc.op("dve", lambda e, j=j, rw=rw: e.tensor_tensor(
                    t1b[:rw, :], bank(4 + j, rows=rw, w=c.sw),
                    rep_r[:rw, :], ALU.mult))
                sc.op("dve", lambda e, j=j, rw=rw: e.scalar_tensor_tensor(
                    xpo[j][:rw, :], rep_m[:rw, :], ncs_sb[:rw, j:j + 1],
                    t1b[:rw, :], ALU.mult, ALU.add))
                sc.wait("sp", SDVE, SDVE.n)
                sc.dma("sp", xp_in[128 * j:128 * j + rw, cw], xpo[j][:rw, :],
                       "st_xp")
            sc.wait("sp", SDVE, SDVE.n)
            sc.dma("sp", xp_in[c.nxp:c.nxp + 1, cw], ms_row[:], "st_xp")
            sc.dma("sp", xp_in[c.nxp + 1:c.nxp + 2, cw], rs_row[:], "st_xp")
            sc.wait("pe", SDVE, SDVE.n)

        sc.wdma("pool", "st_xp")
        sc.op("pool", lambda e: e.collective_compute(
            "AllGather", ALU.bypass, replica_groups=[list(range(8))],
            ins=[xp_in.tensor.ap().opt()], outs=[xp_out.tensor.ap().opt()]),
            incs=((CC0, 1),), chain=False)
        if c.debug_taps:
            sc.wait("pool", CC0, CC0.n)
            sc.dma("pool", taps["t_xp"], xp_out, "st_tap")
        AR.reset()
        sc.barrier()

        AR.mark()
        TH = c.th
        nsw = TH // c.sw
        xpd_bf = AR.alloc(BF16, TH)
        mrep = AR.alloc(F32, TH)
        rrep = AR.alloc(F32, TH)
        dt4 = [AR.alloc(F32, TH) for _ in range(c.ndt)]
        dtx4 = [AR.alloc(BF16, TH) for _ in range(c.ndt)]
        dAb = [AR.alloc(F32, TH) for _ in range(2)]
        _dbx = AR.alloc(BF16, TH)
        dBxb = [_dbx, _dbx]
        _h = AR.alloc(BF16, TH)
        hbuf = [_h, _h]
        hcb = [AR.alloc(BF16, TH) for _ in range(2)]
        Brep = [AR.alloc(BF16, TH) for _ in range(2)]
        Crep = [AR.alloc(BF16, TH) for _ in range(2)]
        _xs = AR.alloc(F32, TH)
        xsl = [_xs, _xs]
        scr1 = dAb[0]
        scr2m = dAb[1]

        brow = [AR.alloc(BF16, TH, rows=1) for _ in range(2)]

        sc.wait("pool", CC0, 1)
        for r8 in range(8):
            rb0 = r8 * nxs
            for row in range(2 * c.d_state):
                sl = row % 2
                sc.wait("pool", PE, PE.n)
                sc.dma("pool", brow[sl],
                       xp_out[rb0 + c.dt_rank + row:
                              rb0 + c.dt_rank + row + 1, :], "m_br")
                sc.wdma("pe", "m_br")
                sc.wait("pe", SACT, SACT.n)
                for wv in range(nsw):
                    sc.op("pe", lambda e, sl=sl, wv=wv: e.matmul(
                        bank(4 + wv, w=c.sw), onesrb_sb[:],
                        brow[sl][:, wv * c.sw:(wv + 1) * c.sw],
                        start=True, stop=True),
                        incs=((PE, 1),) if wv == nsw - 1 else ())
                sc.wait("act", PE, PE.n)
                sc.wdma("act", "st_bcr")
                sc.op("act", lambda e, sl=sl: e.activation(
                    Brep[sl], PS[:, 2048:2048 + TH], AF.Copy))
                sc.wait("sp", SACT, SACT.n)
                sc.dma("sp", bcrep_d[r8, :, row * TH:(row + 1) * TH],
                       Brep[sl], "st_bcr")
                sc.wait("act", PE, PE.n)
        sc.wdma("sp", "st_bcr")

        npass = (c.ndt + 1) // 2
        for b in range(c.batch):
            for th in range(2):
                r = 2 * b + th
                rb = r * nxs
                tok0 = b * c.seq + th * TH
                sc.wait("sp", CC0, 1)
                sc.wait("sp", SDVE, SDVE.n)
                sc.dma("sp", mrep,
                       xp_out[rb + c.nxp:rb + c.nxp + 1, :]
                       .broadcast_to([128, TH]), "m_mr")
                sc.dma("sp", rrep,
                       xp_out[rb + c.nxp + 1:rb + c.nxp + 2, :]
                       .broadcast_to([128, TH]), "m_mr")
                sc.wait("pool", CC0, 1)
                sc.wait("pool", PE, PE.n)
                sc.dma("pool", xpd_bf, xp_out[rb:rb + 128, :], "m_xpd")
                for a in range(c.ndt):
                    sc.wdma("pe", "m_xpd")
                    sc.wait("pe", SACT, SACT.n)
                    sc.wait("pe", SDVE, SDVE.n)
                    for wv in range(nsw):
                        sc.op("pe", lambda e, a=a, wv=wv: e.matmul(
                            bank(wv, w=c.sw),
                            dtwT_sb[:, a * 128:(a + 1) * 128],
                            xpd_bf[:, c.sw * wv:c.sw * (wv + 1)],
                            start=True, stop=True),
                            incs=((PE, 1),) if wv == nsw - 1 else ())
                    sc.wait("act", PE, PE.n)
                    sc.wait("act", SDVE, SDVE.n)
                    sc.op("act", lambda e, a=a: e.activation(
                        scr1, PS[:, :TH], AF.Exp, bias=dtb_sb[:, a:a + 1]), wide=True)
                    sc.op("act", lambda e, a=a: e.activation(
                        dt4[a], scr1, AF.Ln, bias=one1_sb[:]), wide=True)
                for a in range(c.ndt):
                    xb = xsl[a % 2]
                    key = f"m_x{a % 2}"
                    sc.wait("sp", SDVE, SDVE.n)
                    sc.dma("sp", xb,
                           xT_d[a * 128:(a + 1) * 128, tok0:tok0 + TH], key)
                    sc.wdma("dve", key)
                    sc.wdma("dve", "m_mr")
                    sc.wait("dve", SACT, SACT.n)
                    sc.op("dve", lambda e, xb=xb: e.tensor_tensor(
                        scr1, xb, rrep, ALU.mult), wide=True)
                    sc.op("dve", lambda e: e.tensor_tensor(
                        scr1, scr1, mrep, ALU.subtract), wide=True)
                    sc.op("dve", lambda e, a=a: e.tensor_tensor(
                        dtx4[a], dt4[a], scr1, ALU.mult), wide=True)
                for half in range(npass):
                    aa = [a for a in (2 * half, 2 * half + 1) if a < c.ndt]
                    for n in range(c.d_state):
                        emit_reblocks(6)
                        sl = n % 2
                        sc.wait("sp", SDVE, SDVE.n)
                        sc.dma("sp", Brep[sl],
                               bcrep_d[r, :, n * TH:(n + 1) * TH],
                               f"m_bc{sl}")
                        sc.dma("sp", Crep[sl],
                               bcrep_d[r, :, (c.d_state + n) * TH:
                                       (c.d_state + n + 1) * TH],
                               f"m_bc{sl}")
                        for ai, a in enumerate(aa):
                            s2 = (n * len(aa) + ai) % 2
                            ci = a * c.d_state + n
                            sc.wait("act", SDVE, SDVE.n)
                            sc.op("act", lambda e, a=a, s2=s2, ci=ci:
                                  e.activation(dAb[s2], dt4[a], AF.Exp,
                                               scale=A_sb[:, ci:ci + 1]), wide=True)
                            sc.wait("dve", SACT, SACT.n)
                            sc.wdma("dve", f"m_bc{sl}")
                            sc.op("dve", lambda e, a=a, s2=s2, sl=sl:
                                  e.tensor_tensor(dBxb[s2], dtx4[a],
                                                  Brep[sl], ALU.mult), wide=True)
                            if th == 1:
                                sc.op("dve", lambda e, s2=s2, ci=ci:
                                      e.scalar_tensor_tensor(
                                          dBxb[s2][:, 0:1], dAb[s2][:, 0:1],
                                          carry_sb[:, ci:ci + 1],
                                          dBxb[s2][:, 0:1],
                                          ALU.mult, ALU.add))
                            sc.op("dve", lambda e, s2=s2: e.tensor_tensor_scan(
                                hbuf[s2], dAb[s2], dBxb[s2], 0.0,
                                ALU.mult, ALU.add), wide=True)
                            if th == 0:
                                sc.op("dve", lambda e, s2=s2, ci=ci:
                                      e.tensor_copy(carry_sb[:, ci:ci + 1],
                                                    hbuf[s2][:, TH - 1:TH]))
                            sc.wait("dve", PE, PE.n)
                            sc.op("dve", lambda e, s2=s2, sl=sl:
                                  e.tensor_tensor(hcb[s2], hbuf[s2],
                                                  Crep[sl], ALU.mult), wide=True)
                            sc.wait("pe", SDVE, SDVE.n)
                            for wv in range(nsw):
                                sc.op("pe", lambda e, ai=ai, s2=s2, wv=wv,
                                      n=n: e.matmul(
                                          bank(ai * nsw + wv, w=c.sw),
                                          idb_sb[:],
                                          hcb[s2][:, wv * c.sw:
                                                  (wv + 1) * c.sw],
                                          start=(n == 0),
                                          stop=(n == c.d_state - 1)),
                                      incs=((PE, 1),)
                                      if wv == nsw - 1 else ())
                    for ai, a in enumerate(aa):
                        xb = xsl[a % 2]
                        key = f"m_x{a % 2}"
                        sc.wait("sp", SDVE, SDVE.n)
                        sc.dma("sp", xb,
                               xT_d[a * 128:(a + 1) * 128, tok0:tok0 + TH],
                               key)
                        sc.wdma("dve", key)
                        sc.wait("dve", SACT, SACT.n)
                        sc.op("dve", lambda e, xb=xb: e.tensor_tensor(
                            scr1, xb, rrep, ALU.mult), wide=True)
                        sc.op("dve", lambda e: e.tensor_tensor(
                            scr1, scr1, mrep, ALU.subtract), wide=True)
                        sc.op("dve", lambda e, a=a, xb=xb:
                              e.scalar_tensor_tensor(
                                  scr1, scr1, D_sb[:, a:a + 1], xb,
                                  ALU.mult, ALU.add), wide=True)
                        sc.wait("dve", PE, PE.n)
                        sc.op("dve", lambda e, ai=ai: e.tensor_tensor(
                            scr2m, scr1,
                            PS[:, ai * nsw * 512:ai * nsw * 512 + TH],
                            ALU.add), wide=True)
                        sc.wait("sp", SDVE, SDVE.n)
                        sc.dma("sp", y_in[c.dsl * r + 128 * a:
                                          c.dsl * r + 128 * (a + 1), :],
                               scr2m, "st_y")
                        sc.wdma("dve", "st_y")
        emit_reblocks(10 ** 9)
        sc.wdma("pool", "st_y")
        sc.wdma("pool", "conv")

        def _a2a_y(e):
            with allow_ici_all_to_all():
                return e.collective_compute(
                    "AllToAll", ALU.bypass, replica_groups=[list(range(8))],
                    ins=[y_in.tensor.ap().opt()],
                    outs=[y_out.tensor.ap().opt()])
        sc.op("pool", _a2a_y, incs=((CC1, 1),), chain=False)
        if c.debug_taps:
            sc.wait("pool", CC1, CC1.n)
            sc.dma("pool", taps["t_x2"], y_out, "st_tap")
        AR.reset()
        sc.barrier()

        def ffn_phase(x_dram, out_dram, z_dram):
            AR.mark()
            z2 = [AR.alloc(BF16, c.sblk) for _ in range(c.ndc)]
            h2 = [AR.alloc(BF16, c.sblk) for _ in range(c.nic)]
            xlat = [AR.alloc(BF16, c.sblk) for _ in range(c.nlc)]
            moe = [AR.alloc(BF16, c.sblk) for _ in range(c.nlc)]
            pan = [AR.alloc(BF16, 2048) for _ in range(3)]
            x2c = [AR.alloc(F32, c.sblk) for _ in range(2)]
            sqs = AR.alloc(F32, c.sblk)
            sqsb = AR.alloc(BF16, c.sblk)
            repr_ = AR.alloc(F32, c.sblk)
            repm = AR.alloc(F32, c.sblk)
            scr2 = AR.alloc(F32, c.sblk)
            scr3 = AR.alloc(F32, c.sblk)
            gt = AR.alloc(F32, c.sblk)
            lgt = AR.alloc(F32, c.n_exp)
            nbig = AR.alloc(F32, c.n_exp)
            m12 = AR.alloc(F32, 2)
            msc = AR.alloc(F32, max(c.sblk // 128, 1))
            msrep = AR.alloc(BF16, c.sblk)
            fev = [AR.alloc(F32, c.sblk) for _ in range(2)]
            nsub = c.sblk // 128

            def strip_stats(src_dram, cw):
                sc.wdma("dve", "st_z3")
                for k in range(c.ndc):
                    buf = x2c[k % 2]
                    key = f"f_x{k % 2}"
                    if k >= 2:
                        sc.wait("sp", PE, PE.n)
                    sc.wait("sp", SDVE, SDVE.n)
                    sc.dma("sp", buf, src_dram[k * 128:(k + 1) * 128, cw],
                           key)
                    sc.wdma("dve", key)
                    sc.wait("dve", PE, PE.n)
                    sc.op("dve", lambda e, b=buf, k=k: e.tensor_copy(
                        z2[k], b), wide=True)
                    sc.wait("act", SDVE, SDVE.n)
                    sc.wait("act", PE, PE.n)
                    sc.op("act", lambda e, k=k: e.activation(
                        sqsb, z2[k], AF.Square), wide=True)
                    sc.wait("pe", SACT, SACT.n)
                    sc.op("pe", lambda e, k=k: e.matmul(
                        bank(6, rows=1, w=c.sblk), onesb_sb[:], z2[k],
                        start=(k == 0), stop=(k == c.ndc - 1)))
                    sc.op("pe", lambda e, k=k: e.matmul(
                        bank(7, rows=1, w=c.sblk), onesb_sb[:], sqsb,
                        start=(k == 0), stop=(k == c.ndc - 1)),
                        incs=((PE, 1),))
                ln_finalize(mu_row[:, :c.sblk], rs_row[:, :c.sblk],
                            ms_row[:, :c.sblk], c.sblk, c.d_model, 6, 7)
                replicate(rs_row[:, :c.sblk], repr_, 6)
                replicate(ms_row[:, :c.sblk], repm, 7)

            for w in range(c.nsblk):
                cw = slice(c.sblk * w, c.sblk * (w + 1))
                strip_stats(x_dram, cw)
                for k in range(c.ndc):
                    sc.wait("dve", SACT, SACT.n)
                    sc.op("dve", lambda e, k=k: e.tensor_tensor(
                        scr2, z2[k], repr_, ALU.mult), wide=True)
                    sc.op("dve", lambda e, k=k: e.tensor_tensor(
                        z2[k], scr2, repm, ALU.subtract), wide=True)
                sc.wait("pe", SDVE, SDVE.n)
                for k in range(c.ndc):
                    sc.op("pe", lambda e, k=k: e.matmul(
                        bank(0, rows=c.n_exp, w=c.sblk),
                        gwT_sb[:, k * c.n_exp:(k + 1) * c.n_exp], z2[k],
                        start=(k == 0), stop=(k == c.ndc - 1)),
                        incs=((PE, 1),) if k == c.ndc - 1 else ())
                sc.wait("act", PE, PE.n)
                sc.op("act", lambda e: e.activation(
                    gt[:c.n_exp, :], bank(0, rows=c.n_exp, w=c.sblk),
                    AF.Copy))
                sc.wait("pe", SACT, SACT.n)
                for i in range(nsub):
                    sc.op("pe", lambda e, i=i: e.transpose(
                        PS[:, 512 + i * 128:512 + i * 128 + c.n_exp],
                        gt[:c.n_exp, i * 128:(i + 1) * 128],
                        id32[:c.n_exp, :c.n_exp]),
                        incs=((PE, 1),) if i == nsub - 1 else ())
                sc.wait("dve", PE, PE.n)
                sc.op("dve", lambda e: e.memset(nbig, -1e30))
                for i in range(nsub):
                    sc.op("dve", lambda e, i=i: e.tensor_copy(
                        lgt, PS[:, 512 + i * 128:512 + i * 128 + c.n_exp]))
                    sc.op("dve", lambda e: e.reduce_max(
                        m12[:, 0:1], lgt, axis=AX.X))
                    sc.op("dve", lambda e: e.tensor_scalar(
                        eqm[:], lgt, m12[:, 0:1], None, ALU.is_equal))
                    sc.op("dve", lambda e: e.copy_predicated(
                        lgt, eqm[:], nbig))
                    sc.op("dve", lambda e: e.reduce_max(
                        m12[:, 1:2], lgt, axis=AX.X))
                    sc.op("dve", lambda e, i=i: e.tensor_tensor(
                        msc[:, i:i + 1], m12[:, 0:1], m12[:, 1:2], ALU.add))
                sc.wait("pe", SDVE, SDVE.n)
                sc.op("pe", lambda e: e.transpose(
                    PS[:nsub, 512:640], msc[:, :nsub], id32[:, :]),
                    incs=((PE, 1),))
                sc.wait("dve", PE, PE.n)
                sc.wdma("dve", "f_mrow")
                sc.op("dve", lambda e: e.tensor_copy(
                    msct[:nsub, :], PS[:nsub, 512:640]))
                sc.wait("sp", SDVE, SDVE.n)
                sc.wait("sp", PE, PE.n)
                for i in range(nsub):
                    sc.dma("sp", m_row[:, i * 128:(i + 1) * 128],
                           msct[i:i + 1, :], "f_mrow")
                sc.wdma("pe", "f_mrow")
                replicate(m_row[:], msrep, 1)
                sc.wdma("sp", "rbk")
                pan_snap = {0: PE.n, 1: PE.n, 2: PE.n}
                evac_snap = {}
                pslot = [3]

                def pair_mms(pbname, jj, rhs_list, b0, extra=False,
                             st_first=True, sp_last=True):
                    wbf, pb, kc, ncc, oct_, npj, nkq, cols = PBW[pbname]
                    for kq in range(nkq):
                        sl = pslot[0] % 3
                        if pslot[0] >= 3:
                            sc.wait("sp", PE, pan_snap[pslot[0] - 3])
                        sc.dma("sp", pan[sl][:, :oct_ * 256],
                               pb[jj * nkq + kq, :, :], f"f_p{sl}")
                        sc.wdma("pe", f"f_p{sl}")
                        for a in range(oct_):
                            k = kq * oct_ + a
                            for gi in range(2):
                                o = (a * 2 + gi) * 128
                                st = st_first and (k == 0)
                                sp = sp_last and (k == kc - 1)
                                last = (a == oct_ - 1 and gi == 1)
                                sc.op("pe", lambda e, o=o, gi=gi, sl=sl,
                                      b0=b0, st=st, sp=sp, k=k: e.matmul(
                                          bank(b0 + gi, w=c.sblk),
                                          pan[sl][:, o:o + 128],
                                          rhs_list[k],
                                          start=st, stop=sp),
                                      incs=((PE, 1),) if last else ())
                        pan_snap[pslot[0]] = PE.n
                        pslot[0] += 1

                NPAIR = (c.nic + c.nlc) // 2
                for g in range(NPAIR):
                    b0 = (g % 2) * 2
                    if g >= 2:
                        sc.wait("pe", SACT, evac_snap[g - 2][0])
                        sc.wait("pe", SDVE, evac_snap[g - 2][1])
                    else:
                        sc.wait("pe", SACT, SACT.n)
                        sc.wait("pe", SDVE, SDVE.n)
                    if g < c.nic // 2:
                        pair_mms("up", g, z2, b0)
                        ics = [2 * g, 2 * g + 1]
                    else:
                        jj = g - c.nic // 2
                        pair_mms("fc1", jj, z2, b0)
                        ics = [c.nic + 2 * jj, c.nic + 2 * jj + 1]
                    sc.wait("act", PE, PE.n)
                    sc.wait("act", SDVE, SDVE.n)
                    for gi, ic in enumerate(ics):
                        if ic < c.nic:
                            sc.op("act", lambda e, gi=gi, b0=b0: e.activation(
                                fev[gi], bank(b0 + gi, w=c.sblk), AF.Relu), wide=True)
                            sc.wait("dve", SACT, SACT.n)
                            sc.op("dve", lambda e, ic=ic, gi=gi:
                                  e.tensor_tensor(h2[ic], fev[gi],
                                                  fev[gi], ALU.mult), wide=True)
                        else:
                            sc.op("act", lambda e, ic=ic, b0=b0, gi=gi:
                                  e.activation(xlat[ic - c.nic],
                                               bank(b0 + gi, w=c.sblk),
                                               AF.Copy), wide=True)
                    evac_snap[g] = (SACT.n, SDVE.n)

                for g in range(c.nlc // 2):
                    b0 = (g % 2) * 2
                    if g < 2:
                        sc.wait("pe", SACT, SACT.n)
                        sc.wait("pe", SDVE, SDVE.n)
                    else:
                        sc.wait("pe", SDVE, evac_snap[("e", g - 2)][0])
                    pair_mms("exp", g, xlat, b0)
                    sc.wait("dve", PE, PE.n)
                    for gi in range(2):
                        sc.op("dve", lambda e, g=g, gi=gi, b0=b0:
                              e.tensor_tensor(moe[2 * g + gi],
                                              bank(b0 + gi, w=c.sblk),
                                              msrep, ALU.mult), wide=True)
                    evac_snap[("e", g)] = (SDVE.n,)

                ocnt = [0]
                for g in range(c.ndc // 2):
                    b0 = (g % 2) * 2
                    if g < 2:
                        sc.wait("pe", SACT, SACT.n)
                        sc.wait("pe", SDVE, SDVE.n)
                    else:
                        sc.wait("pe", SACT, evac_snap[("d", g - 2)][0])
                    pair_mms("down", g, h2, b0, st_first=True, sp_last=False)
                    pair_mms("fc2", g, moe, b0, st_first=False, sp_last=True)
                    sc.wait("act", PE, PE.n)
                    for gi in range(2):
                        o = 2 * g + gi
                        sc.wdma("act", "st_fa")
                        sc.op("act", lambda e, gi=gi, b0=b0: e.activation(
                            fev[gi], bank(b0 + gi, w=c.sblk), AF.Copy), wide=True)
                        sc.wait("act", PE, PE.n)
                        sc.op("act", lambda e, gi=gi: e.activation(
                            sqs, fev[gi], AF.Square), wide=True)
                        sc.wait("pe", SACT, SACT.n)
                        st0 = (ocnt[0] == 0)
                        fin = (ocnt[0] == c.ndc - 1)
                        sc.op("pe", lambda e, gi=gi, st0=st0, fin=fin:
                              e.matmul(bank(6, rows=1, w=c.sblk),
                                       onesf_sb[:], fev[gi],
                                       start=st0, stop=fin))
                        sc.op("pe", lambda e, st0=st0, fin=fin: e.matmul(
                            bank(7, rows=1, w=c.sblk), onesf_sb[:], sqs,
                            start=st0, stop=fin), incs=((PE, 1),))
                        ocnt[0] += 1
                        sc.wait("sp", SACT, SACT.n)
                        sc.dma("sp", facc_d[o * 128:(o + 1) * 128,
                                            (w % 2) * c.sblk:
                                            (w % 2 + 1) * c.sblk],
                               fev[gi], "st_fa")
                    evac_snap[("d", g)] = (SACT.n,)
                ln_finalize(mu_row[:, :c.sblk], rs_row[:, :c.sblk],
                            ms_row[:, :c.sblk], c.sblk, c.d_model, 6, 7)
                replicate(rs_row[:, :c.sblk], repr_, 6)
                replicate(ms_row[:, :c.sblk], repm, 7)
                sc.wdma("sp", "st_fa")
                sc.wait("sp", PE, PE.n)
                for k in range(c.ndc):
                    buf = x2c[k % 2]
                    fv = fev[k % 2]
                    key = f"f_x{k % 2}"
                    sc.wait("sp", SDVE, SDVE.n)
                    sc.wait("sp", SACT, SACT.n)
                    sc.dma("sp", buf, x_dram[k * 128:(k + 1) * 128, cw], key)
                    sc.dma("sp", fv, facc_d[k * 128:(k + 1) * 128,
                                            (w % 2) * c.sblk:
                                            (w % 2 + 1) * c.sblk], key)
                    sc.wdma("dve", key)
                    sc.wait("dve", SACT, SACT.n)
                    sc.op("dve", lambda e, fv=fv: e.tensor_tensor(
                        scr2, fv, repr_, ALU.mult), wide=True)
                    sc.op("dve", lambda e: e.tensor_tensor(
                        scr2, scr2, repm, ALU.subtract), wide=True)
                    sc.wait("act", SDVE, SDVE.n)
                    sc.wdma("act", "st_out")
                    sc.op("act", lambda e, k=k: e.activation(
                        scr3, scr2, AF.Identity,
                        bias=lnb_sb[:, k:k + 1], scale=lng_sb[:, k:k + 1]), wide=True)
                    sc.wait("dve", SACT, SACT.n)
                    sc.op("dve", lambda e, buf=buf: e.tensor_tensor(
                        scr3, scr3, buf, ALU.add), wide=True)
                    sc.wait("sp", SDVE, SDVE.n)
                    sc.dma("sp", out_dram[k * 128:(k + 1) * 128, cw],
                           scr3, "st_out")
                if z_dram is not None:
                    sc.wdma("sp", "st_out")
                    sc.wdma("dve", "st_z3")
                    strip_stats(out_dram, cw)
                    for k in range(c.ndc):
                        sc.wait("dve", SACT, SACT.n)
                        sc.op("dve", lambda e, k=k: e.tensor_tensor(
                            scr2, z2[k], repr_, ALU.mult), wide=True)
                        sc.op("dve", lambda e, k=k: e.tensor_tensor(
                            z2[k], scr2, repm, ALU.subtract), wide=True)
                        sc.wait("sp", SDVE, SDVE.n)
                        sc.dma("sp", z_dram[w, :, k * c.sblk:(k + 1) * c.sblk],
                               z2[k], "st_z3")
            AR.reset()

        sc.wait("sp", CC1, 1)
        sc.wait("pool", CC1, 1)
        ffn_phase(y_out, x3_d, z3_in)
        if c.debug_taps:
            sc.wdma("pool", "st_out", "st_z3")
            sc.dma("pool", taps["t_x3"], x3_d, "st_tap")
            sc.dma("pool", taps["t_z3"], z3_in, "st_tap")
        sc.barrier()

        if c.stop_after == "p3":
            sc.barrier()
            block = es.enter_context(nc.Block())
            sc.replay(block)
            return nc, taps
        AR.mark()
        z3r = AR.alloc(BF16, c.nsblk * c.ndc * c.sblk)
        wpan = [AR.alloc(BF16, c.ndc * 128) for _ in range(2)]
        qkvev = AR.alloc(BF16, c.tpc)
        nwv = c.tpc // c.sw
        sc.wdma("sp", "st_z3")
        for wv in range(c.nsblk):
            sc.dma("sp", z3r[:, wv * c.ndc * c.sblk:(wv + 1) * c.ndc * c.sblk],
                   z3_in[wv, :, :], "a_z3")
        NCC = 3 * c.heads
        for cc in range(NCC):
            sl = cc % 2
            b0 = (cc % 2) * 4
            if cc >= 2:
                sc.wait("pool", PE, PE.n)
            sc.dma("pool",
                   wpan[sl].rearrange("p (a f) -> p a f", a=c.ndc),
                   wqkv_u[:, cc * 128:(cc + 1) * 128]
                   .rearrange("(a p) f -> p a f", p=128), f"a_w{sl}")
            sc.wdma("pe", f"a_w{sl}")
            sc.wdma("pe", "a_z3")
            if cc >= 2:
                sc.wait("pe", SACT, SACT.n)
            for k in range(c.ndc):
                for wv in range(nwv):
                    sc.op("pe", lambda e, k=k, wv=wv, b0=b0, sl=sl: e.matmul(
                        bank(b0 + wv, w=c.sw),
                        wpan[sl][:, k * 128:(k + 1) * 128],
                        z3r[:, (wv * c.ndc + k) * c.sblk:
                            (wv * c.ndc + k) * c.sblk + c.sw],
                        start=(k == 0), stop=(k == c.ndc - 1)),
                        incs=((PE, 1),)
                        if (k == c.ndc - 1 and wv == nwv - 1) else ())
            sc.wait("act", PE, PE.n)
            sc.wdma("act", "st_qkv")
            for wv in range(nwv):
                sc.op("act", lambda e, wv=wv, b0=b0: e.activation(
                    qkvev[:, wv * c.sw:(wv + 1) * c.sw],
                    bank(b0 + wv, w=c.sw), AF.Copy))
            m, hg = divmod(cc, c.heads) if False else (cc // c.heads,
                                                       cc % c.heads)
            j = hg // c.hbc
            row = j * 3 * c.hbc * 128 + m * c.hbc * 128 + (hg % c.hbc) * 128
            sc.wait("sp", SACT, SACT.n)
            sc.dma("sp", qkv_in[row:row + 128, :], qkvev, "st_qkv")
        sc.wdma("pool", "st_qkv")

        def _a2a_qkv(e):
            with allow_ici_all_to_all():
                return e.collective_compute(
                    "AllToAll", ALU.bypass, replica_groups=[list(range(8))],
                    ins=[qkv_in.tensor.ap().opt()],
                    outs=[qkv_out.tensor.ap().opt()])
        sc.op("pool", _a2a_qkv, incs=((CC2, 1),), chain=False)
        AR.reset()
        sc.barrier()

        AR.mark()
        qsb = AR.alloc(BF16, c.seq)
        ksb = AR.alloc(BF16, c.seq)
        vsb = AR.alloc(BF16, c.seq)
        vstg = AR.alloc(F32, 128)
        vtT = AR.alloc(BF16, c.seq)
        ptb = [AR.alloc(BF16, 2 * c.qmac) for _ in range(2)]
        zrep = AR.alloc(F32, c.qmac)
        atev = AR.alloc(BF16, c.qmac)
        scale = 1.0 / math.sqrt(c.dq)
        sc.wait("sp", CC2, 1)
        for b in range(c.batch):
            for hhl in range(c.hbc):
                sc.wait("sp", PE, PE.n)
                sc.wait("sp", SDVE, SDVE.n)
                for half in range(2):
                    i = 2 * b + half
                    base = i * 3 * c.hbc * 128
                    sc.dma("sp", qsb[:, half * c.th:(half + 1) * c.th],
                           qkv_out[base + hhl * 128:
                                   base + (hhl + 1) * 128, :], "a_qkv")
                    sc.dma("sp", ksb[:, half * c.th:(half + 1) * c.th],
                           qkv_out[base + (c.hbc + hhl) * 128:
                                   base + (c.hbc + hhl + 1) * 128, :],
                           "a_qkv")
                    sc.dma("sp", vtT[:, half * c.th:(half + 1) * c.th],
                           qkv_out[base + (2 * c.hbc + hhl) * 128:
                                   base + (2 * c.hbc + hhl + 1) * 128, :],
                           "a_qkv")
                sc.wdma("dve", "a_qkv")
                for q in range(c.seq // 128):
                    sc.wait("dve", PE, PE.n)
                    sc.op("dve", lambda e, q=q: e.tensor_copy(
                        vstg, vtT[:, q * 128:(q + 1) * 128]))
                    sc.wait("pe", SDVE, SDVE.n)
                    sc.wait("pe", SACT, SACT.n)
                    sc.op("pe", lambda e, q=q: e.transpose(
                        PS[:, 7 * 512:7 * 512 + 128], vstg, id32[:, :]),
                        incs=((PE, 1),))
                    sc.wait("act", PE, PE.n)
                    sc.op("act", lambda e, q=q: e.activation(
                        vsb[:, q * 128:(q + 1) * 128],
                        PS[:, 7 * 512:7 * 512 + 128], AF.Copy))
                sc.wdma("pe", "a_qkv")
                sc.wait("pe", SACT, SACT.n)
                for qm in range(c.nqm):
                    nk = (qm + 1) * c.ktpq
                    q0 = qm * c.qmac
                    for jp in range(0, nk, 2):
                        js = [jp] if jp + 1 >= nk else [jp, jp + 1]
                        sA = (jp // 2) % 2
                        sl2 = (jp // 2) % 2
                        sc.wait("pe", SACT, SACT.n)
                        sc.wait("pe", SDVE, SDVE.n)
                        for ji, j in enumerate(js):
                            sc.op("pe", lambda e, j=j, ji=ji, sA=sA, q0=q0:
                                  e.matmul(
                                      PS[:, sA * 1024 + ji * c.qmac:
                                         sA * 1024 + (ji + 1) * c.qmac],
                                      ksb[:, j * 128:(j + 1) * 128],
                                      qsb[:, q0:q0 + c.qmac],
                                      start=True, stop=True),
                                  incs=((PE, 1),) if ji == len(js) - 1
                                  else ())
                        sc.wait("act", PE, PE.n)
                        sc.wait("act", SDVE, SDVE.n)
                        np_ = len(js) * c.qmac
                        sc.op("act", lambda e, sA=sA, np_=np_, sl2=sl2:
                              e.activation(ptb[sl2][:, :np_],
                                           PS[:, sA * 1024:sA * 1024 + np_],
                                           AF.Exp, scale=scale))
                        for ji, j in enumerate(js):
                            p = j - qm * c.ktpq
                            if p >= 0:
                                sc.wait("dve", SACT, SACT.n)
                                sc.op("dve", lambda e, ji=ji, p=p, sl2=sl2:
                                      e.tensor_tensor(
                                          ptb[sl2][:, ji * c.qmac:
                                                   (ji + 1) * c.qmac],
                                          ptb[sl2][:, ji * c.qmac:
                                                   (ji + 1) * c.qmac],
                                          masks_sb[:, p * c.qmac:
                                                   (p + 1) * c.qmac],
                                          ALU.mult))
                        sc.wait("pe", SACT, SACT.n)
                        sc.wait("pe", SDVE, SDVE.n)
                        for ji, j in enumerate(js):
                            sc.op("pe", lambda e, ji=ji, j=j, nk=nk, sl2=sl2:
                                  e.matmul(
                                      bank(4, w=c.qmac),
                                      vsb[:, j * 128:(j + 1) * 128],
                                      ptb[sl2][:, ji * c.qmac:
                                               (ji + 1) * c.qmac],
                                      start=(j == 0), stop=(j == nk - 1)))
                            sc.op("pe", lambda e, ji=ji, j=j, nk=nk, sl2=sl2:
                                  e.matmul(
                                      bank(5, rows=1, w=c.qmac),
                                      onesb_sb[:],
                                      ptb[sl2][:, ji * c.qmac:
                                               (ji + 1) * c.qmac],
                                      start=(j == 0), stop=(j == nk - 1)),
                                  incs=((PE, 1),) if j == nk - 1 else ())
                    sc.wait("dve", PE, PE.n)
                    sc.op("dve", lambda e: e.reciprocal(
                        z_row[:], bank(5, rows=1, w=c.qmac)))
                    replicate(z_row[:], zrep, 6)
                    sc.wait("dve", SACT, SACT.n)
                    sc.wdma("dve", "st_at")
                    sc.op("dve", lambda e: e.tensor_tensor(
                        atev, bank(4, w=c.qmac), zrep, ALU.mult))
                    jq = 2 * b + (1 if q0 >= c.th else 0)
                    row = jq * c.hbc * 128 + hhl * 128
                    col = q0 % c.th
                    sc.wait("sp", SDVE, SDVE.n)
                    sc.dma("sp", at_in[row:row + 128, col:col + c.qmac],
                           atev, "st_at")
                    sc.wait("pe", SDVE, SDVE.n)
        sc.wdma("pool", "st_at")

        def _a2a_at(e):
            with allow_ici_all_to_all():
                return e.collective_compute(
                    "AllToAll", ALU.bypass, replica_groups=[list(range(8))],
                    ins=[at_in.tensor.ap().opt()],
                    outs=[at_out.tensor.ap().opt()])
        sc.op("pool", _a2a_at, incs=((CC3, 1),), chain=False)
        if c.debug_taps:
            sc.wait("pool", CC3, CC3.n)
            sc.dma("pool", taps["t_at"], at_out, "st_tap")
        AR.reset()
        sc.barrier()

        AR.mark()
        ae = [AR.alloc(BF16, c.tpc) for _ in range(c.nec)]
        wpan2 = [AR.alloc(BF16, (c.nec // 4) * 512) for _ in range(2)]
        x3c = AR.alloc(F32, c.tpc)
        x4c = AR.alloc(F32, c.tpc)
        sc.wait("sp", CC3, 1)
        for k in range(c.nec):
            sc.dma("sp", ae[k], at_out[k * 128:(k + 1) * 128, :], "c_ae")
        for o in range(c.ndc):
            sl = o % 2
            b0 = sl * 4
            if o >= 2:
                sc.wait("pool", PE, PE.n)
            for e4 in range(c.nec // 4):
                sc.dma("pool",
                       wpan2[sl][:, e4 * 512:e4 * 512 + 512]
                       .rearrange("p (a f) -> p a f", a=4),
                       wo[e4 * 512:(e4 + 1) * 512, o * 128:(o + 1) * 128]
                       .rearrange("(a p) f -> p a f", p=128), f"c_w{sl}")
            sc.wdma("pe", "c_ae")
            sc.wdma("pe", f"c_w{sl}")
            if o >= 2:
                sc.wait("pe", SDVE, SDVE.n)
            for ec in range(c.nec):
                e4, er = divmod(ec, 4)
                po = e4 * 512 + er * 128
                for wv in range(nwv):
                    sc.op("pe", lambda e, po=po, ec=ec, wv=wv, b0=b0, sl=sl:
                          e.matmul(bank(b0 + wv, w=c.sw),
                                   wpan2[sl][:, po:po + 128],
                                   ae[ec][:, wv * c.sw:(wv + 1) * c.sw],
                                   start=(ec == 0), stop=(ec == c.nec - 1)),
                          incs=((PE, 1),) if (ec == c.nec - 1
                                              and wv == nwv - 1) else ())
            sc.wait("sp", SDVE, SDVE.n)
            sc.wdma("sp", "st_out")
            sc.dma("sp", x3c, x3_d[o * 128:(o + 1) * 128, :], "c_x3")
            sc.wait("dve", PE, PE.n)
            sc.wdma("dve", "c_x3", "st_x4")
            for wv in range(nwv):
                sc.op("dve", lambda e, wv=wv, b0=b0: e.tensor_tensor(
                    x4c[:, wv * c.sw:(wv + 1) * c.sw],
                    bank(b0 + wv, w=c.sw),
                    x3c[:, wv * c.sw:(wv + 1) * c.sw], ALU.add))
            sc.wait("sp", SDVE, SDVE.n)
            sc.dma("sp", x4_d[o * 128:(o + 1) * 128, :], x4c, "st_x4")
        if c.debug_taps:
            sc.wdma("pool", "st_x4")
            sc.dma("pool", taps["t_x4"], x4_d, "st_tap")
        AR.reset()
        sc.barrier()

        sc.wdma("sp", "st_x4")
        ffn_phase(x4_d, out_xT, None)

        for nm, s in list(sc.sems.items()):
            if nm.startswith("dma_"):
                sc.wait("pool", s)
        sc.barrier()

        block = es.enter_context(nc.Block())
        sc.replay(block)

    return nc, taps



def _make_consts(c):
    import ml_dtypes
    ident = np.eye(128, dtype=np.float32)
    masks = np.zeros((128, c.ktpq * c.qmac), dtype=np.float32)
    for p in range(c.ktpq):
        k = np.arange(128)[:, None]
        q = np.arange(c.qmac)[None, :]
        masks[:, p * c.qmac:(p + 1) * c.qmac] = (128 * p + k <= q)
    return {
        "ident": ident,
        "ones_b": np.ones((128, 1), np.float32).astype(ml_dtypes.bfloat16),
        "ones_f": np.ones((128, 1), np.float32),
        "onesr_f": np.ones((1, 128), np.float32),
        "masks": masks.astype(ml_dtypes.bfloat16),
    }


_NC_CACHE = {}


def kernel(L0_attn_a_x, L0_attn_a_A_log, L0_attn_a_D, L0_attn_a_x_proj_w,
           L0_attn_a_dt_proj_w, L0_attn_a_dt_proj_b, L0_attn_b_wq,
           L0_attn_b_wkv, L0_attn_b_wo, L0_ffn_gate_w, L0_ffn_up_w,
           L0_ffn_down_w, L0_ffn_fc1_w, L0_ffn_fc2_w, L0_ffn_expert_w,
           L0_ffn_ln_gamma, L0_ffn_ln_beta):
    from concourse.bass_utils import run_bass_kernel_spmd

    c = Cfg(d_model=4096, seq=4096, batch=4, ffn_int=8192, ffn_lat=2048)
    x = np.asarray(L0_attn_a_x, dtype=np.float32)
    xT = np.ascontiguousarray(x.reshape(c.tokens, c.d_model).T)

    f32 = lambda a: np.ascontiguousarray(np.asarray(a, dtype=np.float32))
    wq = f32(L0_attn_b_wq)
    wkv = f32(L0_attn_b_wkv)
    wq_u = np.concatenate(
        [wq[:, i * c.dq:i * c.dq + c.dn] for i in range(c.heads)], axis=1)
    wk_u = np.concatenate(
        [wkv[:, i * c.dn:(i + 1) * c.dn] for i in range(c.heads)], axis=1)
    wv_u = np.concatenate(
        [wkv[:, c.heads * c.dn + i * c.dv:c.heads * c.dn + (i + 1) * c.dv]
         for i in range(c.heads)], axis=1)
    wqkv_u = np.ascontiguousarray(np.concatenate([wq_u, wk_u, wv_u], axis=1))
    consts = _make_consts(c)

    A_log = f32(L0_attn_a_A_log)
    D = f32(L0_attn_a_D)
    dtw = f32(L0_attn_a_dt_proj_w)
    dtb = f32(L0_attn_a_dt_proj_b)
    shared = {
        "xpw": f32(L0_attn_a_x_proj_w),
        "wqkv_u": wqkv_u,
        "wo": f32(L0_attn_b_wo),
        "gate_wT": np.ascontiguousarray(f32(L0_ffn_gate_w).T),
        "up_w": f32(L0_ffn_up_w),
        "down_w": f32(L0_ffn_down_w),
        "fc1_w": f32(L0_ffn_fc1_w),
        "fc2_w": f32(L0_ffn_fc2_w),
        "exp_w": f32(L0_ffn_expert_w),
        "ln_g": np.ascontiguousarray(
            f32(L0_ffn_ln_gamma).reshape(c.ndc, 128).T),
        "ln_b": np.ascontiguousarray(
            f32(L0_ffn_ln_beta).reshape(c.ndc, 128).T),
    }
    shared.update(consts)

    maps = []
    for core in range(8):
        sl = slice(c.dsl * core, c.dsl * (core + 1))
        m = dict(shared)
        m["xT_t"] = np.ascontiguousarray(
            xT[:, c.tpc * core:c.tpc * (core + 1)])
        m["xT_d"] = np.ascontiguousarray(xT[sl, :])
        m["A_log_sl"] = np.ascontiguousarray(
            A_log[sl].reshape(c.ndt, 128, c.d_state)
            .transpose(1, 0, 2).reshape(128, c.ndt * c.d_state))
        m["D_sl"] = np.ascontiguousarray(D[sl].reshape(c.ndt, 128).T)
        m["dtw_sl"] = np.ascontiguousarray(
            dtw[sl].reshape(c.ndt, 128, 128)
            .transpose(1, 0, 2).reshape(128, c.ndt * 128))
        m["dtb_sl"] = np.ascontiguousarray(dtb[sl].reshape(c.ndt, 128).T)
        maps.append(m)

    key = (c.d_model, c.seq)
    if key not in _NC_CACHE:
        _NC_CACHE[key] = build(c)
    nc, _ = _NC_CACHE[key]
    res = run_bass_kernel_spmd(nc, maps, core_ids=list(range(8)))
    outT = np.concatenate(
        [np.asarray(r["out_xT"]) for r in res.results], axis=1)
    return np.ascontiguousarray(outT.T).reshape(c.batch, c.seq, c.d_model)


# version 15
# speedup vs baseline: 1.0159x; 1.0159x over previous
import numpy as np

import math
from contextlib import ExitStack

import concourse.bass as bass
import concourse.mybir as mybir
from concourse.aot_env import allow_ici_all_to_all

F32 = mybir.dt.float32
BF16 = mybir.dt.bfloat16
U8 = mybir.dt.uint8
AF = mybir.ActivationFunctionType
ALU = mybir.AluOpType
AX = mybir.AxisListType

ENGS = ("sp", "act", "dve", "pe", "pool")
SELF_CHAIN = ("act", "dve", "pool")


class Cfg:
    def __init__(self, d_model=4096, seq=4096, batch=4, ffn_int=8192,
                 ffn_lat=2048, debug_taps=False, stop_after=None,
                 relax=True):
        self.stop_after = stop_after
        self.relax = relax
        self.d_model = d_model
        self.seq = seq
        self.batch = batch
        self.ffn_int = ffn_int
        self.ffn_lat = ffn_lat
        self.debug_taps = debug_taps
        self.d_state = 16
        self.dt_rank = 128
        self.heads = 32
        self.dn = 128
        self.dv = 128
        self.dq = 192
        self.n_exp = 8
        self.ln_eps = 1e-5
        assert batch == 4
        self.tokens = batch * seq
        self.tpc = self.tokens // 8
        self.dsl = d_model // 8
        self.ndt = self.dsl // 128
        self.ndc = d_model // 128
        self.th = seq // 2
        assert self.th == self.tpc
        self.hpc = self.heads // 2
        self.hbc = self.heads // 8
        self.nec = self.heads * self.dv // 128
        self.sblk = min(512, self.tpc)
        self.nsblk = self.tpc // self.sblk
        self.sw = min(512, self.tpc)
        self.nst = self.tpc // self.sw
        self.qmac = min(512, seq // 2)
        self.nqm = seq // self.qmac
        self.ktpq = self.qmac // 128
        self.tb = min(512, self.th)
        self.ntb = seq // self.tb
        self.nic = ffn_int // 128
        self.nlc = ffn_lat // 128
        self.nxp = self.dt_rank + 2 * self.d_state
        self.nrc = math.ceil(self.nxp / 128)


class Sem:
    def __init__(self, nc, es, name):
        self.h = es.enter_context(nc.semaphore(name))
        self.n = 0


class Sched:
    def __init__(self, nc, es):
        self.nc = nc
        self.es = es
        self.ops = {e: [] for e in ENGS}
        self.sems = {}
        self.waited = {}
        self.last_wide = {}
        self.relax = False
        for e in SELF_CHAIN:
            self.sems["self_" + e] = Sem(nc, es, "self_" + e)

    def sem(self, name):
        if name not in self.sems:
            self.sems[name] = Sem(self.nc, self.es, name)
        return self.sems[name]

    def wait(self, eng, sem, val=None):
        v = sem.n if val is None else val
        key = (eng, id(sem))
        if v <= 0 or self.waited.get(key, -1) >= v:
            return
        self.waited[key] = v
        self.ops[eng].append(lambda e, s=sem.h, v=v: e.wait_ge(s, v))

    def op(self, eng, fn, incs=(), chain=True, wide=False):
        incs = list(incs)
        if eng in SELF_CHAIN and not chain:
            ss = self.sems["self_" + eng]
            self.wait(eng, ss, ss.n)
        if eng in SELF_CHAIN and chain:
            ss = self.sems["self_" + eng]
            if not (self.relax and eng == "dve" and wide
                    and self.last_wide.get(eng, False)):
                self.wait(eng, ss, ss.n)
            self.last_wide[eng] = wide
            incs.append((ss, 1))
        for s, k in incs:
            s.n += k

        def wrapped(e, fn=fn, incs=tuple(incs)):
            ins = fn(e)
            for s, k in incs:
                ins = ins.then_inc(s.h, k)
            return ins
        self.ops[eng].append(wrapped)

    def dma(self, eng, out, in_, key):
        if eng in SELF_CHAIN:
            self.last_wide[eng] = False
        sem = self.sem("dma_" + key)
        sem.n += 16

        def wrapped(e, out=out, in_=in_, s=sem.h):
            return e.dma_start(out=out, in_=in_).then_inc(s, 16)
        self.ops[eng].append(wrapped)

    def wdma(self, eng, *keys):
        for key in keys:
            if ("dma_" + key) in self.sems:
                self.wait(eng, self.sems["dma_" + key])

    def barrier(self):
        bar = self.sem("bar")
        for e in ENGS:
            bar.n += 1
            self.ops[e].append(lambda en, s=bar.h: en.nop().then_inc(s, 1))
        tgt = bar.n
        for e in ENGS:
            self.ops[e].append(lambda en, s=bar.h, v=tgt: en.wait_ge(s, v))

    def replay(self, block):
        sched = self

        @block.sync
        def _(e):
            for f in sched.ops["sp"]:
                f(e)

        @block.scalar
        def _(e):
            for f in sched.ops["act"]:
                f(e)

        @block.vector
        def _(e):
            for f in sched.ops["dve"]:
                f(e)

        @block.tensor
        def _(e):
            for f in sched.ops["pe"]:
                f(e)

        @block.gpsimd
        def _(e):
            for f in sched.ops["pool"]:
                f(e)


class Arena:
    def __init__(self, nc, es, cols_bf16):
        self.guard_h = es.enter_context(
            nc.sbuf_tensor("arena", [128, cols_bf16], BF16))
        self.hb = self.guard_h
        self.h32 = self.hb.bitcast(F32)
        self.cols = cols_bf16
        self.off = 0
        self.marks = []

    def alloc(self, dtype, cols, rows=128):
        e2 = cols * (2 if dtype == F32 else 1)
        self.off = (self.off + 31) & ~31
        assert self.off + e2 <= self.cols, \
            f"arena overflow {self.off}+{e2}>{self.cols}"
        o = self.off
        self.off += e2
        if dtype == BF16:
            return self.hb[:rows, o:o + cols]
        return self.h32[:rows, o // 2:o // 2 + cols]

    def mark(self):
        self.marks.append(self.off)

    def reset(self):
        self.off = self.marks.pop()


def build(cfg: Cfg):
    c = cfg
    nc = bass.Bass("TRN2", target_bir_lowering=False)

    def din(name, shape, dt=F32):
        return nc.dram_tensor(name, shape, dt, kind="ExternalInput").ap()

    xT_t = din("xT_t", [c.d_model, c.tpc])
    xT_d = din("xT_d", [c.dsl, c.tokens])
    A_log_sl = din("A_log_sl", [128, c.ndt * c.d_state])
    D_sl = din("D_sl", [128, c.ndt])
    xpw = din("xpw", [c.nxp, c.d_model])
    dtw_sl = din("dtw_sl", [128, c.ndt * 128])
    dtb_sl = din("dtb_sl", [128, c.ndt])
    wqkv_u = din("wqkv_u", [c.d_model, 3 * c.heads * 128])
    wo = din("wo", [c.heads * c.dv, c.d_model])
    gate_wT = din("gate_wT", [c.n_exp, c.d_model])
    up_w = din("up_w", [c.d_model, c.ffn_int])
    down_w = din("down_w", [c.ffn_int, c.d_model])
    fc1_w = din("fc1_w", [c.d_model, c.ffn_lat])
    fc2_w = din("fc2_w", [c.ffn_lat, c.d_model])
    exp_w = din("exp_w", [c.ffn_lat, c.ffn_lat])
    ln_g = din("ln_g", [128, c.ndc])
    ln_b = din("ln_b", [128, c.ndc])
    ident = din("ident", [128, 128], F32)
    ones_b = din("ones_b", [128, 1], BF16)
    ones_f = din("ones_f", [128, 1], F32)
    onesr_f = din("onesr_f", [1, 128], F32)
    masks = din("masks", [128, c.ktpq * c.qmac], BF16)

    out_xT = nc.dram_tensor("out_xT", [c.d_model, c.tpc], F32,
                            kind="ExternalOutput").ap()

    def scr(name, shape, dt):
        return nc.dram_tensor(name, shape, dt).ap()

    nxs = c.nxp + 2
    xp_in = scr("xp_in", [nxs, c.tpc], F32)
    xp_out = scr("xp_out", [8 * nxs, c.tpc], F32)
    bcrep_d = scr("bcrep_d", [8, 128, 2 * c.d_state * c.th], BF16)
    y_in = scr("y_in", [c.d_model, c.tpc], F32)
    y_out = scr("y_out", [c.d_model, c.tpc], F32)
    z3_in = scr("z3_in", [c.nsblk, 128, c.ndc * c.sblk], BF16)
    qkv_in = scr("qkv_in", [8 * 3 * c.hbc * 128, c.tpc], BF16)
    qkv_out = scr("qkv_out", [8 * 3 * c.hbc * 128, c.tpc], BF16)
    at_in = scr("at_in", [8 * c.hbc * 128, c.tpc], BF16)
    at_out = scr("at_out", [8 * c.hbc * 128, c.tpc], BF16)
    x3_d = scr("x3_d", [c.d_model, c.tpc], F32)
    x4_d = scr("x4_d", [c.d_model, c.tpc], F32)
    facc_d = scr("facc_d", [c.d_model, 2 * c.sblk], F32)
    up_bf = scr("up_bf", [c.d_model, c.ffn_int], BF16)
    down_bf = scr("down_bf", [c.ffn_int, c.d_model], BF16)
    fc1_bf = scr("fc1_bf", [c.d_model, c.ffn_lat], BF16)
    fc2_bf = scr("fc2_bf", [c.ffn_lat, c.d_model], BF16)
    exp_bf = scr("exp_bf", [c.ffn_lat, c.ffn_lat], BF16)

    def pb_meta(rows, cols):
        kc, ncc = rows // 128, cols // 128
        oct_ = min(8, kc)
        return kc, ncc, oct_, ncc // 2, kc // oct_

    PBW = {}
    for nm, (wbf, rows, cols) in {
        "up": (up_bf, c.d_model, c.ffn_int),
        "down": (down_bf, c.ffn_int, c.d_model),
        "fc1": (fc1_bf, c.d_model, c.ffn_lat),
        "fc2": (fc2_bf, c.ffn_lat, c.d_model),
        "exp": (exp_bf, c.ffn_lat, c.ffn_lat),
    }.items():
        kc, ncc, oct_, npj, nkq = pb_meta(rows, cols)
        pb = scr(nm + "_pb", [npj * nkq, 128, oct_ * 256], BF16)
        PBW[nm] = (wbf, pb, kc, ncc, oct_, npj, nkq, cols)

    taps = {}
    if c.debug_taps:
        for nm, shape, dt in [
            ("t_xp", [8 * nxs, c.tpc], F32),
            ("t_x2", [c.d_model, c.tpc], F32),
            ("t_x3", [c.d_model, c.tpc], F32),
            ("t_z3", [c.d_model, c.tpc], BF16),
            ("t_at", [8 * c.hbc * 128, c.tpc], BF16),
            ("t_x4", [c.d_model, c.tpc], F32),
        ]:
            taps[nm] = nc.dram_tensor(nm, shape, dt,
                                      kind="ExternalOutput").ap()

    with ExitStack() as es:
        sc = Sched(nc, es)
        sc.relax = getattr(cfg, "relax", False)
        E = lambda *a: es.enter_context(nc.sbuf_tensor(*a))

        id32 = E("id32", [128, 128], F32)
        onesb_sb = E("onesb_sb", [128, 1], BF16)
        onesf_sb = E("onesf_sb", [128, 1], F32)
        onesr_sb = E("onesr_sb", [1, 128], F32)
        onesrb_sb = E("onesrb_sb", [1, 128], BF16)
        idb_sb = E("idb_sb", [128, 128], BF16)
        masks_sb = E("masks_sb", [128, c.ktpq * c.qmac], BF16)
        A_sb = E("A_sb", [128, c.ndt * c.d_state], F32)
        D_sb = E("D_sb", [128, c.ndt], F32)
        dtb_sb = E("dtb_sb", [128, c.ndt], F32)
        dtwT_sb = E("dtwT_sb", [128, c.ndt * 128], BF16)
        xpwT_sb = E("xpwT_sb", [128, c.ndc * c.nrc * 128], BF16)
        ncs_sb = E("ncs_sb", [128, c.nrc], F32)
        gwT_sb = E("gwT_sb", [128, c.ndc * c.n_exp], BF16)
        lng_sb = E("lng_sb", [128, c.ndc], F32)
        lnb_sb = E("lnb_sb", [128, c.ndc], F32)
        carry_sb = E("carry_sb", [128, max(c.ndt * c.d_state, 16)], F32)
        eps1_sb = E("eps1_sb", [128, 1], F32)
        one1_sb = E("one1_sb", [128, 1], F32)
        mu_row = E("mu_row", [1, c.sw], F32)
        rs_row = E("rs_row", [1, c.sw], F32)
        ms_row = E("ms_row", [1, c.sw], F32)
        z_row = E("z_row", [1, c.qmac], F32)
        m_row = E("m_row", [1, c.sblk], F32)
        msct = E("msct", [max(c.sblk // 128, 1), 128], F32)
        eqm = E("eqm", [128, c.n_exp], U8)

        AR = Arena(nc, es, 88064)

        PS = es.enter_context(nc.psum_tensor("PS", [128, 4096], F32))

        def bank(i, rows=128, w=512):
            return PS[:rows, 512 * i:512 * i + w]

        LD = sc.sem("ld")
        CV = sc.sem("cv")
        PE = sc.sem("pe")
        ST = sc.sem("st")
        CONV = sc.sem("conv")
        CC0, CC1, CC2, CC3 = (sc.sem(f"cc{i}") for i in range(4))
        SDVE = sc.sems["self_dve"]
        SACT = sc.sems["self_act"]
        SPOOL = sc.sems["self_pool"]

        sc.dma("sp", id32[:], ident, "p0")
        sc.dma("sp", onesb_sb[:], ones_b, "p0")
        sc.dma("sp", onesf_sb[:], ones_f, "p0")
        sc.dma("sp", onesr_sb[:], onesr_f, "p0")
        sc.dma("sp", masks_sb[:], masks, "p0")
        sc.dma("sp", D_sb[:], D_sl, "p0")
        sc.dma("sp", dtb_sb[:], dtb_sl, "p0")
        sc.dma("sp", lng_sb[:], ln_g, "p0")
        sc.dma("sp", lnb_sb[:], ln_b, "p0")

        AR.mark()
        tmpA = AR.alloc(F32, c.ndt * c.d_state)
        stage = AR.alloc(F32, max(c.ndt * 128, c.d_model))

        sc.dma("sp", tmpA, A_log_sl, "p0")
        sc.op("dve", lambda e: e.memset(eps1_sb[:], float(c.ln_eps)))
        sc.wait("dve", LD, LD.n) if False else None
        sc.op("dve", lambda e: e.memset(onesrb_sb[:], 1.0))
        sc.wdma("dve", "p0")
        sc.op("dve", lambda e: e.tensor_copy(idb_sb[:], id32[:]))
        sc.op("dve", lambda e: e.memset(one1_sb[:], 1.0))
        sc.wdma("act", "p0")
        sc.op("act", lambda e: e.activation(tmpA, tmpA, AF.Exp))
        sc.wait("dve", SACT, SACT.n)
        sc.op("dve", lambda e: e.tensor_scalar_mul(A_sb[:], tmpA, -1.0))

        ntr = [0]

        def tr128(src, dst, ocols):
            bi = 6 + ntr[0] % 2
            sc.wait("pe", SACT, SACT.n)
            sc.op("pe", lambda e, s=src, bi=bi, oc=ocols: e.transpose(
                PS[:, bi * 512:bi * 512 + oc], s, id32[:oc, :oc]
                if False else id32[:s.shape[0], :s.shape[0]]),
                incs=((PE, 1),))
            sc.wait("act", PE, PE.n)
            sc.op("act", lambda e, bi=bi, d=dst, oc=ocols: e.activation(
                d, PS[:d.shape[0], bi * 512:bi * 512 + oc], AF.Copy))
            ntr[0] += 1

        sc.wdma("pe", "p0")
        sc.dma("sp", stage[:, :c.ndt * 128], dtw_sl, "p0s")
        sc.wdma("pe", "p0s")
        for a in range(c.ndt):
            tr128(stage[:, a * 128:(a + 1) * 128],
                  dtwT_sb[:, a * 128:(a + 1) * 128], 128)

        for j in range(c.nrc):
            r0 = 128 * j
            rw = min(128 * (j + 1), c.nxp) - r0
            sc.wait("sp", SACT, SACT.n)
            sc.dma("sp", stage[:rw, :c.d_model], xpw[r0:r0 + rw, :], "p0s")
            sc.wdma("pe", "p0s")
            for k in range(c.ndc):
                tr128(stage[:rw, k * 128:(k + 1) * 128],
                      xpwT_sb[:, (k * c.nrc + j) * 128:
                              (k * c.nrc + j) * 128 + rw], rw)
            sc.wait("pe", SACT, SACT.n)
            for k in range(c.ndc):
                o = (k * c.nrc + j) * 128
                sc.op("pe", lambda e, o=o, rw=rw, k=k: e.matmul(
                    PS[:rw, :1], xpwT_sb[:, o:o + rw], onesb_sb[:],
                    start=(k == 0), stop=(k == c.ndc - 1)),
                    incs=((PE, 1),) if k == c.ndc - 1 else ())
            sc.wait("dve", PE, PE.n)
            sc.op("dve", lambda e, j=j, rw=rw: e.tensor_scalar_mul(
                ncs_sb[:rw, j:j + 1], PS[:rw, :1], -1.0))
            sc.wait("pe", SDVE, SDVE.n)

        sc.wait("sp", SACT, SACT.n)
        sc.dma("sp", stage[:c.n_exp, :c.d_model], gate_wT, "p0s")
        sc.wdma("pe", "p0s")
        for k in range(c.ndc):
            tr128(stage[:c.n_exp, k * 128:(k + 1) * 128],
                  gwT_sb[:, k * c.n_exp:(k + 1) * c.n_exp], c.n_exp)

        def conv(dst, src, rows, rchunk):
            for r0 in range(0, rows, rchunk):
                r1 = min(r0 + rchunk, rows)
                sc.dma("pool", dst[r0:r1, :], src[r0:r1, :], "conv")

        conv(up_bf, up_w, c.d_model, 512)
        conv(down_bf, down_w, c.ffn_int, 512)
        conv(fc1_bf, fc1_w, c.d_model, 1024)
        conv(fc2_bf, fc2_w, c.ffn_lat, 1024)
        conv(exp_bf, exp_w, c.ffn_lat, 1024)

        def _mk_reblock():
            thunks = []
            for nm, (wbf, pb, kc, ncc, oct_, npj, nkq, cols) in PBW.items():
                for j in range(npj):
                    for kq in range(nkq):
                        for a in range(oct_):
                            def th(j=j, kq=kq, a=a, wbf=wbf, pb=pb,
                                   oct_=oct_, nkq=nkq, cols=cols):
                                r0 = (kq * oct_ + a) * 128
                                sc.dma("act",
                                       pb[j * nkq + kq,
                                          :, a * 256:(a + 1) * 256],
                                       wbf[r0:r0 + 128,
                                           j * 256:(j + 1) * 256], "rbk")
                            thunks.append(th)
            return thunks

        reblock_thunks = _mk_reblock()
        rb_i = [0]

        def emit_reblocks(k):
            if rb_i[0] == 0 and reblock_thunks:
                sc.wdma("act", "conv")
            n = min(k, len(reblock_thunks) - rb_i[0])
            for _ in range(n):
                reblock_thunks[rb_i[0]]()
                rb_i[0] += 1

        AR.reset()
        sc.barrier()

        def ln_finalize(mu_o, rs_o, ms_o, w, d_total, bs, bq):
            sc.wait("dve", PE, PE.n)
            inv = 1.0 / d_total
            sc.op("dve", lambda e: e.tensor_scalar_mul(
                mu_o, bank(bs, rows=1, w=w), inv))
            sc.op("dve", lambda e: e.tensor_scalar_mul(
                rs_o, bank(bq, rows=1, w=w), inv))
            sc.op("dve", lambda e: e.tensor_tensor(ms_o, mu_o, mu_o, ALU.mult))
            sc.op("dve", lambda e: e.tensor_tensor(
                rs_o, rs_o, ms_o, ALU.subtract))
            sc.wait("act", SDVE, SDVE.n)
            sc.op("act", lambda e: e.activation(
                rs_o, rs_o, AF.Ln, bias=eps1_sb[:1, :]))
            sc.op("act", lambda e: e.activation(
                rs_o, rs_o, AF.Exp, scale=-0.5))
            sc.wait("dve", SACT, SACT.n)
            sc.op("dve", lambda e: e.tensor_tensor(ms_o, mu_o, rs_o, ALU.mult))

        def replicate(row_ap, dst, pbank):
            w = row_ap.shape[-1]
            sc.wait("pe", SDVE, SDVE.n)
            sc.wait("pe", SACT, SACT.n)
            sc.op("pe", lambda e, r=row_ap, w=w: e.matmul(
                bank(pbank, w=w), onesr_sb[:], r, start=True, stop=True),
                incs=((PE, 1),))
            sc.wait("act", PE, PE.n)
            sc.op("act", lambda e, d=dst, w=w: e.activation(
                d, bank(pbank, w=w), AF.Copy))

        AR.mark()
        xchk = [AR.alloc(F32, c.sw) for _ in range(2)]
        sqchb = AR.alloc(BF16, c.sw)
        xbfs = [AR.alloc(BF16, c.sw) for _ in range(c.ndc)]
        rep_r = AR.alloc(F32, c.sw)
        rep_m = AR.alloc(F32, c.sw)
        t1b = AR.alloc(F32, c.sw)
        xpo = [AR.alloc(F32, c.sw) for _ in range(c.nrc)]

        for w in range(c.nst):
            cw = slice(c.sw * w, c.sw * (w + 1))
            pe_chunk = []
            sc.wdma("dve", "st_xp")
            for k in range(c.ndc):
                buf = xchk[k % 2]
                key = f"xc{k % 2}"
                if k >= 2:
                    sc.wait("sp", PE, pe_chunk[k - 2])
                    sc.wait("sp", SDVE, SDVE.n)
                sc.dma("sp", buf, xT_t[k * 128:(k + 1) * 128, cw], key)
                sc.wdma("dve", key)
                if w > 0 and k == 0:
                    sc.wait("dve", PE, PE.n)
                sc.op("dve", lambda e, b=buf, k=k: e.tensor_copy(xbfs[k], b))
                sc.wait("act", SDVE, SDVE.n)
                sc.wait("act", PE, PE.n)
                sc.op("act", lambda e, k=k: e.activation(
                    sqchb, xbfs[k], AF.Square))
                sc.wait("pe", SACT, SACT.n)
                sc.op("pe", lambda e, k=k: e.matmul(
                    bank(0, rows=1, w=c.sw), onesb_sb[:], xbfs[k],
                    start=(k == 0), stop=(k == c.ndc - 1)))
                sc.op("pe", lambda e, k=k: e.matmul(
                    bank(1, rows=1, w=c.sw), onesb_sb[:], sqchb,
                    start=(k == 0), stop=(k == c.ndc - 1)), incs=((PE, 1),))
                pe_chunk.append(PE.n)
            ln_finalize(mu_row[:], rs_row[:], ms_row[:], c.sw, c.d_model, 0, 1)
            replicate(rs_row[:], rep_r, 2)
            replicate(ms_row[:], rep_m, 3)
            sc.wait("pe", SDVE, SDVE.n)
            for j in range(c.nrc):
                rw = min(128 * (j + 1), c.nxp) - 128 * j
                for k in range(c.ndc):
                    o = (k * c.nrc + j) * 128
                    sc.op("pe", lambda e, o=o, rw=rw, k=k, j=j: e.matmul(
                        bank(4 + j, rows=rw, w=c.sw), xpwT_sb[:, o:o + rw],
                        xbfs[k], start=(k == 0), stop=(k == c.ndc - 1)),
                        incs=((PE, 1),) if k == c.ndc - 1 else ())
            sc.wait("dve", PE, PE.n)
            sc.wait("dve", SACT, SACT.n)
            for j in range(c.nrc):
                rw = min(128 * (j + 1), c.nxp) - 128 * j
                sc.op("dve", lambda e, j=j, rw=rw: e.tensor_tensor(
                    t1b[:rw, :], bank(4 + j, rows=rw, w=c.sw),
                    rep_r[:rw, :], ALU.mult))
                sc.op("dve", lambda e, j=j, rw=rw: e.scalar_tensor_tensor(
                    xpo[j][:rw, :], rep_m[:rw, :], ncs_sb[:rw, j:j + 1],
                    t1b[:rw, :], ALU.mult, ALU.add))
                sc.wait("sp", SDVE, SDVE.n)
                sc.dma("sp", xp_in[128 * j:128 * j + rw, cw], xpo[j][:rw, :],
                       "st_xp")
            sc.wait("sp", SDVE, SDVE.n)
            sc.dma("sp", xp_in[c.nxp:c.nxp + 1, cw], ms_row[:], "st_xp")
            sc.dma("sp", xp_in[c.nxp + 1:c.nxp + 2, cw], rs_row[:], "st_xp")
            sc.wait("pe", SDVE, SDVE.n)

        sc.wdma("pool", "st_xp")
        sc.op("pool", lambda e: e.collective_compute(
            "AllGather", ALU.bypass, replica_groups=[list(range(8))],
            ins=[xp_in.tensor.ap().opt()], outs=[xp_out.tensor.ap().opt()]),
            incs=((CC0, 1),), chain=False)
        if c.debug_taps:
            sc.wait("pool", CC0, CC0.n)
            sc.dma("pool", taps["t_xp"], xp_out, "st_tap")
        AR.reset()
        sc.barrier()

        AR.mark()
        TH = c.th
        nsw = TH // c.sw
        xpd_bf = AR.alloc(BF16, TH)
        mrep = AR.alloc(F32, TH)
        rrep = AR.alloc(F32, TH)
        dt4 = [AR.alloc(F32, TH) for _ in range(c.ndt)]
        dtx4 = [AR.alloc(BF16, TH) for _ in range(c.ndt)]
        dAb = [AR.alloc(F32, TH) for _ in range(2)]
        _dbx = AR.alloc(BF16, TH)
        dBxb = [_dbx, _dbx]
        _h = AR.alloc(BF16, TH)
        hbuf = [_h, _h]
        hcb = [AR.alloc(BF16, TH) for _ in range(2)]
        Brep = [AR.alloc(BF16, TH) for _ in range(2)]
        Crep = [AR.alloc(BF16, TH) for _ in range(2)]
        _xs = AR.alloc(F32, TH)
        xsl = [_xs, _xs]
        scr1 = dAb[0]
        scr2m = dAb[1]

        brow = [AR.alloc(BF16, TH, rows=1) for _ in range(2)]

        sc.wait("pool", CC0, 1)
        for r8 in range(8):
            rb0 = r8 * nxs
            for row in range(2 * c.d_state):
                sl = row % 2
                sc.wait("pool", PE, PE.n)
                sc.dma("pool", brow[sl],
                       xp_out[rb0 + c.dt_rank + row:
                              rb0 + c.dt_rank + row + 1, :], "m_br")
                sc.wdma("pe", "m_br")
                sc.wait("pe", SACT, SACT.n)
                for wv in range(nsw):
                    sc.op("pe", lambda e, sl=sl, wv=wv: e.matmul(
                        bank(4 + wv, w=c.sw), onesrb_sb[:],
                        brow[sl][:, wv * c.sw:(wv + 1) * c.sw],
                        start=True, stop=True),
                        incs=((PE, 1),) if wv == nsw - 1 else ())
                sc.wait("act", PE, PE.n)
                sc.wdma("act", "st_bcr")
                sc.op("act", lambda e, sl=sl: e.activation(
                    Brep[sl], PS[:, 2048:2048 + TH], AF.Copy))
                sc.wait("sp", SACT, SACT.n)
                sc.dma("sp", bcrep_d[r8, :, row * TH:(row + 1) * TH],
                       Brep[sl], "st_bcr")
                sc.wait("act", PE, PE.n)
        sc.wdma("sp", "st_bcr")

        npass = (c.ndt + 1) // 2
        for b in range(c.batch):
            for th in range(2):
                r = 2 * b + th
                rb = r * nxs
                tok0 = b * c.seq + th * TH
                sc.wait("sp", CC0, 1)
                sc.wait("sp", SDVE, SDVE.n)
                sc.dma("sp", mrep,
                       xp_out[rb + c.nxp:rb + c.nxp + 1, :]
                       .broadcast_to([128, TH]), "m_mr")
                sc.dma("sp", rrep,
                       xp_out[rb + c.nxp + 1:rb + c.nxp + 2, :]
                       .broadcast_to([128, TH]), "m_mr")
                sc.wait("pool", CC0, 1)
                sc.wait("pool", PE, PE.n)
                sc.dma("pool", xpd_bf, xp_out[rb:rb + 128, :], "m_xpd")
                for a in range(c.ndt):
                    sc.wdma("pe", "m_xpd")
                    sc.wait("pe", SACT, SACT.n)
                    sc.wait("pe", SDVE, SDVE.n)
                    for wv in range(nsw):
                        sc.op("pe", lambda e, a=a, wv=wv: e.matmul(
                            bank(wv, w=c.sw),
                            dtwT_sb[:, a * 128:(a + 1) * 128],
                            xpd_bf[:, c.sw * wv:c.sw * (wv + 1)],
                            start=True, stop=True),
                            incs=((PE, 1),) if wv == nsw - 1 else ())
                    sc.wait("act", PE, PE.n)
                    sc.wait("act", SDVE, SDVE.n)
                    sc.op("act", lambda e, a=a: e.activation(
                        scr1, PS[:, :TH], AF.Exp, bias=dtb_sb[:, a:a + 1]), wide=True)
                    sc.op("act", lambda e, a=a: e.activation(
                        dt4[a], scr1, AF.Ln, bias=one1_sb[:]), wide=True)
                for a in range(c.ndt):
                    xb = xsl[a % 2]
                    key = f"m_x{a % 2}"
                    sc.wait("sp", SDVE, SDVE.n)
                    sc.dma("sp", xb,
                           xT_d[a * 128:(a + 1) * 128, tok0:tok0 + TH], key)
                    sc.wdma("dve", key)
                    sc.wdma("dve", "m_mr")
                    sc.wait("dve", SACT, SACT.n)
                    sc.op("dve", lambda e, xb=xb: e.tensor_tensor(
                        scr1, xb, rrep, ALU.mult), wide=True)
                    sc.op("dve", lambda e: e.tensor_tensor(
                        scr1, scr1, mrep, ALU.subtract), wide=True)
                    sc.op("dve", lambda e, a=a: e.tensor_tensor(
                        dtx4[a], dt4[a], scr1, ALU.mult), wide=True)
                for half in range(npass):
                    aa = [a for a in (2 * half, 2 * half + 1) if a < c.ndt]
                    snap_dA = {}
                    snap_scan = {}
                    snap_ymm = {}
                    base_sdve = SDVE.n
                    base_pe = PE.n
                    sc.wdma("act", "st_y")
                    for n in range(c.d_state):
                        emit_reblocks(6)
                        sl = n % 2
                        sc.wait("sp", SDVE, SDVE.n)
                        sc.dma("sp", Brep[sl],
                               bcrep_d[r, :, n * TH:(n + 1) * TH],
                               f"m_bc{sl}")
                        sc.dma("sp", Crep[sl],
                               bcrep_d[r, :, (c.d_state + n) * TH:
                                       (c.d_state + n + 1) * TH],
                               f"m_bc{sl}")
                        for ai, a in enumerate(aa):
                            it = n * len(aa) + ai
                            s2 = it % 2
                            ci = a * c.d_state + n
                            sc.wait("act", SDVE,
                                    snap_scan.get(it - 2, base_sdve))
                            sc.op("act", lambda e, a=a, s2=s2, ci=ci:
                                  e.activation(dAb[s2], dt4[a], AF.Exp,
                                               scale=A_sb[:, ci:ci + 1]),
                                  wide=True)
                            snap_dA[it] = SACT.n
                            sc.wdma("dve", f"m_bc{sl}")
                            sc.op("dve", lambda e, a=a, s2=s2, sl=sl:
                                  e.tensor_tensor(dBxb[s2], dtx4[a],
                                                  Brep[sl], ALU.mult),
                                  wide=True)
                            if th == 1:
                                sc.wait("dve", SACT, snap_dA[it])
                                sc.op("dve", lambda e, s2=s2, ci=ci:
                                      e.scalar_tensor_tensor(
                                          dBxb[s2][:, 0:1], dAb[s2][:, 0:1],
                                          carry_sb[:, ci:ci + 1],
                                          dBxb[s2][:, 0:1],
                                          ALU.mult, ALU.add))
                            sc.wait("dve", SACT, snap_dA[it])
                            sc.op("dve", lambda e, s2=s2: e.tensor_tensor_scan(
                                hbuf[s2], dAb[s2], dBxb[s2], 0.0,
                                ALU.mult, ALU.add), wide=True)
                            snap_scan[it] = SDVE.n
                            if th == 0:
                                sc.op("dve", lambda e, s2=s2, ci=ci:
                                      e.tensor_copy(carry_sb[:, ci:ci + 1],
                                                    hbuf[s2][:, TH - 1:TH]))
                            sc.wait("dve", PE, snap_ymm.get(it - 2, base_pe))
                            sc.op("dve", lambda e, s2=s2, sl=sl:
                                  e.tensor_tensor(hcb[s2], hbuf[s2],
                                                  Crep[sl], ALU.mult),
                                  wide=True)
                            sc.wait("pe", SDVE, SDVE.n)
                            for wv in range(nsw):
                                sc.op("pe", lambda e, ai=ai, s2=s2, wv=wv,
                                      n=n: e.matmul(
                                          bank(ai * nsw + wv, w=c.sw),
                                          idb_sb[:],
                                          hcb[s2][:, wv * c.sw:
                                                  (wv + 1) * c.sw],
                                          start=(n == 0),
                                          stop=(n == c.d_state - 1)),
                                      incs=((PE, 1),)
                                      if wv == nsw - 1 else ())
                            snap_ymm[it] = PE.n
                    for ai, a in enumerate(aa):
                        xb = xsl[a % 2]
                        key = f"m_x{a % 2}"
                        sc.wait("sp", SDVE, SDVE.n)
                        sc.dma("sp", xb,
                               xT_d[a * 128:(a + 1) * 128, tok0:tok0 + TH],
                               key)
                        sc.wdma("dve", key)
                        sc.wait("dve", SACT, SACT.n)
                        sc.op("dve", lambda e, xb=xb: e.tensor_tensor(
                            scr1, xb, rrep, ALU.mult), wide=True)
                        sc.op("dve", lambda e: e.tensor_tensor(
                            scr1, scr1, mrep, ALU.subtract), wide=True)
                        sc.op("dve", lambda e, a=a, xb=xb:
                              e.scalar_tensor_tensor(
                                  scr1, scr1, D_sb[:, a:a + 1], xb,
                                  ALU.mult, ALU.add), wide=True)
                        sc.wait("dve", PE, PE.n)
                        sc.op("dve", lambda e, ai=ai: e.tensor_tensor(
                            scr2m, scr1,
                            PS[:, ai * nsw * 512:ai * nsw * 512 + TH],
                            ALU.add), wide=True)
                        sc.wait("sp", SDVE, SDVE.n)
                        sc.dma("sp", y_in[c.dsl * r + 128 * a:
                                          c.dsl * r + 128 * (a + 1), :],
                               scr2m, "st_y")
                        sc.wdma("dve", "st_y")
        emit_reblocks(10 ** 9)
        sc.wdma("pool", "st_y")
        sc.wdma("pool", "conv")

        def _a2a_y(e):
            with allow_ici_all_to_all():
                return e.collective_compute(
                    "AllToAll", ALU.bypass, replica_groups=[list(range(8))],
                    ins=[y_in.tensor.ap().opt()],
                    outs=[y_out.tensor.ap().opt()])
        sc.op("pool", _a2a_y, incs=((CC1, 1),), chain=False)
        if c.debug_taps:
            sc.wait("pool", CC1, CC1.n)
            sc.dma("pool", taps["t_x2"], y_out, "st_tap")
        AR.reset()
        sc.barrier()

        def ffn_phase(x_dram, out_dram, z_dram):
            AR.mark()
            z2 = [AR.alloc(BF16, c.sblk) for _ in range(c.ndc)]
            h2 = [AR.alloc(BF16, c.sblk) for _ in range(c.nic)]
            xlat = [AR.alloc(BF16, c.sblk) for _ in range(c.nlc)]
            moe = [AR.alloc(BF16, c.sblk) for _ in range(c.nlc)]
            pan = [AR.alloc(BF16, 2048) for _ in range(3)]
            x2c = [AR.alloc(F32, c.sblk) for _ in range(2)]
            sqs = AR.alloc(F32, c.sblk)
            sqsb = AR.alloc(BF16, c.sblk)
            repr_ = AR.alloc(F32, c.sblk)
            repm = AR.alloc(F32, c.sblk)
            scr2 = AR.alloc(F32, c.sblk)
            scr3 = AR.alloc(F32, c.sblk)
            gt = AR.alloc(F32, c.sblk)
            lgt = AR.alloc(F32, c.n_exp)
            nbig = AR.alloc(F32, c.n_exp)
            m12 = AR.alloc(F32, 2)
            msc = AR.alloc(F32, max(c.sblk // 128, 1))
            msrep = AR.alloc(BF16, c.sblk)
            fev = [AR.alloc(F32, c.sblk) for _ in range(2)]
            nsub = c.sblk // 128

            def strip_stats(src_dram, cw):
                sc.wdma("dve", "st_z3")
                for k in range(c.ndc):
                    buf = x2c[k % 2]
                    key = f"f_x{k % 2}"
                    if k >= 2:
                        sc.wait("sp", PE, PE.n)
                    sc.wait("sp", SDVE, SDVE.n)
                    sc.dma("sp", buf, src_dram[k * 128:(k + 1) * 128, cw],
                           key)
                    sc.wdma("dve", key)
                    sc.wait("dve", PE, PE.n)
                    sc.op("dve", lambda e, b=buf, k=k: e.tensor_copy(
                        z2[k], b), wide=True)
                    sc.wait("act", SDVE, SDVE.n)
                    sc.wait("act", PE, PE.n)
                    sc.op("act", lambda e, k=k: e.activation(
                        sqsb, z2[k], AF.Square), wide=True)
                    sc.wait("pe", SACT, SACT.n)
                    sc.op("pe", lambda e, k=k: e.matmul(
                        bank(6, rows=1, w=c.sblk), onesb_sb[:], z2[k],
                        start=(k == 0), stop=(k == c.ndc - 1)))
                    sc.op("pe", lambda e, k=k: e.matmul(
                        bank(7, rows=1, w=c.sblk), onesb_sb[:], sqsb,
                        start=(k == 0), stop=(k == c.ndc - 1)),
                        incs=((PE, 1),))
                ln_finalize(mu_row[:, :c.sblk], rs_row[:, :c.sblk],
                            ms_row[:, :c.sblk], c.sblk, c.d_model, 6, 7)
                replicate(rs_row[:, :c.sblk], repr_, 6)
                replicate(ms_row[:, :c.sblk], repm, 7)

            for w in range(c.nsblk):
                cw = slice(c.sblk * w, c.sblk * (w + 1))
                strip_stats(x_dram, cw)
                for k in range(c.ndc):
                    sc.wait("dve", SACT, SACT.n)
                    sc.op("dve", lambda e, k=k: e.tensor_tensor(
                        scr2, z2[k], repr_, ALU.mult), wide=True)
                    sc.op("dve", lambda e, k=k: e.tensor_tensor(
                        z2[k], scr2, repm, ALU.subtract), wide=True)
                sc.wait("pe", SDVE, SDVE.n)
                for k in range(c.ndc):
                    sc.op("pe", lambda e, k=k: e.matmul(
                        bank(0, rows=c.n_exp, w=c.sblk),
                        gwT_sb[:, k * c.n_exp:(k + 1) * c.n_exp], z2[k],
                        start=(k == 0), stop=(k == c.ndc - 1)),
                        incs=((PE, 1),) if k == c.ndc - 1 else ())
                sc.wait("act", PE, PE.n)
                sc.op("act", lambda e: e.activation(
                    gt[:c.n_exp, :], bank(0, rows=c.n_exp, w=c.sblk),
                    AF.Copy))
                sc.wait("pe", SACT, SACT.n)
                for i in range(nsub):
                    sc.op("pe", lambda e, i=i: e.transpose(
                        PS[:, 512 + i * 128:512 + i * 128 + c.n_exp],
                        gt[:c.n_exp, i * 128:(i + 1) * 128],
                        id32[:c.n_exp, :c.n_exp]),
                        incs=((PE, 1),) if i == nsub - 1 else ())
                sc.wait("dve", PE, PE.n)
                sc.op("dve", lambda e: e.memset(nbig, -1e30))
                for i in range(nsub):
                    sc.op("dve", lambda e, i=i: e.tensor_copy(
                        lgt, PS[:, 512 + i * 128:512 + i * 128 + c.n_exp]))
                    sc.op("dve", lambda e: e.reduce_max(
                        m12[:, 0:1], lgt, axis=AX.X))
                    sc.op("dve", lambda e: e.tensor_scalar(
                        eqm[:], lgt, m12[:, 0:1], None, ALU.is_equal))
                    sc.op("dve", lambda e: e.copy_predicated(
                        lgt, eqm[:], nbig))
                    sc.op("dve", lambda e: e.reduce_max(
                        m12[:, 1:2], lgt, axis=AX.X))
                    sc.op("dve", lambda e, i=i: e.tensor_tensor(
                        msc[:, i:i + 1], m12[:, 0:1], m12[:, 1:2], ALU.add))
                sc.wait("pe", SDVE, SDVE.n)
                sc.op("pe", lambda e: e.transpose(
                    PS[:nsub, 512:640], msc[:, :nsub], id32[:, :]),
                    incs=((PE, 1),))
                sc.wait("dve", PE, PE.n)
                sc.wdma("dve", "f_mrow")
                sc.op("dve", lambda e: e.tensor_copy(
                    msct[:nsub, :], PS[:nsub, 512:640]))
                sc.wait("sp", SDVE, SDVE.n)
                sc.wait("sp", PE, PE.n)
                for i in range(nsub):
                    sc.dma("sp", m_row[:, i * 128:(i + 1) * 128],
                           msct[i:i + 1, :], "f_mrow")
                sc.wdma("pe", "f_mrow")
                replicate(m_row[:], msrep, 1)
                sc.wdma("sp", "rbk")
                pan_snap = {0: PE.n, 1: PE.n, 2: PE.n}
                evac_snap = {}
                pslot = [3]

                def pair_mms(pbname, jj, rhs_list, b0, extra=False,
                             st_first=True, sp_last=True):
                    wbf, pb, kc, ncc, oct_, npj, nkq, cols = PBW[pbname]
                    for kq in range(nkq):
                        sl = pslot[0] % 3
                        if pslot[0] >= 3:
                            sc.wait("sp", PE, pan_snap[pslot[0] - 3])
                        sc.dma("sp", pan[sl][:, :oct_ * 256],
                               pb[jj * nkq + kq, :, :], f"f_p{sl}")
                        sc.wdma("pe", f"f_p{sl}")
                        for a in range(oct_):
                            k = kq * oct_ + a
                            for gi in range(2):
                                o = (a * 2 + gi) * 128
                                st = st_first and (k == 0)
                                sp = sp_last and (k == kc - 1)
                                last = (a == oct_ - 1 and gi == 1)
                                sc.op("pe", lambda e, o=o, gi=gi, sl=sl,
                                      b0=b0, st=st, sp=sp, k=k: e.matmul(
                                          bank(b0 + gi, w=c.sblk),
                                          pan[sl][:, o:o + 128],
                                          rhs_list[k],
                                          start=st, stop=sp),
                                      incs=((PE, 1),) if last else ())
                        pan_snap[pslot[0]] = PE.n
                        pslot[0] += 1

                NPAIR = (c.nic + c.nlc) // 2
                for g in range(NPAIR):
                    b0 = (g % 2) * 2
                    if g >= 2:
                        sc.wait("pe", SACT, evac_snap[g - 2][0])
                        sc.wait("pe", SDVE, evac_snap[g - 2][1])
                    else:
                        sc.wait("pe", SACT, SACT.n)
                        sc.wait("pe", SDVE, SDVE.n)
                    if g < c.nic // 2:
                        pair_mms("up", g, z2, b0)
                        ics = [2 * g, 2 * g + 1]
                    else:
                        jj = g - c.nic // 2
                        pair_mms("fc1", jj, z2, b0)
                        ics = [c.nic + 2 * jj, c.nic + 2 * jj + 1]
                    sc.wait("act", PE, PE.n)
                    sc.wait("act", SDVE, SDVE.n)
                    for gi, ic in enumerate(ics):
                        if ic < c.nic:
                            sc.op("act", lambda e, gi=gi, b0=b0: e.activation(
                                fev[gi], bank(b0 + gi, w=c.sblk), AF.Relu), wide=True)
                            sc.wait("dve", SACT, SACT.n)
                            sc.op("dve", lambda e, ic=ic, gi=gi:
                                  e.tensor_tensor(h2[ic], fev[gi],
                                                  fev[gi], ALU.mult), wide=True)
                        else:
                            sc.op("act", lambda e, ic=ic, b0=b0, gi=gi:
                                  e.activation(xlat[ic - c.nic],
                                               bank(b0 + gi, w=c.sblk),
                                               AF.Copy), wide=True)
                    evac_snap[g] = (SACT.n, SDVE.n)

                for g in range(c.nlc // 2):
                    b0 = (g % 2) * 2
                    if g < 2:
                        sc.wait("pe", SACT, SACT.n)
                        sc.wait("pe", SDVE, SDVE.n)
                    else:
                        sc.wait("pe", SDVE, evac_snap[("e", g - 2)][0])
                    pair_mms("exp", g, xlat, b0)
                    sc.wait("dve", PE, PE.n)
                    for gi in range(2):
                        sc.op("dve", lambda e, g=g, gi=gi, b0=b0:
                              e.tensor_tensor(moe[2 * g + gi],
                                              bank(b0 + gi, w=c.sblk),
                                              msrep, ALU.mult), wide=True)
                    evac_snap[("e", g)] = (SDVE.n,)

                ocnt = [0]
                for g in range(c.ndc // 2):
                    b0 = (g % 2) * 2
                    if g < 2:
                        sc.wait("pe", SACT, SACT.n)
                        sc.wait("pe", SDVE, SDVE.n)
                    else:
                        sc.wait("pe", SACT, evac_snap[("d", g - 2)][0])
                    pair_mms("down", g, h2, b0, st_first=True, sp_last=False)
                    pair_mms("fc2", g, moe, b0, st_first=False, sp_last=True)
                    sc.wait("act", PE, PE.n)
                    for gi in range(2):
                        o = 2 * g + gi
                        sc.wdma("act", "st_fa")
                        sc.op("act", lambda e, gi=gi, b0=b0: e.activation(
                            fev[gi], bank(b0 + gi, w=c.sblk), AF.Copy), wide=True)
                        sc.wait("act", PE, PE.n)
                        sc.op("act", lambda e, gi=gi: e.activation(
                            sqs, fev[gi], AF.Square), wide=True)
                        sc.wait("pe", SACT, SACT.n)
                        st0 = (ocnt[0] == 0)
                        fin = (ocnt[0] == c.ndc - 1)
                        sc.op("pe", lambda e, gi=gi, st0=st0, fin=fin:
                              e.matmul(bank(6, rows=1, w=c.sblk),
                                       onesf_sb[:], fev[gi],
                                       start=st0, stop=fin))
                        sc.op("pe", lambda e, st0=st0, fin=fin: e.matmul(
                            bank(7, rows=1, w=c.sblk), onesf_sb[:], sqs,
                            start=st0, stop=fin), incs=((PE, 1),))
                        ocnt[0] += 1
                        sc.wait("sp", SACT, SACT.n)
                        sc.dma("sp", facc_d[o * 128:(o + 1) * 128,
                                            (w % 2) * c.sblk:
                                            (w % 2 + 1) * c.sblk],
                               fev[gi], "st_fa")
                    evac_snap[("d", g)] = (SACT.n,)
                ln_finalize(mu_row[:, :c.sblk], rs_row[:, :c.sblk],
                            ms_row[:, :c.sblk], c.sblk, c.d_model, 6, 7)
                replicate(rs_row[:, :c.sblk], repr_, 6)
                replicate(ms_row[:, :c.sblk], repm, 7)
                sc.wdma("sp", "st_fa")
                sc.wait("sp", PE, PE.n)
                for k in range(c.ndc):
                    buf = x2c[k % 2]
                    fv = fev[k % 2]
                    key = f"f_x{k % 2}"
                    sc.wait("sp", SDVE, SDVE.n)
                    sc.wait("sp", SACT, SACT.n)
                    sc.dma("sp", buf, x_dram[k * 128:(k + 1) * 128, cw], key)
                    sc.dma("sp", fv, facc_d[k * 128:(k + 1) * 128,
                                            (w % 2) * c.sblk:
                                            (w % 2 + 1) * c.sblk], key)
                    sc.wdma("dve", key)
                    sc.wait("dve", SACT, SACT.n)
                    sc.op("dve", lambda e, fv=fv: e.tensor_tensor(
                        scr2, fv, repr_, ALU.mult), wide=True)
                    sc.op("dve", lambda e: e.tensor_tensor(
                        scr2, scr2, repm, ALU.subtract), wide=True)
                    sc.wait("act", SDVE, SDVE.n)
                    sc.wdma("act", "st_out")
                    sc.op("act", lambda e, k=k: e.activation(
                        scr3, scr2, AF.Identity,
                        bias=lnb_sb[:, k:k + 1], scale=lng_sb[:, k:k + 1]), wide=True)
                    sc.wait("dve", SACT, SACT.n)
                    sc.op("dve", lambda e, buf=buf: e.tensor_tensor(
                        scr3, scr3, buf, ALU.add), wide=True)
                    sc.wait("sp", SDVE, SDVE.n)
                    sc.dma("sp", out_dram[k * 128:(k + 1) * 128, cw],
                           scr3, "st_out")
                if z_dram is not None:
                    sc.wdma("sp", "st_out")
                    sc.wdma("dve", "st_z3")
                    strip_stats(out_dram, cw)
                    for k in range(c.ndc):
                        sc.wait("dve", SACT, SACT.n)
                        sc.op("dve", lambda e, k=k: e.tensor_tensor(
                            scr2, z2[k], repr_, ALU.mult), wide=True)
                        sc.op("dve", lambda e, k=k: e.tensor_tensor(
                            z2[k], scr2, repm, ALU.subtract), wide=True)
                        sc.wait("sp", SDVE, SDVE.n)
                        sc.dma("sp", z_dram[w, :, k * c.sblk:(k + 1) * c.sblk],
                               z2[k], "st_z3")
            AR.reset()

        sc.wait("sp", CC1, 1)
        sc.wait("pool", CC1, 1)
        ffn_phase(y_out, x3_d, z3_in)
        if c.debug_taps:
            sc.wdma("pool", "st_out", "st_z3")
            sc.dma("pool", taps["t_x3"], x3_d, "st_tap")
            sc.dma("pool", taps["t_z3"], z3_in, "st_tap")
        sc.barrier()

        if c.stop_after == "p3":
            sc.barrier()
            block = es.enter_context(nc.Block())
            sc.replay(block)
            return nc, taps
        AR.mark()
        z3r = AR.alloc(BF16, c.nsblk * c.ndc * c.sblk)
        wpan = [AR.alloc(BF16, c.ndc * 128) for _ in range(2)]
        qkvev = AR.alloc(BF16, c.tpc)
        nwv = c.tpc // c.sw
        sc.wdma("sp", "st_z3")
        for wv in range(c.nsblk):
            sc.dma("sp", z3r[:, wv * c.ndc * c.sblk:(wv + 1) * c.ndc * c.sblk],
                   z3_in[wv, :, :], "a_z3")
        NCC = 3 * c.heads
        for cc in range(NCC):
            sl = cc % 2
            b0 = (cc % 2) * 4
            if cc >= 2:
                sc.wait("pool", PE, PE.n)
            sc.dma("pool",
                   wpan[sl].rearrange("p (a f) -> p a f", a=c.ndc),
                   wqkv_u[:, cc * 128:(cc + 1) * 128]
                   .rearrange("(a p) f -> p a f", p=128), f"a_w{sl}")
            sc.wdma("pe", f"a_w{sl}")
            sc.wdma("pe", "a_z3")
            if cc >= 2:
                sc.wait("pe", SACT, SACT.n)
            for k in range(c.ndc):
                for wv in range(nwv):
                    sc.op("pe", lambda e, k=k, wv=wv, b0=b0, sl=sl: e.matmul(
                        bank(b0 + wv, w=c.sw),
                        wpan[sl][:, k * 128:(k + 1) * 128],
                        z3r[:, (wv * c.ndc + k) * c.sblk:
                            (wv * c.ndc + k) * c.sblk + c.sw],
                        start=(k == 0), stop=(k == c.ndc - 1)),
                        incs=((PE, 1),)
                        if (k == c.ndc - 1 and wv == nwv - 1) else ())
            sc.wait("act", PE, PE.n)
            sc.wdma("act", "st_qkv")
            for wv in range(nwv):
                sc.op("act", lambda e, wv=wv, b0=b0: e.activation(
                    qkvev[:, wv * c.sw:(wv + 1) * c.sw],
                    bank(b0 + wv, w=c.sw), AF.Copy))
            m, hg = divmod(cc, c.heads) if False else (cc // c.heads,
                                                       cc % c.heads)
            j = hg // c.hbc
            row = j * 3 * c.hbc * 128 + m * c.hbc * 128 + (hg % c.hbc) * 128
            sc.wait("sp", SACT, SACT.n)
            sc.dma("sp", qkv_in[row:row + 128, :], qkvev, "st_qkv")
        sc.wdma("pool", "st_qkv")

        def _a2a_qkv(e):
            with allow_ici_all_to_all():
                return e.collective_compute(
                    "AllToAll", ALU.bypass, replica_groups=[list(range(8))],
                    ins=[qkv_in.tensor.ap().opt()],
                    outs=[qkv_out.tensor.ap().opt()])
        sc.op("pool", _a2a_qkv, incs=((CC2, 1),), chain=False)
        AR.reset()
        sc.barrier()

        AR.mark()
        qsb = AR.alloc(BF16, c.seq)
        ksb = AR.alloc(BF16, c.seq)
        vsb = AR.alloc(BF16, c.seq)
        vstg = AR.alloc(F32, 128)
        vtT = AR.alloc(BF16, c.seq)
        ptb = [AR.alloc(BF16, 2 * c.qmac) for _ in range(2)]
        zrep = AR.alloc(F32, c.qmac)
        atev = AR.alloc(BF16, c.qmac)
        scale = 1.0 / math.sqrt(c.dq)
        sc.wait("sp", CC2, 1)
        for b in range(c.batch):
            for hhl in range(c.hbc):
                sc.wait("sp", PE, PE.n)
                sc.wait("sp", SDVE, SDVE.n)
                for half in range(2):
                    i = 2 * b + half
                    base = i * 3 * c.hbc * 128
                    sc.dma("sp", qsb[:, half * c.th:(half + 1) * c.th],
                           qkv_out[base + hhl * 128:
                                   base + (hhl + 1) * 128, :], "a_qkv")
                    sc.dma("sp", ksb[:, half * c.th:(half + 1) * c.th],
                           qkv_out[base + (c.hbc + hhl) * 128:
                                   base + (c.hbc + hhl + 1) * 128, :],
                           "a_qkv")
                    sc.dma("sp", vtT[:, half * c.th:(half + 1) * c.th],
                           qkv_out[base + (2 * c.hbc + hhl) * 128:
                                   base + (2 * c.hbc + hhl + 1) * 128, :],
                           "a_qkv")
                sc.wdma("dve", "a_qkv")
                for q in range(c.seq // 128):
                    sc.wait("dve", PE, PE.n)
                    sc.op("dve", lambda e, q=q: e.tensor_copy(
                        vstg, vtT[:, q * 128:(q + 1) * 128]))
                    sc.wait("pe", SDVE, SDVE.n)
                    sc.wait("pe", SACT, SACT.n)
                    sc.op("pe", lambda e, q=q: e.transpose(
                        PS[:, 7 * 512:7 * 512 + 128], vstg, id32[:, :]),
                        incs=((PE, 1),))
                    sc.wait("act", PE, PE.n)
                    sc.op("act", lambda e, q=q: e.activation(
                        vsb[:, q * 128:(q + 1) * 128],
                        PS[:, 7 * 512:7 * 512 + 128], AF.Copy))
                sc.wdma("pe", "a_qkv")
                sc.wait("pe", SACT, SACT.n)
                for qm in range(c.nqm):
                    nk = (qm + 1) * c.ktpq
                    q0 = qm * c.qmac
                    for jp in range(0, nk, 2):
                        js = [jp] if jp + 1 >= nk else [jp, jp + 1]
                        sA = (jp // 2) % 2
                        sl2 = (jp // 2) % 2
                        sc.wait("pe", SACT, SACT.n)
                        sc.wait("pe", SDVE, SDVE.n)
                        for ji, j in enumerate(js):
                            sc.op("pe", lambda e, j=j, ji=ji, sA=sA, q0=q0:
                                  e.matmul(
                                      PS[:, sA * 1024 + ji * c.qmac:
                                         sA * 1024 + (ji + 1) * c.qmac],
                                      ksb[:, j * 128:(j + 1) * 128],
                                      qsb[:, q0:q0 + c.qmac],
                                      start=True, stop=True),
                                  incs=((PE, 1),) if ji == len(js) - 1
                                  else ())
                        sc.wait("act", PE, PE.n)
                        sc.wait("act", SDVE, SDVE.n)
                        np_ = len(js) * c.qmac
                        sc.op("act", lambda e, sA=sA, np_=np_, sl2=sl2:
                              e.activation(ptb[sl2][:, :np_],
                                           PS[:, sA * 1024:sA * 1024 + np_],
                                           AF.Exp, scale=scale))
                        for ji, j in enumerate(js):
                            p = j - qm * c.ktpq
                            if p >= 0:
                                sc.wait("dve", SACT, SACT.n)
                                sc.op("dve", lambda e, ji=ji, p=p, sl2=sl2:
                                      e.tensor_tensor(
                                          ptb[sl2][:, ji * c.qmac:
                                                   (ji + 1) * c.qmac],
                                          ptb[sl2][:, ji * c.qmac:
                                                   (ji + 1) * c.qmac],
                                          masks_sb[:, p * c.qmac:
                                                   (p + 1) * c.qmac],
                                          ALU.mult))
                        sc.wait("pe", SACT, SACT.n)
                        sc.wait("pe", SDVE, SDVE.n)
                        for ji, j in enumerate(js):
                            sc.op("pe", lambda e, ji=ji, j=j, nk=nk, sl2=sl2:
                                  e.matmul(
                                      bank(4, w=c.qmac),
                                      vsb[:, j * 128:(j + 1) * 128],
                                      ptb[sl2][:, ji * c.qmac:
                                               (ji + 1) * c.qmac],
                                      start=(j == 0), stop=(j == nk - 1)))
                            sc.op("pe", lambda e, ji=ji, j=j, nk=nk, sl2=sl2:
                                  e.matmul(
                                      bank(5, rows=1, w=c.qmac),
                                      onesb_sb[:],
                                      ptb[sl2][:, ji * c.qmac:
                                               (ji + 1) * c.qmac],
                                      start=(j == 0), stop=(j == nk - 1)),
                                  incs=((PE, 1),) if j == nk - 1 else ())
                    sc.wait("dve", PE, PE.n)
                    sc.op("dve", lambda e: e.reciprocal(
                        z_row[:], bank(5, rows=1, w=c.qmac)))
                    replicate(z_row[:], zrep, 6)
                    sc.wait("dve", SACT, SACT.n)
                    sc.wdma("dve", "st_at")
                    sc.op("dve", lambda e: e.tensor_tensor(
                        atev, bank(4, w=c.qmac), zrep, ALU.mult))
                    jq = 2 * b + (1 if q0 >= c.th else 0)
                    row = jq * c.hbc * 128 + hhl * 128
                    col = q0 % c.th
                    sc.wait("sp", SDVE, SDVE.n)
                    sc.dma("sp", at_in[row:row + 128, col:col + c.qmac],
                           atev, "st_at")
                    sc.wait("pe", SDVE, SDVE.n)
        sc.wdma("pool", "st_at")

        def _a2a_at(e):
            with allow_ici_all_to_all():
                return e.collective_compute(
                    "AllToAll", ALU.bypass, replica_groups=[list(range(8))],
                    ins=[at_in.tensor.ap().opt()],
                    outs=[at_out.tensor.ap().opt()])
        sc.op("pool", _a2a_at, incs=((CC3, 1),), chain=False)
        if c.debug_taps:
            sc.wait("pool", CC3, CC3.n)
            sc.dma("pool", taps["t_at"], at_out, "st_tap")
        AR.reset()
        sc.barrier()

        AR.mark()
        ae = [AR.alloc(BF16, c.tpc) for _ in range(c.nec)]
        wpan2 = [AR.alloc(BF16, (c.nec // 4) * 512) for _ in range(2)]
        x3c = AR.alloc(F32, c.tpc)
        x4c = AR.alloc(F32, c.tpc)
        sc.wait("sp", CC3, 1)
        for k in range(c.nec):
            sc.dma("sp", ae[k], at_out[k * 128:(k + 1) * 128, :], "c_ae")
        for o in range(c.ndc):
            sl = o % 2
            b0 = sl * 4
            if o >= 2:
                sc.wait("pool", PE, PE.n)
            for e4 in range(c.nec // 4):
                sc.dma("pool",
                       wpan2[sl][:, e4 * 512:e4 * 512 + 512]
                       .rearrange("p (a f) -> p a f", a=4),
                       wo[e4 * 512:(e4 + 1) * 512, o * 128:(o + 1) * 128]
                       .rearrange("(a p) f -> p a f", p=128), f"c_w{sl}")
            sc.wdma("pe", "c_ae")
            sc.wdma("pe", f"c_w{sl}")
            if o >= 2:
                sc.wait("pe", SDVE, SDVE.n)
            for ec in range(c.nec):
                e4, er = divmod(ec, 4)
                po = e4 * 512 + er * 128
                for wv in range(nwv):
                    sc.op("pe", lambda e, po=po, ec=ec, wv=wv, b0=b0, sl=sl:
                          e.matmul(bank(b0 + wv, w=c.sw),
                                   wpan2[sl][:, po:po + 128],
                                   ae[ec][:, wv * c.sw:(wv + 1) * c.sw],
                                   start=(ec == 0), stop=(ec == c.nec - 1)),
                          incs=((PE, 1),) if (ec == c.nec - 1
                                              and wv == nwv - 1) else ())
            sc.wait("sp", SDVE, SDVE.n)
            sc.wdma("sp", "st_out")
            sc.dma("sp", x3c, x3_d[o * 128:(o + 1) * 128, :], "c_x3")
            sc.wait("dve", PE, PE.n)
            sc.wdma("dve", "c_x3", "st_x4")
            for wv in range(nwv):
                sc.op("dve", lambda e, wv=wv, b0=b0: e.tensor_tensor(
                    x4c[:, wv * c.sw:(wv + 1) * c.sw],
                    bank(b0 + wv, w=c.sw),
                    x3c[:, wv * c.sw:(wv + 1) * c.sw], ALU.add))
            sc.wait("sp", SDVE, SDVE.n)
            sc.dma("sp", x4_d[o * 128:(o + 1) * 128, :], x4c, "st_x4")
        if c.debug_taps:
            sc.wdma("pool", "st_x4")
            sc.dma("pool", taps["t_x4"], x4_d, "st_tap")
        AR.reset()
        sc.barrier()

        sc.wdma("sp", "st_x4")
        ffn_phase(x4_d, out_xT, None)

        for nm, s in list(sc.sems.items()):
            if nm.startswith("dma_"):
                sc.wait("pool", s)
        sc.barrier()

        block = es.enter_context(nc.Block())
        sc.replay(block)

    return nc, taps



def _make_consts(c):
    import ml_dtypes
    ident = np.eye(128, dtype=np.float32)
    masks = np.zeros((128, c.ktpq * c.qmac), dtype=np.float32)
    for p in range(c.ktpq):
        k = np.arange(128)[:, None]
        q = np.arange(c.qmac)[None, :]
        masks[:, p * c.qmac:(p + 1) * c.qmac] = (128 * p + k <= q)
    return {
        "ident": ident,
        "ones_b": np.ones((128, 1), np.float32).astype(ml_dtypes.bfloat16),
        "ones_f": np.ones((128, 1), np.float32),
        "onesr_f": np.ones((1, 128), np.float32),
        "masks": masks.astype(ml_dtypes.bfloat16),
    }


_NC_CACHE = {}


def kernel(L0_attn_a_x, L0_attn_a_A_log, L0_attn_a_D, L0_attn_a_x_proj_w,
           L0_attn_a_dt_proj_w, L0_attn_a_dt_proj_b, L0_attn_b_wq,
           L0_attn_b_wkv, L0_attn_b_wo, L0_ffn_gate_w, L0_ffn_up_w,
           L0_ffn_down_w, L0_ffn_fc1_w, L0_ffn_fc2_w, L0_ffn_expert_w,
           L0_ffn_ln_gamma, L0_ffn_ln_beta):
    from concourse.bass_utils import run_bass_kernel_spmd

    c = Cfg(d_model=4096, seq=4096, batch=4, ffn_int=8192, ffn_lat=2048,
            relax=True)
    x = np.asarray(L0_attn_a_x, dtype=np.float32)
    xT = np.ascontiguousarray(x.reshape(c.tokens, c.d_model).T)

    f32 = lambda a: np.ascontiguousarray(np.asarray(a, dtype=np.float32))
    wq = f32(L0_attn_b_wq)
    wkv = f32(L0_attn_b_wkv)
    wq_u = np.concatenate(
        [wq[:, i * c.dq:i * c.dq + c.dn] for i in range(c.heads)], axis=1)
    wk_u = np.concatenate(
        [wkv[:, i * c.dn:(i + 1) * c.dn] for i in range(c.heads)], axis=1)
    wv_u = np.concatenate(
        [wkv[:, c.heads * c.dn + i * c.dv:c.heads * c.dn + (i + 1) * c.dv]
         for i in range(c.heads)], axis=1)
    wqkv_u = np.ascontiguousarray(np.concatenate([wq_u, wk_u, wv_u], axis=1))
    consts = _make_consts(c)

    A_log = f32(L0_attn_a_A_log)
    D = f32(L0_attn_a_D)
    dtw = f32(L0_attn_a_dt_proj_w)
    dtb = f32(L0_attn_a_dt_proj_b)
    shared = {
        "xpw": f32(L0_attn_a_x_proj_w),
        "wqkv_u": wqkv_u,
        "wo": f32(L0_attn_b_wo),
        "gate_wT": np.ascontiguousarray(f32(L0_ffn_gate_w).T),
        "up_w": f32(L0_ffn_up_w),
        "down_w": f32(L0_ffn_down_w),
        "fc1_w": f32(L0_ffn_fc1_w),
        "fc2_w": f32(L0_ffn_fc2_w),
        "exp_w": f32(L0_ffn_expert_w),
        "ln_g": np.ascontiguousarray(
            f32(L0_ffn_ln_gamma).reshape(c.ndc, 128).T),
        "ln_b": np.ascontiguousarray(
            f32(L0_ffn_ln_beta).reshape(c.ndc, 128).T),
    }
    shared.update(consts)

    maps = []
    for core in range(8):
        sl = slice(c.dsl * core, c.dsl * (core + 1))
        m = dict(shared)
        m["xT_t"] = np.ascontiguousarray(
            xT[:, c.tpc * core:c.tpc * (core + 1)])
        m["xT_d"] = np.ascontiguousarray(xT[sl, :])
        m["A_log_sl"] = np.ascontiguousarray(
            A_log[sl].reshape(c.ndt, 128, c.d_state)
            .transpose(1, 0, 2).reshape(128, c.ndt * c.d_state))
        m["D_sl"] = np.ascontiguousarray(D[sl].reshape(c.ndt, 128).T)
        m["dtw_sl"] = np.ascontiguousarray(
            dtw[sl].reshape(c.ndt, 128, 128)
            .transpose(1, 0, 2).reshape(128, c.ndt * 128))
        m["dtb_sl"] = np.ascontiguousarray(dtb[sl].reshape(c.ndt, 128).T)
        maps.append(m)

    key = (c.d_model, c.seq)
    if key not in _NC_CACHE:
        _NC_CACHE[key] = build(c)
    nc, _ = _NC_CACHE[key]
    res = run_bass_kernel_spmd(nc, maps, core_ids=list(range(8)))
    outT = np.concatenate(
        [np.asarray(r["out_xT"]) for r in res.results], axis=1)
    return np.ascontiguousarray(outT.T).reshape(c.batch, c.seq, c.d_model)


# version 19
# speedup vs baseline: 1.0490x; 1.0327x over previous
import numpy as np

import math
from contextlib import ExitStack

import concourse.bass as bass
import concourse.mybir as mybir
from concourse.aot_env import allow_ici_all_to_all

F32 = mybir.dt.float32
BF16 = mybir.dt.bfloat16
U8 = mybir.dt.uint8
AF = mybir.ActivationFunctionType
ALU = mybir.AluOpType
AX = mybir.AxisListType

ENGS = ("sp", "act", "dve", "pe", "pool")
SELF_CHAIN = ("act", "dve", "pool")


class Cfg:
    def __init__(self, d_model=4096, seq=4096, batch=4, ffn_int=8192,
                 ffn_lat=2048, debug_taps=False, stop_after=None,
                 relax=True):
        self.stop_after = stop_after
        self.relax = relax
        self.d_model = d_model
        self.seq = seq
        self.batch = batch
        self.ffn_int = ffn_int
        self.ffn_lat = ffn_lat
        self.debug_taps = debug_taps
        self.d_state = 16
        self.dt_rank = 128
        self.heads = 32
        self.dn = 128
        self.dv = 128
        self.dq = 192
        self.n_exp = 8
        self.ln_eps = 1e-5
        assert batch == 4
        self.tokens = batch * seq
        self.tpc = self.tokens // 8
        self.dsl = d_model // 8
        self.ndt = self.dsl // 128
        self.ndc = d_model // 128
        self.th = seq // 2
        assert self.th == self.tpc
        self.hpc = self.heads // 2
        self.hbc = self.heads // 8
        self.nec = self.heads * self.dv // 128
        self.sblk = min(512, self.tpc)
        self.nsblk = self.tpc // self.sblk
        self.sw = min(512, self.tpc)
        self.nst = self.tpc // self.sw
        self.qmac = min(512, seq // 2)
        self.nqm = seq // self.qmac
        self.ktpq = self.qmac // 128
        self.tb = min(512, self.th)
        self.ntb = seq // self.tb
        self.nic = ffn_int // 128
        self.nlc = ffn_lat // 128
        self.nxp = self.dt_rank + 2 * self.d_state
        self.nrc = math.ceil(self.nxp / 128)


class Sem:
    def __init__(self, nc, es, name):
        self.h = es.enter_context(nc.semaphore(name))
        self.n = 0


class Sched:
    def __init__(self, nc, es):
        self.nc = nc
        self.es = es
        self.ops = {e: [] for e in ENGS}
        self.sems = {}
        self.waited = {}
        self.last_wide = {}
        self.relax = False
        for e in SELF_CHAIN:
            self.sems["self_" + e] = Sem(nc, es, "self_" + e)

    def sem(self, name):
        if name not in self.sems:
            self.sems[name] = Sem(self.nc, self.es, name)
        return self.sems[name]

    def wait(self, eng, sem, val=None):
        v = sem.n if val is None else val
        key = (eng, id(sem))
        if v <= 0 or self.waited.get(key, -1) >= v:
            return
        self.waited[key] = v
        self.ops[eng].append(lambda e, s=sem.h, v=v: e.wait_ge(s, v))

    def op(self, eng, fn, incs=(), chain=True, wide=False):
        incs = list(incs)
        if eng in SELF_CHAIN and not chain:
            ss = self.sems["self_" + eng]
            self.wait(eng, ss, ss.n)
        if eng in SELF_CHAIN and chain:
            ss = self.sems["self_" + eng]
            if not (self.relax and eng == "dve" and wide
                    and self.last_wide.get(eng, False)):
                self.wait(eng, ss, ss.n)
            self.last_wide[eng] = wide
            incs.append((ss, 1))
        for s, k in incs:
            s.n += k

        def wrapped(e, fn=fn, incs=tuple(incs)):
            ins = fn(e)
            for s, k in incs:
                ins = ins.then_inc(s.h, k)
            return ins
        self.ops[eng].append(wrapped)

    def dma(self, eng, out, in_, key):
        if eng in SELF_CHAIN:
            self.last_wide[eng] = False
        sem = self.sem("dma_" + key)
        sem.n += 16

        def wrapped(e, out=out, in_=in_, s=sem.h):
            return e.dma_start(out=out, in_=in_).then_inc(s, 16)
        self.ops[eng].append(wrapped)

    def wdma(self, eng, *keys):
        for key in keys:
            if ("dma_" + key) in self.sems:
                self.wait(eng, self.sems["dma_" + key])

    def barrier(self):
        bar = self.sem("bar")
        for e in ENGS:
            bar.n += 1
            self.ops[e].append(lambda en, s=bar.h: en.nop().then_inc(s, 1))
        tgt = bar.n
        for e in ENGS:
            self.ops[e].append(lambda en, s=bar.h, v=tgt: en.wait_ge(s, v))

    def replay(self, block):
        sched = self

        @block.sync
        def _(e):
            for f in sched.ops["sp"]:
                f(e)

        @block.scalar
        def _(e):
            for f in sched.ops["act"]:
                f(e)

        @block.vector
        def _(e):
            for f in sched.ops["dve"]:
                f(e)

        @block.tensor
        def _(e):
            for f in sched.ops["pe"]:
                f(e)

        @block.gpsimd
        def _(e):
            for f in sched.ops["pool"]:
                f(e)


class Arena:
    def __init__(self, nc, es, cols_bf16):
        self.guard_h = es.enter_context(
            nc.sbuf_tensor("arena", [128, cols_bf16], BF16))
        self.hb = self.guard_h
        self.h32 = self.hb.bitcast(F32)
        self.cols = cols_bf16
        self.off = 0
        self.marks = []

    def alloc(self, dtype, cols, rows=128):
        e2 = cols * (2 if dtype == F32 else 1)
        self.off = (self.off + 31) & ~31
        assert self.off + e2 <= self.cols, \
            f"arena overflow {self.off}+{e2}>{self.cols}"
        o = self.off
        self.off += e2
        if dtype == BF16:
            return self.hb[:rows, o:o + cols]
        return self.h32[:rows, o // 2:o // 2 + cols]

    def mark(self):
        self.marks.append(self.off)

    def reset(self):
        self.off = self.marks.pop()


def build(cfg: Cfg):
    c = cfg
    nc = bass.Bass("TRN2", target_bir_lowering=False)

    def din(name, shape, dt=F32):
        return nc.dram_tensor(name, shape, dt, kind="ExternalInput").ap()

    xT_t = din("xT_t", [c.d_model, c.tpc])
    xT_d = din("xT_d", [c.dsl, c.tokens])
    wqkv_p = din("wqkv_p", [3 * c.heads, 128, c.ndc * 128])
    wo_p = din("wo_p", [c.ndc, 128, c.nec * 128])
    A_log_sl = din("A_log_sl", [128, c.ndt * c.d_state])
    D_sl = din("D_sl", [128, c.ndt])
    xpw = din("xpw", [c.nxp, c.d_model])
    dtw_sl = din("dtw_sl", [128, c.ndt * 128])
    dtb_sl = din("dtb_sl", [128, c.ndt])
    gate_wT = din("gate_wT", [c.n_exp, c.d_model])
    up_w = din("up_w", [c.d_model, c.ffn_int])
    down_w = din("down_w", [c.ffn_int, c.d_model])
    fc1_w = din("fc1_w", [c.d_model, c.ffn_lat])
    fc2_w = din("fc2_w", [c.ffn_lat, c.d_model])
    exp_w = din("exp_w", [c.ffn_lat, c.ffn_lat])
    ln_g = din("ln_g", [128, c.ndc])
    ln_b = din("ln_b", [128, c.ndc])
    ident = din("ident", [128, 128], F32)
    ones_b = din("ones_b", [128, 1], BF16)
    ones_f = din("ones_f", [128, 1], F32)
    onesr_f = din("onesr_f", [1, 128], F32)
    masks = din("masks", [128, c.ktpq * c.qmac], BF16)

    out_xT = nc.dram_tensor("out_xT", [c.d_model, c.tpc], F32,
                            kind="ExternalOutput").ap()

    def scr(name, shape, dt):
        return nc.dram_tensor(name, shape, dt).ap()

    nxs = c.nxp + 2
    xp_in = scr("xp_in", [nxs, c.tpc], F32)
    xp_out = scr("xp_out", [8 * nxs, c.tpc], F32)
    y_in = scr("y_in", [c.d_model, c.tpc], F32)
    y_out = scr("y_out", [c.d_model, c.tpc], F32)
    z3_in = scr("z3_in", [c.nsblk, 128, c.ndc * c.sblk], BF16)
    qkv_in = scr("qkv_in", [8 * 3 * c.hbc * 128, c.tpc], BF16)
    qkv_out = scr("qkv_out", [8 * 3 * c.hbc * 128, c.tpc], BF16)
    at_in = scr("at_in", [8 * c.hbc * 128, c.tpc], BF16)
    at_out = scr("at_out", [8 * c.hbc * 128, c.tpc], BF16)
    x3_d = scr("x3_d", [c.d_model, c.tpc], F32)
    x4_d = scr("x4_d", [c.d_model, c.tpc], F32)
    facc_d = scr("facc_d", [c.d_model, 2 * c.sblk], F32)
    up_bf = scr("up_bf", [c.d_model, c.ffn_int], BF16)
    down_bf = scr("down_bf", [c.ffn_int, c.d_model], BF16)
    fc1_bf = scr("fc1_bf", [c.d_model, c.ffn_lat], BF16)
    fc2_bf = scr("fc2_bf", [c.ffn_lat, c.d_model], BF16)
    exp_bf = scr("exp_bf", [c.ffn_lat, c.ffn_lat], BF16)

    def pb_meta(rows, cols):
        kc, ncc = rows // 128, cols // 128
        oct_ = min(8, kc)
        return kc, ncc, oct_, ncc // 2, kc // oct_

    PBW = {}
    for nm, (wbf, rows, cols) in {
        "up": (up_bf, c.d_model, c.ffn_int),
        "down": (down_bf, c.ffn_int, c.d_model),
        "fc1": (fc1_bf, c.d_model, c.ffn_lat),
        "fc2": (fc2_bf, c.ffn_lat, c.d_model),
        "exp": (exp_bf, c.ffn_lat, c.ffn_lat),
    }.items():
        kc, ncc, oct_, npj, nkq = pb_meta(rows, cols)
        pb = scr(nm + "_pb", [npj * nkq, 128, oct_ * 256], BF16)
        PBW[nm] = (wbf, pb, kc, ncc, oct_, npj, nkq, cols)

    taps = {}
    if c.debug_taps:
        for nm, shape, dt in [
            ("t_xp", [8 * nxs, c.tpc], F32),
            ("t_x2", [c.d_model, c.tpc], F32),
            ("t_x3", [c.d_model, c.tpc], F32),
            ("t_z3", [c.d_model, c.tpc], BF16),
            ("t_at", [8 * c.hbc * 128, c.tpc], BF16),
            ("t_x4", [c.d_model, c.tpc], F32),
        ]:
            taps[nm] = nc.dram_tensor(nm, shape, dt,
                                      kind="ExternalOutput").ap()

    with ExitStack() as es:
        sc = Sched(nc, es)
        sc.relax = getattr(cfg, "relax", False)
        E = lambda *a: es.enter_context(nc.sbuf_tensor(*a))

        id32 = E("id32", [128, 128], F32)
        onesb_sb = E("onesb_sb", [128, 1], BF16)
        onesf_sb = E("onesf_sb", [128, 1], F32)
        onesr_sb = E("onesr_sb", [1, 128], F32)
        onesrb_sb = E("onesrb_sb", [1, 128], BF16)
        idb_sb = E("idb_sb", [128, 128], BF16)
        masks_sb = E("masks_sb", [128, c.ktpq * c.qmac], BF16)
        A_sb = E("A_sb", [128, c.ndt * c.d_state], F32)
        D_sb = E("D_sb", [128, c.ndt], F32)
        dtb_sb = E("dtb_sb", [128, c.ndt], F32)
        dtwT_sb = E("dtwT_sb", [128, c.ndt * 128], BF16)
        xpwT_sb = E("xpwT_sb", [128, c.ndc * c.nrc * 128], BF16)
        ncs_sb = E("ncs_sb", [128, c.nrc], F32)
        gwT_sb = E("gwT_sb", [128, c.ndc * c.n_exp], BF16)
        lng_sb = E("lng_sb", [128, c.ndc], F32)
        lnb_sb = E("lnb_sb", [128, c.ndc], F32)
        carry_sb = E("carry_sb", [128, max(c.ndt * c.d_state, 16)], F32)
        eps1_sb = E("eps1_sb", [128, 1], F32)
        one1_sb = E("one1_sb", [128, 1], F32)
        mu_row = E("mu_row", [1, c.sw], F32)
        rs_row = E("rs_row", [1, c.sw], F32)
        ms_row = E("ms_row", [1, c.sw], F32)
        z_row = E("z_row", [1, c.qmac], F32)
        m_row = E("m_row", [1, c.sblk], F32)
        msct = E("msct", [max(c.sblk // 128, 1), 128], F32)
        eqm = E("eqm", [128, c.n_exp], U8)

        AR = Arena(nc, es, 88064)

        PS = es.enter_context(nc.psum_tensor("PS", [128, 4096], F32))

        def bank(i, rows=128, w=512):
            return PS[:rows, 512 * i:512 * i + w]

        LD = sc.sem("ld")
        CV = sc.sem("cv")
        PE = sc.sem("pe")
        ST = sc.sem("st")
        CONV = sc.sem("conv")
        CC0, CC1, CC2, CC3 = (sc.sem(f"cc{i}") for i in range(4))
        SDVE = sc.sems["self_dve"]
        SACT = sc.sems["self_act"]
        SPOOL = sc.sems["self_pool"]

        sc.dma("sp", id32[:], ident, "p0")
        sc.dma("sp", onesb_sb[:], ones_b, "p0")
        sc.dma("sp", onesf_sb[:], ones_f, "p0")
        sc.dma("sp", onesr_sb[:], onesr_f, "p0")
        sc.dma("sp", masks_sb[:], masks, "p0")
        sc.dma("sp", D_sb[:], D_sl, "p0")
        sc.dma("sp", dtb_sb[:], dtb_sl, "p0")
        sc.dma("sp", lng_sb[:], ln_g, "p0")
        sc.dma("sp", lnb_sb[:], ln_b, "p0")

        AR.mark()
        tmpA = AR.alloc(F32, c.ndt * c.d_state)
        stage = AR.alloc(F32, max(c.ndt * 128, c.d_model))

        sc.dma("sp", tmpA, A_log_sl, "p0")
        sc.op("dve", lambda e: e.memset(eps1_sb[:], float(c.ln_eps)))
        sc.wait("dve", LD, LD.n) if False else None
        sc.op("dve", lambda e: e.memset(onesrb_sb[:], 1.0))
        sc.wdma("dve", "p0")
        sc.op("dve", lambda e: e.tensor_copy(idb_sb[:], id32[:]))
        sc.op("dve", lambda e: e.memset(one1_sb[:], 1.0))
        sc.wdma("act", "p0")
        sc.op("act", lambda e: e.activation(tmpA, tmpA, AF.Exp))
        sc.wait("dve", SACT, SACT.n)
        sc.op("dve", lambda e: e.tensor_scalar_mul(A_sb[:], tmpA, -1.0))

        ntr = [0]

        def tr128(src, dst, ocols):
            bi = 6 + ntr[0] % 2
            sc.wait("pe", SACT, SACT.n)
            sc.op("pe", lambda e, s=src, bi=bi, oc=ocols: e.transpose(
                PS[:, bi * 512:bi * 512 + oc], s, id32[:oc, :oc]
                if False else id32[:s.shape[0], :s.shape[0]]),
                incs=((PE, 1),))
            sc.wait("act", PE, PE.n)
            sc.op("act", lambda e, bi=bi, d=dst, oc=ocols: e.activation(
                d, PS[:d.shape[0], bi * 512:bi * 512 + oc], AF.Copy))
            ntr[0] += 1

        sc.wdma("pe", "p0")
        sc.dma("sp", stage[:, :c.ndt * 128], dtw_sl, "p0s")
        sc.wdma("pe", "p0s")
        for a in range(c.ndt):
            tr128(stage[:, a * 128:(a + 1) * 128],
                  dtwT_sb[:, a * 128:(a + 1) * 128], 128)

        for j in range(c.nrc):
            r0 = 128 * j
            rw = min(128 * (j + 1), c.nxp) - r0
            sc.wait("sp", SACT, SACT.n)
            sc.dma("sp", stage[:rw, :c.d_model], xpw[r0:r0 + rw, :], "p0s")
            sc.wdma("pe", "p0s")
            for k in range(c.ndc):
                tr128(stage[:rw, k * 128:(k + 1) * 128],
                      xpwT_sb[:, (k * c.nrc + j) * 128:
                              (k * c.nrc + j) * 128 + rw], rw)
            sc.wait("pe", SACT, SACT.n)
            for k in range(c.ndc):
                o = (k * c.nrc + j) * 128
                sc.op("pe", lambda e, o=o, rw=rw, k=k: e.matmul(
                    PS[:rw, :1], xpwT_sb[:, o:o + rw], onesb_sb[:],
                    start=(k == 0), stop=(k == c.ndc - 1)),
                    incs=((PE, 1),) if k == c.ndc - 1 else ())
            sc.wait("dve", PE, PE.n)
            sc.op("dve", lambda e, j=j, rw=rw: e.tensor_scalar_mul(
                ncs_sb[:rw, j:j + 1], PS[:rw, :1], -1.0))
            sc.wait("pe", SDVE, SDVE.n)

        sc.wait("sp", SACT, SACT.n)
        sc.dma("sp", stage[:c.n_exp, :c.d_model], gate_wT, "p0s")
        sc.wdma("pe", "p0s")
        for k in range(c.ndc):
            tr128(stage[:c.n_exp, k * 128:(k + 1) * 128],
                  gwT_sb[:, k * c.n_exp:(k + 1) * c.n_exp], c.n_exp)

        def conv(dst, src, rows, rchunk):
            for r0 in range(0, rows, rchunk):
                r1 = min(r0 + rchunk, rows)
                sc.dma("pool", dst[r0:r1, :], src[r0:r1, :], "conv")

        conv(up_bf, up_w, c.d_model, 512)
        conv(down_bf, down_w, c.ffn_int, 512)
        conv(fc1_bf, fc1_w, c.d_model, 1024)
        conv(fc2_bf, fc2_w, c.ffn_lat, 1024)
        conv(exp_bf, exp_w, c.ffn_lat, 1024)

        def _mk_reblock():
            thunks = []
            for nm, (wbf, pb, kc, ncc, oct_, npj, nkq, cols) in PBW.items():
                for j in range(npj):
                    for kq in range(nkq):
                        for a in range(oct_):
                            def th(j=j, kq=kq, a=a, wbf=wbf, pb=pb,
                                   oct_=oct_, nkq=nkq, cols=cols):
                                r0 = (kq * oct_ + a) * 128
                                sc.dma("act",
                                       pb[j * nkq + kq,
                                          :, a * 256:(a + 1) * 256],
                                       wbf[r0:r0 + 128,
                                           j * 256:(j + 1) * 256], "rbk")
                            thunks.append(th)
            return thunks

        reblock_thunks = _mk_reblock()
        rb_i = [0]

        def emit_reblocks(k):
            if rb_i[0] == 0 and reblock_thunks:
                sc.wdma("act", "conv")
            n = min(k, len(reblock_thunks) - rb_i[0])
            for _ in range(n):
                reblock_thunks[rb_i[0]]()
                rb_i[0] += 1

        AR.reset()
        sc.barrier()

        def ln_finalize(mu_o, rs_o, ms_o, w, d_total, bs, bq):
            sc.wait("dve", PE, PE.n)
            inv = 1.0 / d_total
            sc.op("dve", lambda e: e.tensor_scalar_mul(
                mu_o, bank(bs, rows=1, w=w), inv))
            sc.op("dve", lambda e: e.tensor_scalar_mul(
                rs_o, bank(bq, rows=1, w=w), inv))
            sc.op("dve", lambda e: e.tensor_tensor(ms_o, mu_o, mu_o, ALU.mult))
            sc.op("dve", lambda e: e.tensor_tensor(
                rs_o, rs_o, ms_o, ALU.subtract))
            sc.wait("act", SDVE, SDVE.n)
            sc.op("act", lambda e: e.activation(
                rs_o, rs_o, AF.Ln, bias=eps1_sb[:1, :]))
            sc.op("act", lambda e: e.activation(
                rs_o, rs_o, AF.Exp, scale=-0.5))
            sc.wait("dve", SACT, SACT.n)
            sc.op("dve", lambda e: e.tensor_tensor(ms_o, mu_o, rs_o, ALU.mult))

        def replicate(row_ap, dst, pbank):
            w = row_ap.shape[-1]
            sc.wait("pe", SDVE, SDVE.n)
            sc.wait("pe", SACT, SACT.n)
            sc.op("pe", lambda e, r=row_ap, w=w: e.matmul(
                bank(pbank, w=w), onesr_sb[:], r, start=True, stop=True),
                incs=((PE, 1),))
            sc.wait("act", PE, PE.n)
            sc.op("act", lambda e, d=dst, w=w: e.activation(
                d, bank(pbank, w=w), AF.Copy))

        AR.mark()
        xchk = [AR.alloc(F32, c.sw) for _ in range(2)]
        sqchb = AR.alloc(BF16, c.sw)
        xbfs = [AR.alloc(BF16, c.sw) for _ in range(c.ndc)]
        rep_r = AR.alloc(F32, c.sw)
        rep_m = AR.alloc(F32, c.sw)
        t1b = AR.alloc(F32, c.sw)
        xpo = [AR.alloc(F32, c.sw) for _ in range(c.nrc)]

        for w in range(c.nst):
            cw = slice(c.sw * w, c.sw * (w + 1))
            pe_chunk = []
            sc.wdma("dve", "st_xp")
            for k in range(c.ndc):
                buf = xchk[k % 2]
                key = f"xc{k % 2}"
                if k >= 2:
                    sc.wait("sp", PE, pe_chunk[k - 2])
                    sc.wait("sp", SDVE, SDVE.n)
                sc.dma("sp", buf, xT_t[k * 128:(k + 1) * 128, cw], key)
                sc.wdma("dve", key)
                if w > 0 and k == 0:
                    sc.wait("dve", PE, PE.n)
                sc.op("dve", lambda e, b=buf, k=k: e.tensor_copy(xbfs[k], b))
                sc.wait("act", SDVE, SDVE.n)
                sc.wait("act", PE, PE.n)
                sc.op("act", lambda e, k=k: e.activation(
                    sqchb, xbfs[k], AF.Square))
                sc.wait("pe", SACT, SACT.n)
                sc.op("pe", lambda e, k=k: e.matmul(
                    bank(0, rows=1, w=c.sw), onesb_sb[:], xbfs[k],
                    start=(k == 0), stop=(k == c.ndc - 1)))
                sc.op("pe", lambda e, k=k: e.matmul(
                    bank(1, rows=1, w=c.sw), onesb_sb[:], sqchb,
                    start=(k == 0), stop=(k == c.ndc - 1)), incs=((PE, 1),))
                pe_chunk.append(PE.n)
            ln_finalize(mu_row[:], rs_row[:], ms_row[:], c.sw, c.d_model, 0, 1)
            replicate(rs_row[:], rep_r, 2)
            replicate(ms_row[:], rep_m, 3)
            sc.wait("pe", SDVE, SDVE.n)
            for j in range(c.nrc):
                rw = min(128 * (j + 1), c.nxp) - 128 * j
                for k in range(c.ndc):
                    o = (k * c.nrc + j) * 128
                    sc.op("pe", lambda e, o=o, rw=rw, k=k, j=j: e.matmul(
                        bank(4 + j, rows=rw, w=c.sw), xpwT_sb[:, o:o + rw],
                        xbfs[k], start=(k == 0), stop=(k == c.ndc - 1)),
                        incs=((PE, 1),) if k == c.ndc - 1 else ())
            sc.wait("dve", PE, PE.n)
            sc.wait("dve", SACT, SACT.n)
            for j in range(c.nrc):
                rw = min(128 * (j + 1), c.nxp) - 128 * j
                sc.op("dve", lambda e, j=j, rw=rw: e.tensor_tensor(
                    t1b[:rw, :], bank(4 + j, rows=rw, w=c.sw),
                    rep_r[:rw, :], ALU.mult))
                sc.op("dve", lambda e, j=j, rw=rw: e.scalar_tensor_tensor(
                    xpo[j][:rw, :], rep_m[:rw, :], ncs_sb[:rw, j:j + 1],
                    t1b[:rw, :], ALU.mult, ALU.add))
                sc.wait("sp", SDVE, SDVE.n)
                sc.dma("sp", xp_in[128 * j:128 * j + rw, cw], xpo[j][:rw, :],
                       "st_xp")
            sc.wait("sp", SDVE, SDVE.n)
            sc.dma("sp", xp_in[c.nxp:c.nxp + 1, cw], ms_row[:], "st_xp")
            sc.dma("sp", xp_in[c.nxp + 1:c.nxp + 2, cw], rs_row[:], "st_xp")
            sc.wait("pe", SDVE, SDVE.n)

        sc.wdma("pool", "st_xp")
        sc.op("pool", lambda e: e.collective_compute(
            "AllGather", ALU.bypass, replica_groups=[list(range(8))],
            ins=[xp_in.tensor.ap().opt()], outs=[xp_out.tensor.ap().opt()]),
            incs=((CC0, 1),), chain=False)
        if c.debug_taps:
            sc.wait("pool", CC0, CC0.n)
            sc.dma("pool", taps["t_xp"], xp_out, "st_tap")
        AR.reset()
        sc.barrier()

        AR.mark()
        TH = c.th
        nsw = TH // c.sw
        xpd_bf = AR.alloc(BF16, TH)
        xpd_f32 = AR.alloc(F32, TH)
        mrep = AR.alloc(F32, TH)
        rrep = AR.alloc(F32, TH)
        dt4 = [AR.alloc(F32, TH) for _ in range(c.ndt)]
        dtx4 = [AR.alloc(BF16, TH) for _ in range(c.ndt)]
        dAb = [AR.alloc(F32, TH) for _ in range(2)]
        _dbx = AR.alloc(BF16, TH)
        dBxb = [_dbx, _dbx]
        _h = AR.alloc(BF16, TH)
        hbuf = [_h, _h]
        hcb = [AR.alloc(BF16, TH) for _ in range(2)]
        Brep = [AR.alloc(F32, TH) for _ in range(2)]
        Crep = [AR.alloc(F32, TH) for _ in range(2)]
        _xs = AR.alloc(F32, TH)
        xsl = [_xs, _xs]
        scr1 = dAb[0]
        scr2m = dAb[1]

        npass = (c.ndt + 1) // 2
        for b in range(c.batch):
            for th in range(2):
                r = 2 * b + th
                rb = r * nxs
                tok0 = b * c.seq + th * TH
                sc.wait("sp", CC0, 1)
                sc.wait("sp", SDVE, SDVE.n)
                sc.dma("sp", mrep,
                       xp_out[rb + c.nxp:rb + c.nxp + 1, :]
                       .broadcast_to([128, TH]), "m_mr")
                sc.dma("sp", rrep,
                       xp_out[rb + c.nxp + 1:rb + c.nxp + 2, :]
                       .broadcast_to([128, TH]), "m_mr")
                sc.wait("sp", SACT, SACT.n)
                sc.dma("sp", xpd_f32, xp_out[rb:rb + 128, :], "m_xpd")
                sc.wdma("act", "m_xpd")
                sc.wait("act", PE, PE.n)
                sc.op("act", lambda e: e.activation(
                    xpd_bf, xpd_f32, AF.Copy), wide=True)
                for a in range(c.ndt):
                    sc.wait("pe", SACT, SACT.n)
                    sc.wait("pe", SDVE, SDVE.n)
                    for wv in range(nsw):
                        sc.op("pe", lambda e, a=a, wv=wv: e.matmul(
                            bank(wv, w=c.sw),
                            dtwT_sb[:, a * 128:(a + 1) * 128],
                            xpd_bf[:, c.sw * wv:c.sw * (wv + 1)],
                            start=True, stop=True),
                            incs=((PE, 1),) if wv == nsw - 1 else ())
                    sc.wait("act", PE, PE.n)
                    sc.wait("act", SDVE, SDVE.n)
                    sc.op("act", lambda e, a=a: e.activation(
                        scr1, PS[:, :TH], AF.Exp, bias=dtb_sb[:, a:a + 1]), wide=True)
                    sc.op("act", lambda e, a=a: e.activation(
                        dt4[a], scr1, AF.Ln, bias=one1_sb[:]), wide=True)
                for a in range(c.ndt):
                    xb = xsl[a % 2]
                    key = f"m_x{a % 2}"
                    sc.wait("sp", SDVE, SDVE.n)
                    sc.dma("sp", xb,
                           xT_d[a * 128:(a + 1) * 128, tok0:tok0 + TH], key)
                    sc.wdma("dve", key)
                    sc.wdma("dve", "m_mr")
                    sc.wait("dve", SACT, SACT.n)
                    sc.op("dve", lambda e, xb=xb: e.tensor_tensor(
                        scr1, xb, rrep, ALU.mult), wide=True)
                    sc.op("dve", lambda e: e.tensor_tensor(
                        scr1, scr1, mrep, ALU.subtract), wide=True)
                    sc.op("dve", lambda e, a=a: e.tensor_tensor(
                        dtx4[a], dt4[a], scr1, ALU.mult), wide=True)
                for half in range(npass):
                    aa = [a for a in (2 * half, 2 * half + 1) if a < c.ndt]
                    snap_dA = {}
                    snap_scan = {}
                    snap_ymm = {}
                    base_sdve = SDVE.n
                    base_pe = PE.n
                    sc.wdma("act", "st_y")
                    for n in range(c.d_state):
                        emit_reblocks(12)
                        sl = n % 2
                        rowb = rb + c.dt_rank + n
                        rowc = rb + c.dt_rank + c.d_state + n
                        sc.wait("sp", SDVE, SDVE.n)
                        sc.dma("sp", Brep[sl],
                               xp_out[rowb:rowb + 1, :]
                               .broadcast_to([128, TH]), f"m_bc{sl}")
                        sc.dma("sp", Crep[sl],
                               xp_out[rowc:rowc + 1, :]
                               .broadcast_to([128, TH]), f"m_bc{sl}")
                        for ai, a in enumerate(aa):
                            it = n * len(aa) + ai
                            s2 = it % 2
                            ci = a * c.d_state + n
                            sc.wait("act", SDVE,
                                    snap_scan.get(it - 2, base_sdve))
                            sc.op("act", lambda e, a=a, s2=s2, ci=ci:
                                  e.activation(dAb[s2], dt4[a], AF.Exp,
                                               scale=A_sb[:, ci:ci + 1]),
                                  wide=True)
                            snap_dA[it] = SACT.n
                            sc.wdma("dve", f"m_bc{sl}")
                            sc.op("dve", lambda e, a=a, s2=s2, sl=sl:
                                  e.tensor_tensor(dBxb[s2], dtx4[a],
                                                  Brep[sl], ALU.mult),
                                  wide=True)
                            if th == 1:
                                sc.wait("dve", SACT, snap_dA[it])
                                sc.op("dve", lambda e, s2=s2, ci=ci:
                                      e.scalar_tensor_tensor(
                                          dBxb[s2][:, 0:1], dAb[s2][:, 0:1],
                                          carry_sb[:, ci:ci + 1],
                                          dBxb[s2][:, 0:1],
                                          ALU.mult, ALU.add))
                            sc.wait("dve", SACT, snap_dA[it])
                            sc.op("dve", lambda e, s2=s2: e.tensor_tensor_scan(
                                hbuf[s2], dAb[s2], dBxb[s2], 0.0,
                                ALU.mult, ALU.add), wide=True)
                            snap_scan[it] = SDVE.n
                            if th == 0:
                                sc.op("dve", lambda e, s2=s2, ci=ci:
                                      e.tensor_copy(carry_sb[:, ci:ci + 1],
                                                    hbuf[s2][:, TH - 1:TH]))
                            sc.wait("dve", PE, snap_ymm.get(it - 2, base_pe))
                            sc.op("dve", lambda e, s2=s2, sl=sl:
                                  e.tensor_tensor(hcb[s2], hbuf[s2],
                                                  Crep[sl], ALU.mult),
                                  wide=True)
                            sc.wait("pe", SDVE, SDVE.n)
                            for wv in range(nsw):
                                sc.op("pe", lambda e, ai=ai, s2=s2, wv=wv,
                                      n=n: e.matmul(
                                          bank(ai * nsw + wv, w=c.sw),
                                          idb_sb[:],
                                          hcb[s2][:, wv * c.sw:
                                                  (wv + 1) * c.sw],
                                          start=(n == 0),
                                          stop=(n == c.d_state - 1)),
                                      incs=((PE, 1),)
                                      if wv == nsw - 1 else ())
                            snap_ymm[it] = PE.n
                    for ai, a in enumerate(aa):
                        xb = xsl[a % 2]
                        key = f"m_x{a % 2}"
                        sc.wait("sp", SDVE, SDVE.n)
                        sc.dma("sp", xb,
                               xT_d[a * 128:(a + 1) * 128, tok0:tok0 + TH],
                               key)
                        sc.wdma("dve", key)
                        sc.wait("dve", SACT, SACT.n)
                        sc.op("dve", lambda e, xb=xb: e.tensor_tensor(
                            scr1, xb, rrep, ALU.mult), wide=True)
                        sc.op("dve", lambda e: e.tensor_tensor(
                            scr1, scr1, mrep, ALU.subtract), wide=True)
                        sc.op("dve", lambda e, a=a, xb=xb:
                              e.scalar_tensor_tensor(
                                  scr1, scr1, D_sb[:, a:a + 1], xb,
                                  ALU.mult, ALU.add), wide=True)
                        sc.wait("dve", PE, PE.n)
                        sc.op("dve", lambda e, ai=ai: e.tensor_tensor(
                            scr2m, scr1,
                            PS[:, ai * nsw * 512:ai * nsw * 512 + TH],
                            ALU.add), wide=True)
                        sc.wait("sp", SDVE, SDVE.n)
                        sc.dma("sp", y_in[c.dsl * r + 128 * a:
                                          c.dsl * r + 128 * (a + 1), :],
                               scr2m, "st_y")
                        sc.wdma("dve", "st_y")
        emit_reblocks(10 ** 9)
        sc.wdma("pool", "st_y")
        sc.wdma("pool", "conv")

        def _a2a_y(e):
            with allow_ici_all_to_all():
                return e.collective_compute(
                    "AllToAll", ALU.bypass, replica_groups=[list(range(8))],
                    ins=[y_in.tensor.ap().opt()],
                    outs=[y_out.tensor.ap().opt()])
        sc.op("pool", _a2a_y, incs=((CC1, 1),), chain=False)
        if c.debug_taps:
            sc.wait("pool", CC1, CC1.n)
            sc.dma("pool", taps["t_x2"], y_out, "st_tap")
        AR.reset()
        sc.barrier()

        def ffn_phase(x_dram, out_dram, z_dram):
            AR.mark()
            z2 = [AR.alloc(BF16, c.sblk) for _ in range(c.ndc)]
            h2 = [AR.alloc(BF16, c.sblk) for _ in range(c.nic)]
            xlat = [AR.alloc(BF16, c.sblk) for _ in range(c.nlc)]
            moe = [AR.alloc(BF16, c.sblk) for _ in range(c.nlc)]
            pan = [AR.alloc(BF16, 2048) for _ in range(5)]
            x2c = [AR.alloc(F32, c.sblk) for _ in range(2)]
            sqs = AR.alloc(F32, c.sblk)
            sqsb = AR.alloc(BF16, c.sblk)
            repr_ = AR.alloc(F32, c.sblk)
            repm = AR.alloc(F32, c.sblk)
            scr2 = AR.alloc(F32, c.sblk)
            scr3 = AR.alloc(F32, c.sblk)
            gt = AR.alloc(F32, c.sblk)
            lgt = AR.alloc(F32, c.n_exp)
            nbig = AR.alloc(F32, c.n_exp)
            m12 = AR.alloc(F32, 2)
            msc = AR.alloc(F32, max(c.sblk // 128, 1))
            msrep = AR.alloc(BF16, c.sblk)
            fev = [AR.alloc(F32, c.sblk) for _ in range(2)]
            nsub = c.sblk // 128
            pan_snap = {}
            pslot = [0]
            PQF = ("pool", "act")

            def strip_stats(src_dram, cw):
                sc.wdma("dve", "st_z3")
                for k in range(c.ndc):
                    buf = x2c[k % 2]
                    key = f"f_x{k % 2}"
                    if k >= 2:
                        sc.wait("sp", PE, PE.n)
                    sc.wait("sp", SDVE, SDVE.n)
                    sc.dma("sp", buf, src_dram[k * 128:(k + 1) * 128, cw],
                           key)
                    sc.wdma("dve", key)
                    sc.wait("dve", PE, PE.n)
                    sc.op("dve", lambda e, b=buf, k=k: e.tensor_copy(
                        z2[k], b), wide=True)
                    sc.wait("act", SDVE, SDVE.n)
                    sc.wait("act", PE, PE.n)
                    sc.op("act", lambda e, k=k: e.activation(
                        sqsb, z2[k], AF.Square), wide=True)
                    sc.wait("pe", SACT, SACT.n)
                    sc.op("pe", lambda e, k=k: e.matmul(
                        bank(6, rows=1, w=c.sblk), onesb_sb[:], z2[k],
                        start=(k == 0), stop=(k == c.ndc - 1)))
                    sc.op("pe", lambda e, k=k: e.matmul(
                        bank(7, rows=1, w=c.sblk), onesb_sb[:], sqsb,
                        start=(k == 0), stop=(k == c.ndc - 1)),
                        incs=((PE, 1),))
                ln_finalize(mu_row[:, :c.sblk], rs_row[:, :c.sblk],
                            ms_row[:, :c.sblk], c.sblk, c.d_model, 6, 7)
                replicate(rs_row[:, :c.sblk], repr_, 6)
                replicate(ms_row[:, :c.sblk], repm, 7)

            for w in range(c.nsblk):
                cw = slice(c.sblk * w, c.sblk * (w + 1))
                strip_stats(x_dram, cw)
                for k in range(c.ndc):
                    sc.wait("dve", SACT, SACT.n)
                    sc.op("dve", lambda e, k=k: e.tensor_tensor(
                        scr2, z2[k], repr_, ALU.mult), wide=True)
                    sc.op("dve", lambda e, k=k: e.tensor_tensor(
                        z2[k], scr2, repm, ALU.subtract), wide=True)
                sc.wait("pe", SDVE, SDVE.n)
                for k in range(c.ndc):
                    sc.op("pe", lambda e, k=k: e.matmul(
                        bank(0, rows=c.n_exp, w=c.sblk),
                        gwT_sb[:, k * c.n_exp:(k + 1) * c.n_exp], z2[k],
                        start=(k == 0), stop=(k == c.ndc - 1)),
                        incs=((PE, 1),) if k == c.ndc - 1 else ())
                sc.wait("act", PE, PE.n)
                sc.op("act", lambda e: e.activation(
                    gt[:c.n_exp, :], bank(0, rows=c.n_exp, w=c.sblk),
                    AF.Copy))
                sc.wait("pe", SACT, SACT.n)
                for i in range(nsub):
                    sc.op("pe", lambda e, i=i: e.transpose(
                        PS[:, 512 + i * 128:512 + i * 128 + c.n_exp],
                        gt[:c.n_exp, i * 128:(i + 1) * 128],
                        id32[:c.n_exp, :c.n_exp]),
                        incs=((PE, 1),) if i == nsub - 1 else ())
                sc.wait("dve", PE, PE.n)
                sc.op("dve", lambda e: e.memset(nbig, -1e30))
                for i in range(nsub):
                    sc.op("dve", lambda e, i=i: e.tensor_copy(
                        lgt, PS[:, 512 + i * 128:512 + i * 128 + c.n_exp]))
                    sc.op("dve", lambda e: e.reduce_max(
                        m12[:, 0:1], lgt, axis=AX.X))
                    sc.op("dve", lambda e: e.tensor_scalar(
                        eqm[:], lgt, m12[:, 0:1], None, ALU.is_equal))
                    sc.op("dve", lambda e: e.copy_predicated(
                        lgt, eqm[:], nbig))
                    sc.op("dve", lambda e: e.reduce_max(
                        m12[:, 1:2], lgt, axis=AX.X))
                    sc.op("dve", lambda e, i=i: e.tensor_tensor(
                        msc[:, i:i + 1], m12[:, 0:1], m12[:, 1:2], ALU.add))
                sc.wait("pe", SDVE, SDVE.n)
                sc.op("pe", lambda e: e.transpose(
                    PS[:nsub, 512:640], msc[:, :nsub], id32[:, :]),
                    incs=((PE, 1),))
                sc.wait("dve", PE, PE.n)
                sc.wdma("dve", "f_mrow")
                sc.op("dve", lambda e: e.tensor_copy(
                    msct[:nsub, :], PS[:nsub, 512:640]))
                sc.wait("sp", SDVE, SDVE.n)
                sc.wait("sp", PE, PE.n)
                for i in range(nsub):
                    sc.dma("sp", m_row[:, i * 128:(i + 1) * 128],
                           msct[i:i + 1, :], "f_mrow")
                sc.wdma("pe", "f_mrow")
                replicate(m_row[:], msrep, 1)
                for q_ in ("sp", "pool", "act"):
                    sc.wdma(q_, "rbk")
                evac_snap = {}

                def pair_mms(pbname, jj, rhs_list, b0, extra=False,
                             st_first=True, sp_last=True):
                    wbf, pb, kc, ncc, oct_, npj, nkq, cols = PBW[pbname]
                    for kq in range(nkq):
                        sl = pslot[0] % 5
                        q_ = PQF[pslot[0] % 2]
                        if pslot[0] >= 5:
                            sc.wait(q_, PE, pan_snap[pslot[0] - 5])
                        sc.dma(q_, pan[sl][:, :oct_ * 256],
                               pb[jj * nkq + kq, :, :], f"f_p{sl}")
                        sc.wdma("pe", f"f_p{sl}")
                        for a in range(oct_):
                            k = kq * oct_ + a
                            for gi in range(2):
                                o = (a * 2 + gi) * 128
                                st = st_first and (k == 0)
                                sp = sp_last and (k == kc - 1)
                                last = (a == oct_ - 1 and gi == 1)
                                sc.op("pe", lambda e, o=o, gi=gi, sl=sl,
                                      b0=b0, st=st, sp=sp, k=k: e.matmul(
                                          bank(b0 + gi, w=c.sblk),
                                          pan[sl][:, o:o + 128],
                                          rhs_list[k],
                                          start=st, stop=sp),
                                      incs=((PE, 1),) if last else ())
                        pan_snap[pslot[0]] = PE.n
                        pslot[0] += 1

                NPAIR = (c.nic + c.nlc) // 2
                for g in range(NPAIR):
                    b0 = (g % 2) * 2
                    if g >= 2:
                        sc.wait("pe", SACT, evac_snap[g - 2][0])
                        sc.wait("pe", SDVE, evac_snap[g - 2][1])
                    else:
                        sc.wait("pe", SACT, SACT.n)
                        sc.wait("pe", SDVE, SDVE.n)
                    if g < c.nic // 2:
                        pair_mms("up", g, z2, b0)
                        ics = [2 * g, 2 * g + 1]
                    else:
                        jj = g - c.nic // 2
                        pair_mms("fc1", jj, z2, b0)
                        ics = [c.nic + 2 * jj, c.nic + 2 * jj + 1]
                    sc.wait("act", PE, PE.n)
                    sc.wait("act", SDVE, SDVE.n)
                    for gi, ic in enumerate(ics):
                        if ic < c.nic:
                            sc.op("act", lambda e, gi=gi, b0=b0: e.activation(
                                fev[gi], bank(b0 + gi, w=c.sblk), AF.Relu), wide=True)
                            sc.wait("dve", SACT, SACT.n)
                            sc.op("dve", lambda e, ic=ic, gi=gi:
                                  e.tensor_tensor(h2[ic], fev[gi],
                                                  fev[gi], ALU.mult), wide=True)
                        else:
                            sc.op("act", lambda e, ic=ic, b0=b0, gi=gi:
                                  e.activation(xlat[ic - c.nic],
                                               bank(b0 + gi, w=c.sblk),
                                               AF.Copy), wide=True)
                    evac_snap[g] = (SACT.n, SDVE.n)

                for g in range(c.nlc // 2):
                    b0 = (g % 2) * 2
                    if g < 2:
                        sc.wait("pe", SACT, SACT.n)
                        sc.wait("pe", SDVE, SDVE.n)
                    else:
                        sc.wait("pe", SDVE, evac_snap[("e", g - 2)][0])
                    pair_mms("exp", g, xlat, b0)
                    sc.wait("dve", PE, PE.n)
                    for gi in range(2):
                        sc.op("dve", lambda e, g=g, gi=gi, b0=b0:
                              e.tensor_tensor(moe[2 * g + gi],
                                              bank(b0 + gi, w=c.sblk),
                                              msrep, ALU.mult), wide=True)
                    evac_snap[("e", g)] = (SDVE.n,)

                ocnt = [0]
                for g in range(c.ndc // 2):
                    b0 = (g % 2) * 2
                    if g < 2:
                        sc.wait("pe", SACT, SACT.n)
                        sc.wait("pe", SDVE, SDVE.n)
                    else:
                        sc.wait("pe", SACT, evac_snap[("d", g - 2)][0])
                    pair_mms("down", g, h2, b0, st_first=True, sp_last=False)
                    pair_mms("fc2", g, moe, b0, st_first=False, sp_last=True)
                    sc.wait("act", PE, PE.n)
                    for gi in range(2):
                        o = 2 * g + gi
                        sc.wdma("act", "st_fa")
                        sc.op("act", lambda e, gi=gi, b0=b0: e.activation(
                            fev[gi], bank(b0 + gi, w=c.sblk), AF.Copy), wide=True)
                        sc.wait("act", PE, PE.n)
                        sc.op("act", lambda e, gi=gi: e.activation(
                            sqs, fev[gi], AF.Square), wide=True)
                        sc.wait("pe", SACT, SACT.n)
                        st0 = (ocnt[0] == 0)
                        fin = (ocnt[0] == c.ndc - 1)
                        sc.op("pe", lambda e, gi=gi, st0=st0, fin=fin:
                              e.matmul(bank(6, rows=1, w=c.sblk),
                                       onesf_sb[:], fev[gi],
                                       start=st0, stop=fin))
                        sc.op("pe", lambda e, st0=st0, fin=fin: e.matmul(
                            bank(7, rows=1, w=c.sblk), onesf_sb[:], sqs,
                            start=st0, stop=fin), incs=((PE, 1),))
                        ocnt[0] += 1
                        sc.wait("sp", SACT, SACT.n)
                        sc.dma("sp", facc_d[o * 128:(o + 1) * 128,
                                            (w % 2) * c.sblk:
                                            (w % 2 + 1) * c.sblk],
                               fev[gi], "st_fa")
                    evac_snap[("d", g)] = (SACT.n,)
                ln_finalize(mu_row[:, :c.sblk], rs_row[:, :c.sblk],
                            ms_row[:, :c.sblk], c.sblk, c.d_model, 6, 7)
                replicate(rs_row[:, :c.sblk], repr_, 6)
                replicate(ms_row[:, :c.sblk], repm, 7)
                sc.wdma("sp", "st_fa")
                sc.wait("sp", PE, PE.n)
                for k in range(c.ndc):
                    buf = x2c[k % 2]
                    fv = fev[k % 2]
                    key = f"f_x{k % 2}"
                    sc.wait("sp", SDVE, SDVE.n)
                    sc.wait("sp", SACT, SACT.n)
                    sc.dma("sp", buf, x_dram[k * 128:(k + 1) * 128, cw], key)
                    sc.dma("sp", fv, facc_d[k * 128:(k + 1) * 128,
                                            (w % 2) * c.sblk:
                                            (w % 2 + 1) * c.sblk], key)
                    sc.wdma("dve", key)
                    sc.wait("dve", SACT, SACT.n)
                    sc.op("dve", lambda e, fv=fv: e.tensor_tensor(
                        scr2, fv, repr_, ALU.mult), wide=True)
                    sc.op("dve", lambda e: e.tensor_tensor(
                        scr2, scr2, repm, ALU.subtract), wide=True)
                    sc.wait("act", SDVE, SDVE.n)
                    sc.wdma("act", "st_out")
                    sc.op("act", lambda e, k=k: e.activation(
                        scr3, scr2, AF.Identity,
                        bias=lnb_sb[:, k:k + 1], scale=lng_sb[:, k:k + 1]), wide=True)
                    sc.wait("dve", SACT, SACT.n)
                    sc.op("dve", lambda e, buf=buf: e.tensor_tensor(
                        scr3, scr3, buf, ALU.add), wide=True)
                    sc.wait("sp", SDVE, SDVE.n)
                    sc.dma("sp", out_dram[k * 128:(k + 1) * 128, cw],
                           scr3, "st_out")
                if z_dram is not None:
                    sc.wdma("sp", "st_out")
                    sc.wdma("dve", "st_z3")
                    strip_stats(out_dram, cw)
                    for k in range(c.ndc):
                        sc.wait("dve", SACT, SACT.n)
                        sc.op("dve", lambda e, k=k: e.tensor_tensor(
                            scr2, z2[k], repr_, ALU.mult), wide=True)
                        sc.op("dve", lambda e, k=k: e.tensor_tensor(
                            z2[k], scr2, repm, ALU.subtract), wide=True)
                        sc.wait("sp", SDVE, SDVE.n)
                        sc.dma("sp", z_dram[w, :, k * c.sblk:(k + 1) * c.sblk],
                               z2[k], "st_z3")
            AR.reset()

        sc.wait("sp", CC1, 1)
        sc.wait("pool", CC1, 1)
        ffn_phase(y_out, x3_d, z3_in)
        if c.debug_taps:
            sc.wdma("pool", "st_out", "st_z3")
            sc.dma("pool", taps["t_x3"], x3_d, "st_tap")
            sc.dma("pool", taps["t_z3"], z3_in, "st_tap")
        sc.barrier()

        if c.stop_after == "p3":
            sc.barrier()
            block = es.enter_context(nc.Block())
            sc.replay(block)
            return nc, taps
        AR.mark()
        z3r = AR.alloc(BF16, c.nsblk * c.ndc * c.sblk)
        wpan = [AR.alloc(BF16, c.ndc * 128) for _ in range(4)]
        qkvev = AR.alloc(BF16, c.tpc)
        nwv = c.tpc // c.sw
        sc.wdma("sp", "st_z3")
        for wv in range(c.nsblk):
            sc.dma("sp", z3r[:, wv * c.ndc * c.sblk:(wv + 1) * c.ndc * c.sblk],
                   z3_in[wv, :, :], "a_z3")
        NCC = 3 * c.heads
        qsnap = {}
        esnap = {}
        for cc in range(NCC):
            sl = cc % 4
            b0 = (cc % 2) * 4
            if cc >= 4:
                sc.wait("pool", PE, qsnap[cc - 4])
            sc.dma("pool", wpan[sl], wqkv_p[cc, :, :], f"a_w{sl}")
            sc.wdma("pe", f"a_w{sl}")
            sc.wdma("pe", "a_z3")
            if cc >= 2:
                sc.wait("pe", SACT, esnap[cc - 2])
            for k in range(c.ndc):
                for wv in range(nwv):
                    sc.op("pe", lambda e, k=k, wv=wv, b0=b0, sl=sl: e.matmul(
                        bank(b0 + wv, w=c.sw),
                        wpan[sl][:, k * 128:(k + 1) * 128],
                        z3r[:, (wv * c.ndc + k) * c.sblk:
                            (wv * c.ndc + k) * c.sblk + c.sw],
                        start=(k == 0), stop=(k == c.ndc - 1)),
                        incs=((PE, 1),)
                        if (k == c.ndc - 1 and wv == nwv - 1) else ())
            qsnap[cc] = PE.n
            sc.wait("act", PE, PE.n)
            sc.wdma("act", "st_qkv")
            for wv in range(nwv):
                sc.op("act", lambda e, wv=wv, b0=b0: e.activation(
                    qkvev[:, wv * c.sw:(wv + 1) * c.sw],
                    bank(b0 + wv, w=c.sw), AF.Copy))
            esnap[cc] = SACT.n
            m, hg = divmod(cc, c.heads) if False else (cc // c.heads,
                                                       cc % c.heads)
            j = hg // c.hbc
            row = j * 3 * c.hbc * 128 + m * c.hbc * 128 + (hg % c.hbc) * 128
            sc.wait("sp", SACT, SACT.n)
            sc.dma("sp", qkv_in[row:row + 128, :], qkvev, "st_qkv")
        sc.wdma("pool", "st_qkv")

        def _a2a_qkv(e):
            with allow_ici_all_to_all():
                return e.collective_compute(
                    "AllToAll", ALU.bypass, replica_groups=[list(range(8))],
                    ins=[qkv_in.tensor.ap().opt()],
                    outs=[qkv_out.tensor.ap().opt()])
        sc.op("pool", _a2a_qkv, incs=((CC2, 1),), chain=False)
        AR.reset()
        sc.barrier()

        AR.mark()
        qsb = AR.alloc(BF16, c.seq)
        ksb = AR.alloc(BF16, c.seq)
        vsb = AR.alloc(BF16, c.seq)
        vstg = AR.alloc(F32, 128)
        vtT = AR.alloc(BF16, c.seq)
        ptb = [AR.alloc(BF16, 2 * c.qmac) for _ in range(2)]
        zrep = AR.alloc(F32, c.qmac)
        atev = AR.alloc(BF16, c.qmac)
        scale = 1.0 / math.sqrt(c.dq)
        sc.wait("sp", CC2, 1)
        for b in range(c.batch):
            for hhl in range(c.hbc):
                sc.wait("sp", PE, PE.n)
                sc.wait("sp", SDVE, SDVE.n)
                for half in range(2):
                    i = 2 * b + half
                    base = i * 3 * c.hbc * 128
                    sc.dma("sp", qsb[:, half * c.th:(half + 1) * c.th],
                           qkv_out[base + hhl * 128:
                                   base + (hhl + 1) * 128, :], "a_qkv")
                    sc.dma("sp", ksb[:, half * c.th:(half + 1) * c.th],
                           qkv_out[base + (c.hbc + hhl) * 128:
                                   base + (c.hbc + hhl + 1) * 128, :],
                           "a_qkv")
                    sc.dma("sp", vtT[:, half * c.th:(half + 1) * c.th],
                           qkv_out[base + (2 * c.hbc + hhl) * 128:
                                   base + (2 * c.hbc + hhl + 1) * 128, :],
                           "a_qkv")
                sc.wdma("dve", "a_qkv")
                for q in range(c.seq // 128):
                    sc.wait("dve", PE, PE.n)
                    sc.op("dve", lambda e, q=q: e.tensor_copy(
                        vstg, vtT[:, q * 128:(q + 1) * 128]))
                    sc.wait("pe", SDVE, SDVE.n)
                    sc.wait("pe", SACT, SACT.n)
                    sc.op("pe", lambda e, q=q: e.transpose(
                        PS[:, 7 * 512:7 * 512 + 128], vstg, id32[:, :]),
                        incs=((PE, 1),))
                    sc.wait("act", PE, PE.n)
                    sc.op("act", lambda e, q=q: e.activation(
                        vsb[:, q * 128:(q + 1) * 128],
                        PS[:, 7 * 512:7 * 512 + 128], AF.Copy))
                sc.wdma("pe", "a_qkv")
                sc.wait("pe", SACT, SACT.n)
                for qm in range(c.nqm):
                    nk = (qm + 1) * c.ktpq
                    q0 = qm * c.qmac
                    for jp in range(0, nk, 2):
                        js = [jp] if jp + 1 >= nk else [jp, jp + 1]
                        sA = (jp // 2) % 2
                        sl2 = (jp // 2) % 2
                        sc.wait("pe", SACT, SACT.n)
                        sc.wait("pe", SDVE, SDVE.n)
                        for ji, j in enumerate(js):
                            sc.op("pe", lambda e, j=j, ji=ji, sA=sA, q0=q0:
                                  e.matmul(
                                      PS[:, sA * 1024 + ji * c.qmac:
                                         sA * 1024 + (ji + 1) * c.qmac],
                                      ksb[:, j * 128:(j + 1) * 128],
                                      qsb[:, q0:q0 + c.qmac],
                                      start=True, stop=True),
                                  incs=((PE, 1),) if ji == len(js) - 1
                                  else ())
                        sc.wait("act", PE, PE.n)
                        sc.wait("act", SDVE, SDVE.n)
                        np_ = len(js) * c.qmac
                        sc.op("act", lambda e, sA=sA, np_=np_, sl2=sl2:
                              e.activation(ptb[sl2][:, :np_],
                                           PS[:, sA * 1024:sA * 1024 + np_],
                                           AF.Exp, scale=scale))
                        for ji, j in enumerate(js):
                            p = j - qm * c.ktpq
                            if p >= 0:
                                sc.wait("dve", SACT, SACT.n)
                                sc.op("dve", lambda e, ji=ji, p=p, sl2=sl2:
                                      e.tensor_tensor(
                                          ptb[sl2][:, ji * c.qmac:
                                                   (ji + 1) * c.qmac],
                                          ptb[sl2][:, ji * c.qmac:
                                                   (ji + 1) * c.qmac],
                                          masks_sb[:, p * c.qmac:
                                                   (p + 1) * c.qmac],
                                          ALU.mult))
                        sc.wait("pe", SACT, SACT.n)
                        sc.wait("pe", SDVE, SDVE.n)
                        for ji, j in enumerate(js):
                            sc.op("pe", lambda e, ji=ji, j=j, nk=nk, sl2=sl2:
                                  e.matmul(
                                      bank(4, w=c.qmac),
                                      vsb[:, j * 128:(j + 1) * 128],
                                      ptb[sl2][:, ji * c.qmac:
                                               (ji + 1) * c.qmac],
                                      start=(j == 0), stop=(j == nk - 1)))
                            sc.op("pe", lambda e, ji=ji, j=j, nk=nk, sl2=sl2:
                                  e.matmul(
                                      bank(5, rows=1, w=c.qmac),
                                      onesb_sb[:],
                                      ptb[sl2][:, ji * c.qmac:
                                               (ji + 1) * c.qmac],
                                      start=(j == 0), stop=(j == nk - 1)),
                                  incs=((PE, 1),) if j == nk - 1 else ())
                    sc.wait("dve", PE, PE.n)
                    sc.op("dve", lambda e: e.reciprocal(
                        z_row[:], bank(5, rows=1, w=c.qmac)))
                    replicate(z_row[:], zrep, 6)
                    sc.wait("dve", SACT, SACT.n)
                    sc.wdma("dve", "st_at")
                    sc.op("dve", lambda e: e.tensor_tensor(
                        atev, bank(4, w=c.qmac), zrep, ALU.mult))
                    jq = 2 * b + (1 if q0 >= c.th else 0)
                    row = jq * c.hbc * 128 + hhl * 128
                    col = q0 % c.th
                    sc.wait("sp", SDVE, SDVE.n)
                    sc.dma("sp", at_in[row:row + 128, col:col + c.qmac],
                           atev, "st_at")
                    sc.wait("pe", SDVE, SDVE.n)
        sc.wdma("pool", "st_at")

        def _a2a_at(e):
            with allow_ici_all_to_all():
                return e.collective_compute(
                    "AllToAll", ALU.bypass, replica_groups=[list(range(8))],
                    ins=[at_in.tensor.ap().opt()],
                    outs=[at_out.tensor.ap().opt()])
        sc.op("pool", _a2a_at, incs=((CC3, 1),), chain=False)
        if c.debug_taps:
            sc.wait("pool", CC3, CC3.n)
            sc.dma("pool", taps["t_at"], at_out, "st_tap")
        AR.reset()
        sc.barrier()

        AR.mark()
        ae = [AR.alloc(BF16, c.tpc) for _ in range(c.nec)]
        wpan2 = [AR.alloc(BF16, c.nec * 128) for _ in range(2)]
        x3c = AR.alloc(F32, c.tpc)
        x4c = AR.alloc(F32, c.tpc)
        sc.wait("sp", CC3, 1)
        for k in range(c.nec):
            sc.dma("sp", ae[k], at_out[k * 128:(k + 1) * 128, :], "c_ae")
        osnap = {}
        dsnap = {}
        for o in range(c.ndc):
            sl = o % 2
            b0 = sl * 4
            if o >= 2:
                sc.wait("pool", PE, osnap[o - 2])
            sc.dma("pool", wpan2[sl], wo_p[o, :, :], f"c_w{sl}")
            sc.wdma("pe", "c_ae")
            sc.wdma("pe", f"c_w{sl}")
            if o >= 2:
                sc.wait("pe", SDVE, dsnap[o - 2])
            for ec in range(c.nec):
                for wv in range(nwv):
                    sc.op("pe", lambda e, ec=ec, wv=wv, b0=b0, sl=sl:
                          e.matmul(bank(b0 + wv, w=c.sw),
                                   wpan2[sl][:, ec * 128:(ec + 1) * 128],
                                   ae[ec][:, wv * c.sw:(wv + 1) * c.sw],
                                   start=(ec == 0), stop=(ec == c.nec - 1)),
                          incs=((PE, 1),) if (ec == c.nec - 1
                                              and wv == nwv - 1) else ())
            osnap[o] = PE.n
            sc.wait("sp", SDVE, SDVE.n)
            sc.wdma("sp", "st_out")
            sc.dma("sp", x3c, x3_d[o * 128:(o + 1) * 128, :], "c_x3")
            sc.wait("dve", PE, PE.n)
            sc.wdma("dve", "c_x3", "st_x4")
            for wv in range(nwv):
                sc.op("dve", lambda e, wv=wv, b0=b0: e.tensor_tensor(
                    x4c[:, wv * c.sw:(wv + 1) * c.sw],
                    bank(b0 + wv, w=c.sw),
                    x3c[:, wv * c.sw:(wv + 1) * c.sw], ALU.add))
            dsnap[o] = SDVE.n
            sc.wait("sp", SDVE, SDVE.n)
            sc.dma("sp", x4_d[o * 128:(o + 1) * 128, :], x4c, "st_x4")
        if c.debug_taps:
            sc.wdma("pool", "st_x4")
            sc.dma("pool", taps["t_x4"], x4_d, "st_tap")
        AR.reset()
        sc.barrier()

        sc.wdma("sp", "st_x4")
        ffn_phase(x4_d, out_xT, None)

        for nm, s in list(sc.sems.items()):
            if nm.startswith("dma_"):
                sc.wait("pool", s)
        sc.barrier()

        block = es.enter_context(nc.Block())
        sc.replay(block)

    return nc, taps



def _make_consts(c):
    import ml_dtypes
    ident = np.eye(128, dtype=np.float32)
    masks = np.zeros((128, c.ktpq * c.qmac), dtype=np.float32)
    for p in range(c.ktpq):
        k = np.arange(128)[:, None]
        q = np.arange(c.qmac)[None, :]
        masks[:, p * c.qmac:(p + 1) * c.qmac] = (128 * p + k <= q)
    return {
        "ident": ident,
        "ones_b": np.ones((128, 1), np.float32).astype(ml_dtypes.bfloat16),
        "ones_f": np.ones((128, 1), np.float32),
        "onesr_f": np.ones((1, 128), np.float32),
        "masks": masks.astype(ml_dtypes.bfloat16),
    }


_NC_CACHE = {}


def kernel(L0_attn_a_x, L0_attn_a_A_log, L0_attn_a_D, L0_attn_a_x_proj_w,
           L0_attn_a_dt_proj_w, L0_attn_a_dt_proj_b, L0_attn_b_wq,
           L0_attn_b_wkv, L0_attn_b_wo, L0_ffn_gate_w, L0_ffn_up_w,
           L0_ffn_down_w, L0_ffn_fc1_w, L0_ffn_fc2_w, L0_ffn_expert_w,
           L0_ffn_ln_gamma, L0_ffn_ln_beta):
    from concourse.bass_utils import run_bass_kernel_spmd

    c = Cfg(d_model=4096, seq=4096, batch=4, ffn_int=8192, ffn_lat=2048,
            relax=True)
    x = np.asarray(L0_attn_a_x, dtype=np.float32)
    xT = np.ascontiguousarray(x.reshape(c.tokens, c.d_model).T)

    f32 = lambda a: np.ascontiguousarray(np.asarray(a, dtype=np.float32))
    wq = f32(L0_attn_b_wq)
    wkv = f32(L0_attn_b_wkv)
    wq_u = np.concatenate(
        [wq[:, i * c.dq:i * c.dq + c.dn] for i in range(c.heads)], axis=1)
    wk_u = np.concatenate(
        [wkv[:, i * c.dn:(i + 1) * c.dn] for i in range(c.heads)], axis=1)
    wv_u = np.concatenate(
        [wkv[:, c.heads * c.dn + i * c.dv:c.heads * c.dn + (i + 1) * c.dv]
         for i in range(c.heads)], axis=1)
    wqkv_u = np.ascontiguousarray(np.concatenate([wq_u, wk_u, wv_u], axis=1))
    consts = _make_consts(c)

    A_log = f32(L0_attn_a_A_log)
    D = f32(L0_attn_a_D)
    dtw = f32(L0_attn_a_dt_proj_w)
    dtb = f32(L0_attn_a_dt_proj_b)
    shared = {
        "xpw": f32(L0_attn_a_x_proj_w),
        "wqkv_p": np.ascontiguousarray(
            wqkv_u.reshape(c.ndc, 128, 3 * c.heads, 128)
            .transpose(2, 1, 0, 3).reshape(3 * c.heads, 128, c.ndc * 128)),
        "wo_p": np.ascontiguousarray(
            f32(L0_attn_b_wo).reshape(c.nec, 128, c.ndc, 128)
            .transpose(2, 1, 0, 3).reshape(c.ndc, 128, c.nec * 128)),
        "gate_wT": np.ascontiguousarray(f32(L0_ffn_gate_w).T),
        "up_w": f32(L0_ffn_up_w),
        "down_w": f32(L0_ffn_down_w),
        "fc1_w": f32(L0_ffn_fc1_w),
        "fc2_w": f32(L0_ffn_fc2_w),
        "exp_w": f32(L0_ffn_expert_w),
        "ln_g": np.ascontiguousarray(
            f32(L0_ffn_ln_gamma).reshape(c.ndc, 128).T),
        "ln_b": np.ascontiguousarray(
            f32(L0_ffn_ln_beta).reshape(c.ndc, 128).T),
    }
    shared.update(consts)

    maps = []
    for core in range(8):
        sl = slice(c.dsl * core, c.dsl * (core + 1))
        m = dict(shared)
        m["xT_t"] = np.ascontiguousarray(
            xT[:, c.tpc * core:c.tpc * (core + 1)])
        m["xT_d"] = np.ascontiguousarray(xT[sl, :])
        m["A_log_sl"] = np.ascontiguousarray(
            A_log[sl].reshape(c.ndt, 128, c.d_state)
            .transpose(1, 0, 2).reshape(128, c.ndt * c.d_state))
        m["D_sl"] = np.ascontiguousarray(D[sl].reshape(c.ndt, 128).T)
        m["dtw_sl"] = np.ascontiguousarray(
            dtw[sl].reshape(c.ndt, 128, 128)
            .transpose(1, 0, 2).reshape(128, c.ndt * 128))
        m["dtb_sl"] = np.ascontiguousarray(dtb[sl].reshape(c.ndt, 128).T)
        maps.append(m)

    key = (c.d_model, c.seq)
    if key not in _NC_CACHE:
        _NC_CACHE[key] = build(c)
    nc, _ = _NC_CACHE[key]
    res = run_bass_kernel_spmd(nc, maps, core_ids=list(range(8)))
    outT = np.concatenate(
        [np.asarray(r["out_xT"]) for r in res.results], axis=1)
    return np.ascontiguousarray(outT.T).reshape(c.batch, c.seq, c.d_model)


# version 20 (final)
# speedup vs baseline: 1.1220x; 1.0696x over previous
import numpy as np

import math
from contextlib import ExitStack

import concourse.bass as bass
import concourse.mybir as mybir
from concourse.aot_env import allow_ici_all_to_all

F32 = mybir.dt.float32
BF16 = mybir.dt.bfloat16
U8 = mybir.dt.uint8
AF = mybir.ActivationFunctionType
ALU = mybir.AluOpType
AX = mybir.AxisListType

ENGS = ("sp", "act", "dve", "pe", "pool")
SELF_CHAIN = ("act", "dve", "pool")


class Cfg:
    def __init__(self, d_model=4096, seq=4096, batch=4, ffn_int=8192,
                 ffn_lat=2048, debug_taps=False, stop_after=None,
                 relax=True):
        self.stop_after = stop_after
        self.relax = relax
        self.d_model = d_model
        self.seq = seq
        self.batch = batch
        self.ffn_int = ffn_int
        self.ffn_lat = ffn_lat
        self.debug_taps = debug_taps
        self.d_state = 16
        self.dt_rank = 128
        self.heads = 32
        self.dn = 128
        self.dv = 128
        self.dq = 192
        self.n_exp = 8
        self.ln_eps = 1e-5
        assert batch == 4
        self.tokens = batch * seq
        self.tpc = self.tokens // 8
        self.dsl = d_model // 8
        self.ndt = self.dsl // 128
        self.ndc = d_model // 128
        self.th = seq // 2
        assert self.th == self.tpc
        self.hpc = self.heads // 2
        self.hbc = self.heads // 8
        self.nec = self.heads * self.dv // 128
        self.sblk = min(512, self.tpc)
        self.nsblk = self.tpc // self.sblk
        self.sw = min(512, self.tpc)
        self.nst = self.tpc // self.sw
        self.qmac = min(512, seq // 2)
        self.nqm = seq // self.qmac
        self.ktpq = self.qmac // 128
        self.tb = min(512, self.th)
        self.ntb = seq // self.tb
        self.nic = ffn_int // 128
        self.nlc = ffn_lat // 128
        self.nxp = self.dt_rank + 2 * self.d_state
        self.nrc = math.ceil(self.nxp / 128)


class Sem:
    def __init__(self, nc, es, name):
        self.h = es.enter_context(nc.semaphore(name))
        self.n = 0


class Sched:
    def __init__(self, nc, es):
        self.nc = nc
        self.es = es
        self.ops = {e: [] for e in ENGS}
        self.sems = {}
        self.waited = {}
        self.last_wide = {}
        self.relax = False
        for e in SELF_CHAIN:
            self.sems["self_" + e] = Sem(nc, es, "self_" + e)

    def sem(self, name):
        if name not in self.sems:
            self.sems[name] = Sem(self.nc, self.es, name)
        return self.sems[name]

    def wait(self, eng, sem, val=None):
        v = sem.n if val is None else val
        key = (eng, id(sem))
        if v <= 0 or self.waited.get(key, -1) >= v:
            return
        self.waited[key] = v
        self.ops[eng].append(lambda e, s=sem.h, v=v: e.wait_ge(s, v))

    def op(self, eng, fn, incs=(), chain=True, wide=False):
        incs = list(incs)
        if eng in SELF_CHAIN and not chain:
            ss = self.sems["self_" + eng]
            self.wait(eng, ss, ss.n)
        if eng in SELF_CHAIN and chain:
            ss = self.sems["self_" + eng]
            if not (self.relax and eng == "dve" and wide
                    and self.last_wide.get(eng, False)):
                self.wait(eng, ss, ss.n)
            self.last_wide[eng] = wide
            incs.append((ss, 1))
        for s, k in incs:
            s.n += k

        def wrapped(e, fn=fn, incs=tuple(incs)):
            ins = fn(e)
            for s, k in incs:
                ins = ins.then_inc(s.h, k)
            return ins
        self.ops[eng].append(wrapped)

    def dma(self, eng, out, in_, key):
        if eng in SELF_CHAIN:
            self.last_wide[eng] = False
        sem = self.sem("dma_" + key)
        sem.n += 16

        def wrapped(e, out=out, in_=in_, s=sem.h):
            return e.dma_start(out=out, in_=in_).then_inc(s, 16)
        self.ops[eng].append(wrapped)

    def wdma(self, eng, *keys):
        for key in keys:
            if ("dma_" + key) in self.sems:
                self.wait(eng, self.sems["dma_" + key])

    def barrier(self):
        bar = self.sem("bar")
        for e in ENGS:
            bar.n += 1
            self.ops[e].append(lambda en, s=bar.h: en.nop().then_inc(s, 1))
        tgt = bar.n
        for e in ENGS:
            self.ops[e].append(lambda en, s=bar.h, v=tgt: en.wait_ge(s, v))

    def replay(self, block):
        sched = self

        @block.sync
        def _(e):
            for f in sched.ops["sp"]:
                f(e)

        @block.scalar
        def _(e):
            for f in sched.ops["act"]:
                f(e)

        @block.vector
        def _(e):
            for f in sched.ops["dve"]:
                f(e)

        @block.tensor
        def _(e):
            for f in sched.ops["pe"]:
                f(e)

        @block.gpsimd
        def _(e):
            for f in sched.ops["pool"]:
                f(e)


class Arena:
    def __init__(self, nc, es, cols_bf16):
        self.guard_h = es.enter_context(
            nc.sbuf_tensor("arena", [128, cols_bf16], BF16))
        self.hb = self.guard_h
        self.h32 = self.hb.bitcast(F32)
        self.cols = cols_bf16
        self.off = 0
        self.marks = []

    def alloc(self, dtype, cols, rows=128):
        e2 = cols * (2 if dtype == F32 else 1)
        self.off = (self.off + 31) & ~31
        assert self.off + e2 <= self.cols, \
            f"arena overflow {self.off}+{e2}>{self.cols}"
        o = self.off
        self.off += e2
        if dtype == BF16:
            return self.hb[:rows, o:o + cols]
        return self.h32[:rows, o // 2:o // 2 + cols]

    def mark(self):
        self.marks.append(self.off)

    def reset(self):
        self.off = self.marks.pop()


def build(cfg: Cfg):
    c = cfg
    nc = bass.Bass("TRN2", target_bir_lowering=False)

    def din(name, shape, dt=F32):
        return nc.dram_tensor(name, shape, dt, kind="ExternalInput").ap()

    xT_t = din("xT_t", [c.d_model, c.tpc])
    xT_d = din("xT_d", [c.dsl, c.tokens])
    wqkv_p = din("wqkv_p", [3 * c.heads, 128, c.ndc * 128])
    wo_p = din("wo_p", [c.ndc, 128, c.nec * 128])
    A_log_sl = din("A_log_sl", [128, c.ndt * c.d_state])
    D_sl = din("D_sl", [128, c.ndt])
    xpw = din("xpw", [c.nxp, c.d_model])
    dtw_sl = din("dtw_sl", [128, c.ndt * 128])
    dtb_sl = din("dtb_sl", [128, c.ndt])
    gate_wT = din("gate_wT", [c.n_exp, c.d_model])
    up_w = din("up_w", [c.d_model, c.ffn_int])
    down_w = din("down_w", [c.ffn_int, c.d_model])
    fc1_w = din("fc1_w", [c.d_model, c.ffn_lat])
    fc2_w = din("fc2_w", [c.ffn_lat, c.d_model])
    exp_w = din("exp_w", [c.ffn_lat, c.ffn_lat])
    ln_g = din("ln_g", [128, c.ndc])
    ln_b = din("ln_b", [128, c.ndc])
    ident = din("ident", [128, 128], F32)
    ones_b = din("ones_b", [128, 1], BF16)
    ones_f = din("ones_f", [128, 1], F32)
    onesr_f = din("onesr_f", [1, 128], F32)
    masks = din("masks", [128, c.ktpq * c.qmac], BF16)

    out_xT = nc.dram_tensor("out_xT", [c.d_model, c.tpc], F32,
                            kind="ExternalOutput").ap()

    def scr(name, shape, dt):
        return nc.dram_tensor(name, shape, dt).ap()

    nxs = c.nxp + 2
    xp_in = scr("xp_in", [nxs, c.tpc], F32)
    xp_out = scr("xp_out", [8 * nxs, c.tpc], F32)
    y_in = scr("y_in", [c.d_model, c.tpc], F32)
    y_out = scr("y_out", [c.d_model, c.tpc], F32)
    z3_in = scr("z3_in", [c.nsblk, 128, c.ndc * c.sblk], BF16)
    qkv_in = scr("qkv_in", [8 * 3 * c.hbc * 128, c.tpc], BF16)
    qkv_out = scr("qkv_out", [8 * 3 * c.hbc * 128, c.tpc], BF16)
    at_in = scr("at_in", [8 * c.hbc * 128, c.tpc], BF16)
    at_out = scr("at_out", [8 * c.hbc * 128, c.tpc], BF16)
    x3_d = scr("x3_d", [c.d_model, c.tpc], F32)
    x4_d = scr("x4_d", [c.d_model, c.tpc], F32)
    facc_d = scr("facc_d", [c.d_model, 2 * c.sblk], F32)
    up_bf = scr("up_bf", [c.d_model, c.ffn_int], BF16)
    down_bf = scr("down_bf", [c.ffn_int, c.d_model], BF16)
    fc1_bf = scr("fc1_bf", [c.d_model, c.ffn_lat], BF16)
    fc2_bf = scr("fc2_bf", [c.ffn_lat, c.d_model], BF16)
    exp_bf = scr("exp_bf", [c.ffn_lat, c.ffn_lat], BF16)

    def pb_meta(rows, cols):
        kc, ncc = rows // 128, cols // 128
        oct_ = min(8, kc)
        return kc, ncc, oct_, ncc // 2, kc // oct_

    PBW = {}
    for nm, (wbf, rows, cols) in {
        "up": (up_bf, c.d_model, c.ffn_int),
        "down": (down_bf, c.ffn_int, c.d_model),
        "fc1": (fc1_bf, c.d_model, c.ffn_lat),
        "fc2": (fc2_bf, c.ffn_lat, c.d_model),
        "exp": (exp_bf, c.ffn_lat, c.ffn_lat),
    }.items():
        kc, ncc, oct_, npj, nkq = pb_meta(rows, cols)
        pb = scr(nm + "_pb", [npj * nkq, 128, oct_ * 256], BF16)
        PBW[nm] = (wbf, pb, kc, ncc, oct_, npj, nkq, cols)

    taps = {}
    if c.debug_taps:
        for nm, shape, dt in [
            ("t_xp", [8 * nxs, c.tpc], F32),
            ("t_x2", [c.d_model, c.tpc], F32),
            ("t_x3", [c.d_model, c.tpc], F32),
            ("t_z3", [c.d_model, c.tpc], BF16),
            ("t_at", [8 * c.hbc * 128, c.tpc], BF16),
            ("t_x4", [c.d_model, c.tpc], F32),
        ]:
            taps[nm] = nc.dram_tensor(nm, shape, dt,
                                      kind="ExternalOutput").ap()

    with ExitStack() as es:
        sc = Sched(nc, es)
        sc.relax = getattr(cfg, "relax", False)
        E = lambda *a: es.enter_context(nc.sbuf_tensor(*a))

        id32 = E("id32", [128, 128], F32)
        onesb_sb = E("onesb_sb", [128, 1], BF16)
        onesf_sb = E("onesf_sb", [128, 1], F32)
        onesr_sb = E("onesr_sb", [1, 128], F32)
        onesrb_sb = E("onesrb_sb", [1, 128], BF16)
        idb_sb = E("idb_sb", [128, 128], BF16)
        masks_sb = E("masks_sb", [128, c.ktpq * c.qmac], BF16)
        A_sb = E("A_sb", [128, c.ndt * c.d_state], F32)
        D_sb = E("D_sb", [128, c.ndt], F32)
        dtb_sb = E("dtb_sb", [128, c.ndt], F32)
        dtwT_sb = E("dtwT_sb", [128, c.ndt * 128], BF16)
        xpwT_sb = E("xpwT_sb", [128, c.ndc * c.nrc * 128], BF16)
        ncs_sb = E("ncs_sb", [128, c.nrc], F32)
        gwT_sb = E("gwT_sb", [128, c.ndc * c.n_exp], BF16)
        lng_sb = E("lng_sb", [128, c.ndc], F32)
        lnb_sb = E("lnb_sb", [128, c.ndc], F32)
        carry_sb = E("carry_sb", [128, max(c.ndt * c.d_state, 16)], F32)
        eps1_sb = E("eps1_sb", [128, 1], F32)
        one1_sb = E("one1_sb", [128, 1], F32)
        mu_row = E("mu_row", [1, c.sw], F32)
        rs_row = E("rs_row", [1, c.sw], F32)
        ms_row = E("ms_row", [1, c.sw], F32)
        z_row = E("z_row", [1, c.qmac], F32)
        m_row = E("m_row", [1, c.sblk], F32)
        msct = E("msct", [max(c.sblk // 128, 1), 128], F32)
        eqm = E("eqm", [128, c.n_exp], U8)

        AR = Arena(nc, es, 88064)

        PS = es.enter_context(nc.psum_tensor("PS", [128, 4096], F32))

        def bank(i, rows=128, w=512):
            return PS[:rows, 512 * i:512 * i + w]

        LD = sc.sem("ld")
        CV = sc.sem("cv")
        PE = sc.sem("pe")
        ST = sc.sem("st")
        CONV = sc.sem("conv")
        CC0, CC1, CC2, CC3 = (sc.sem(f"cc{i}") for i in range(4))
        SDVE = sc.sems["self_dve"]
        SACT = sc.sems["self_act"]
        SPOOL = sc.sems["self_pool"]

        sc.dma("sp", id32[:], ident, "p0")
        sc.dma("sp", onesb_sb[:], ones_b, "p0")
        sc.dma("sp", onesf_sb[:], ones_f, "p0")
        sc.dma("sp", onesr_sb[:], onesr_f, "p0")
        sc.dma("sp", masks_sb[:], masks, "p0")
        sc.dma("sp", D_sb[:], D_sl, "p0")
        sc.dma("sp", dtb_sb[:], dtb_sl, "p0")
        sc.dma("sp", lng_sb[:], ln_g, "p0")
        sc.dma("sp", lnb_sb[:], ln_b, "p0")

        AR.mark()
        tmpA = AR.alloc(F32, c.ndt * c.d_state)
        stage = AR.alloc(F32, max(c.ndt * 128, c.d_model))

        sc.dma("sp", tmpA, A_log_sl, "p0")
        sc.op("dve", lambda e: e.memset(eps1_sb[:], float(c.ln_eps)))
        sc.wait("dve", LD, LD.n) if False else None
        sc.op("dve", lambda e: e.memset(onesrb_sb[:], 1.0))
        sc.wdma("dve", "p0")
        sc.op("dve", lambda e: e.tensor_copy(idb_sb[:], id32[:]))
        sc.op("dve", lambda e: e.memset(one1_sb[:], 1.0))
        sc.wdma("act", "p0")
        sc.op("act", lambda e: e.activation(tmpA, tmpA, AF.Exp))
        sc.wait("dve", SACT, SACT.n)
        sc.op("dve", lambda e: e.tensor_scalar_mul(A_sb[:], tmpA, -1.0))

        ntr = [0]

        def tr128(src, dst, ocols):
            bi = 6 + ntr[0] % 2
            sc.wait("pe", SACT, SACT.n)
            sc.op("pe", lambda e, s=src, bi=bi, oc=ocols: e.transpose(
                PS[:, bi * 512:bi * 512 + oc], s, id32[:oc, :oc]
                if False else id32[:s.shape[0], :s.shape[0]]),
                incs=((PE, 1),))
            sc.wait("act", PE, PE.n)
            sc.op("act", lambda e, bi=bi, d=dst, oc=ocols: e.activation(
                d, PS[:d.shape[0], bi * 512:bi * 512 + oc], AF.Copy))
            ntr[0] += 1

        sc.wdma("pe", "p0")
        sc.dma("sp", stage[:, :c.ndt * 128], dtw_sl, "p0s")
        sc.wdma("pe", "p0s")
        for a in range(c.ndt):
            tr128(stage[:, a * 128:(a + 1) * 128],
                  dtwT_sb[:, a * 128:(a + 1) * 128], 128)

        for j in range(c.nrc):
            r0 = 128 * j
            rw = min(128 * (j + 1), c.nxp) - r0
            sc.wait("sp", SACT, SACT.n)
            sc.dma("sp", stage[:rw, :c.d_model], xpw[r0:r0 + rw, :], "p0s")
            sc.wdma("pe", "p0s")
            for k in range(c.ndc):
                tr128(stage[:rw, k * 128:(k + 1) * 128],
                      xpwT_sb[:, (k * c.nrc + j) * 128:
                              (k * c.nrc + j) * 128 + rw], rw)
            sc.wait("pe", SACT, SACT.n)
            for k in range(c.ndc):
                o = (k * c.nrc + j) * 128
                sc.op("pe", lambda e, o=o, rw=rw, k=k: e.matmul(
                    PS[:rw, :1], xpwT_sb[:, o:o + rw], onesb_sb[:],
                    start=(k == 0), stop=(k == c.ndc - 1)),
                    incs=((PE, 1),) if k == c.ndc - 1 else ())
            sc.wait("dve", PE, PE.n)
            sc.op("dve", lambda e, j=j, rw=rw: e.tensor_scalar_mul(
                ncs_sb[:rw, j:j + 1], PS[:rw, :1], -1.0))
            sc.wait("pe", SDVE, SDVE.n)

        sc.wait("sp", SACT, SACT.n)
        sc.dma("sp", stage[:c.n_exp, :c.d_model], gate_wT, "p0s")
        sc.wdma("pe", "p0s")
        for k in range(c.ndc):
            tr128(stage[:c.n_exp, k * 128:(k + 1) * 128],
                  gwT_sb[:, k * c.n_exp:(k + 1) * c.n_exp], c.n_exp)

        def conv(dst, src, rows, rchunk):
            for r0 in range(0, rows, rchunk):
                r1 = min(r0 + rchunk, rows)
                sc.dma("pool", dst[r0:r1, :], src[r0:r1, :], "conv")

        conv(up_bf, up_w, c.d_model, 512)
        conv(down_bf, down_w, c.ffn_int, 512)
        conv(fc1_bf, fc1_w, c.d_model, 1024)
        conv(fc2_bf, fc2_w, c.ffn_lat, 1024)
        conv(exp_bf, exp_w, c.ffn_lat, 1024)

        def _mk_reblock():
            thunks = []
            for nm, (wbf, pb, kc, ncc, oct_, npj, nkq, cols) in PBW.items():
                for j in range(npj):
                    for kq in range(nkq):
                        for a in range(oct_):
                            def th(j=j, kq=kq, a=a, wbf=wbf, pb=pb,
                                   oct_=oct_, nkq=nkq, cols=cols):
                                r0 = (kq * oct_ + a) * 128
                                sc.dma("act",
                                       pb[j * nkq + kq,
                                          :, a * 256:(a + 1) * 256],
                                       wbf[r0:r0 + 128,
                                           j * 256:(j + 1) * 256], "rbk")
                            thunks.append(th)
            return thunks

        reblock_thunks = _mk_reblock()
        rb_i = [0]

        def emit_reblocks(k):
            if rb_i[0] == 0 and reblock_thunks:
                sc.wdma("act", "conv")
            n = min(k, len(reblock_thunks) - rb_i[0])
            for _ in range(n):
                reblock_thunks[rb_i[0]]()
                rb_i[0] += 1

        AR.reset()
        sc.barrier()

        def ln_finalize(mu_o, rs_o, ms_o, w, d_total, bs, bq):
            sc.wait("dve", PE, PE.n)
            inv = 1.0 / d_total
            sc.op("dve", lambda e: e.tensor_scalar_mul(
                mu_o, bank(bs, rows=1, w=w), inv))
            sc.op("dve", lambda e: e.tensor_scalar_mul(
                rs_o, bank(bq, rows=1, w=w), inv))
            sc.op("dve", lambda e: e.tensor_tensor(ms_o, mu_o, mu_o, ALU.mult))
            sc.op("dve", lambda e: e.tensor_tensor(
                rs_o, rs_o, ms_o, ALU.subtract))
            sc.wait("act", SDVE, SDVE.n)
            sc.op("act", lambda e: e.activation(
                rs_o, rs_o, AF.Ln, bias=eps1_sb[:1, :]))
            sc.op("act", lambda e: e.activation(
                rs_o, rs_o, AF.Exp, scale=-0.5))
            sc.wait("dve", SACT, SACT.n)
            sc.op("dve", lambda e: e.tensor_tensor(ms_o, mu_o, rs_o, ALU.mult))

        def replicate(row_ap, dst, pbank):
            w = row_ap.shape[-1]
            sc.wait("pe", SDVE, SDVE.n)
            sc.wait("pe", SACT, SACT.n)
            sc.op("pe", lambda e, r=row_ap, w=w: e.matmul(
                bank(pbank, w=w), onesr_sb[:], r, start=True, stop=True),
                incs=((PE, 1),))
            sc.wait("act", PE, PE.n)
            sc.op("act", lambda e, d=dst, w=w: e.activation(
                d, bank(pbank, w=w), AF.Copy))

        AR.mark()
        xchk = [AR.alloc(F32, c.sw) for _ in range(2)]
        sqchb = AR.alloc(BF16, c.sw)
        xbfs = [AR.alloc(BF16, c.sw) for _ in range(c.ndc)]
        rep_r = AR.alloc(F32, c.sw)
        rep_m = AR.alloc(F32, c.sw)
        t1b = AR.alloc(F32, c.sw)
        xpo = [AR.alloc(F32, c.sw) for _ in range(c.nrc)]

        for w in range(c.nst):
            cw = slice(c.sw * w, c.sw * (w + 1))
            pe_chunk = []
            sc.wdma("dve", "st_xp")
            for k in range(c.ndc):
                buf = xchk[k % 2]
                key = f"xc{k % 2}"
                if k >= 2:
                    sc.wait("sp", PE, pe_chunk[k - 2])
                    sc.wait("sp", SDVE, SDVE.n)
                sc.dma("sp", buf, xT_t[k * 128:(k + 1) * 128, cw], key)
                sc.wdma("dve", key)
                if w > 0 and k == 0:
                    sc.wait("dve", PE, PE.n)
                sc.op("dve", lambda e, b=buf, k=k: e.tensor_copy(xbfs[k], b))
                sc.wait("act", SDVE, SDVE.n)
                sc.wait("act", PE, PE.n)
                sc.op("act", lambda e, k=k: e.activation(
                    sqchb, xbfs[k], AF.Square))
                sc.wait("pe", SACT, SACT.n)
                sc.op("pe", lambda e, k=k: e.matmul(
                    bank(0, rows=1, w=c.sw), onesb_sb[:], xbfs[k],
                    start=(k == 0), stop=(k == c.ndc - 1)))
                sc.op("pe", lambda e, k=k: e.matmul(
                    bank(1, rows=1, w=c.sw), onesb_sb[:], sqchb,
                    start=(k == 0), stop=(k == c.ndc - 1)), incs=((PE, 1),))
                pe_chunk.append(PE.n)
            ln_finalize(mu_row[:], rs_row[:], ms_row[:], c.sw, c.d_model, 0, 1)
            replicate(rs_row[:], rep_r, 2)
            replicate(ms_row[:], rep_m, 3)
            sc.wait("pe", SDVE, SDVE.n)
            for j in range(c.nrc):
                rw = min(128 * (j + 1), c.nxp) - 128 * j
                for k in range(c.ndc):
                    o = (k * c.nrc + j) * 128
                    sc.op("pe", lambda e, o=o, rw=rw, k=k, j=j: e.matmul(
                        bank(4 + j, rows=rw, w=c.sw), xpwT_sb[:, o:o + rw],
                        xbfs[k], start=(k == 0), stop=(k == c.ndc - 1)),
                        incs=((PE, 1),) if k == c.ndc - 1 else ())
            sc.wait("dve", PE, PE.n)
            sc.wait("dve", SACT, SACT.n)
            for j in range(c.nrc):
                rw = min(128 * (j + 1), c.nxp) - 128 * j
                sc.op("dve", lambda e, j=j, rw=rw: e.tensor_tensor(
                    t1b[:rw, :], bank(4 + j, rows=rw, w=c.sw),
                    rep_r[:rw, :], ALU.mult))
                sc.op("dve", lambda e, j=j, rw=rw: e.scalar_tensor_tensor(
                    xpo[j][:rw, :], rep_m[:rw, :], ncs_sb[:rw, j:j + 1],
                    t1b[:rw, :], ALU.mult, ALU.add))
                sc.wait("sp", SDVE, SDVE.n)
                sc.dma("sp", xp_in[128 * j:128 * j + rw, cw], xpo[j][:rw, :],
                       "st_xp")
            sc.wait("sp", SDVE, SDVE.n)
            sc.dma("sp", xp_in[c.nxp:c.nxp + 1, cw], ms_row[:], "st_xp")
            sc.dma("sp", xp_in[c.nxp + 1:c.nxp + 2, cw], rs_row[:], "st_xp")
            sc.wait("pe", SDVE, SDVE.n)

        sc.wdma("pool", "st_xp")
        sc.op("pool", lambda e: e.collective_compute(
            "AllGather", ALU.bypass, replica_groups=[list(range(8))],
            ins=[xp_in.tensor.ap().opt()], outs=[xp_out.tensor.ap().opt()]),
            incs=((CC0, 1),), chain=False)
        if c.debug_taps:
            sc.wait("pool", CC0, CC0.n)
            sc.dma("pool", taps["t_xp"], xp_out, "st_tap")
        AR.reset()
        sc.barrier()

        AR.mark()
        TH = c.th
        nsw = TH // c.sw
        xpd_bf = AR.alloc(BF16, TH)
        xpd_f32 = AR.alloc(F32, TH)
        mrep = AR.alloc(F32, TH)
        rrep = AR.alloc(F32, TH)
        dt4 = [AR.alloc(F32, TH) for _ in range(c.ndt)]
        dtx4 = [AR.alloc(BF16, TH) for _ in range(c.ndt)]
        dAb = [AR.alloc(F32, TH) for _ in range(2)]
        _dbx = AR.alloc(BF16, TH)
        dBxb = [_dbx, _dbx]
        _h = AR.alloc(BF16, TH)
        hbuf = [_h, _h]
        hcb = [AR.alloc(BF16, TH) for _ in range(2)]
        Brep = [AR.alloc(F32, TH) for _ in range(2)]
        Crep = [AR.alloc(F32, TH) for _ in range(2)]
        _xs = AR.alloc(F32, TH)
        xsl = [_xs, _xs]
        scr1 = dAb[0]
        scr2m = dAb[1]

        npass = (c.ndt + 1) // 2
        for b in range(c.batch):
            for th in range(2):
                r = 2 * b + th
                rb = r * nxs
                tok0 = b * c.seq + th * TH
                sc.wait("sp", CC0, 1)
                sc.wait("sp", SDVE, SDVE.n)
                sc.dma("sp", mrep,
                       xp_out[rb + c.nxp:rb + c.nxp + 1, :]
                       .broadcast_to([128, TH]), "m_mr")
                sc.dma("sp", rrep,
                       xp_out[rb + c.nxp + 1:rb + c.nxp + 2, :]
                       .broadcast_to([128, TH]), "m_mr")
                sc.wait("sp", SACT, SACT.n)
                sc.dma("sp", xpd_f32, xp_out[rb:rb + 128, :], "m_xpd")
                sc.wdma("act", "m_xpd")
                sc.wait("act", PE, PE.n)
                sc.op("act", lambda e: e.activation(
                    xpd_bf, xpd_f32, AF.Copy), wide=True)
                for a in range(c.ndt):
                    sc.wait("pe", SACT, SACT.n)
                    sc.wait("pe", SDVE, SDVE.n)
                    for wv in range(nsw):
                        sc.op("pe", lambda e, a=a, wv=wv: e.matmul(
                            bank(wv, w=c.sw),
                            dtwT_sb[:, a * 128:(a + 1) * 128],
                            xpd_bf[:, c.sw * wv:c.sw * (wv + 1)],
                            start=True, stop=True),
                            incs=((PE, 1),) if wv == nsw - 1 else ())
                    sc.wait("act", PE, PE.n)
                    sc.wait("act", SDVE, SDVE.n)
                    sc.op("act", lambda e, a=a: e.activation(
                        scr1, PS[:, :TH], AF.Exp, bias=dtb_sb[:, a:a + 1]), wide=True)
                    sc.op("act", lambda e, a=a: e.activation(
                        dt4[a], scr1, AF.Ln, bias=one1_sb[:]), wide=True)
                for a in range(c.ndt):
                    xb = xsl[a % 2]
                    key = f"m_x{a % 2}"
                    sc.wait("sp", SDVE, SDVE.n)
                    sc.dma("sp", xb,
                           xT_d[a * 128:(a + 1) * 128, tok0:tok0 + TH], key)
                    sc.wdma("dve", key)
                    sc.wdma("dve", "m_mr")
                    sc.wait("dve", SACT, SACT.n)
                    sc.op("dve", lambda e, xb=xb: e.tensor_tensor(
                        scr1, xb, rrep, ALU.mult), wide=True)
                    sc.op("dve", lambda e: e.tensor_tensor(
                        scr1, scr1, mrep, ALU.subtract), wide=True)
                    sc.op("dve", lambda e, a=a: e.tensor_tensor(
                        dtx4[a], dt4[a], scr1, ALU.mult), wide=True)
                for half in range(npass):
                    aa = [a for a in (2 * half, 2 * half + 1) if a < c.ndt]
                    snap_dA = {}
                    snap_scan = {}
                    snap_ymm = {}
                    base_sdve = SDVE.n
                    base_pe = PE.n
                    sc.wdma("act", "st_y")
                    for n in range(c.d_state):
                        emit_reblocks(12)
                        sl = n % 2
                        rowb = rb + c.dt_rank + n
                        rowc = rb + c.dt_rank + c.d_state + n
                        sc.wait("sp", SDVE, SDVE.n)
                        sc.dma("sp", Brep[sl],
                               xp_out[rowb:rowb + 1, :]
                               .broadcast_to([128, TH]), f"m_bc{sl}")
                        sc.dma("sp", Crep[sl],
                               xp_out[rowc:rowc + 1, :]
                               .broadcast_to([128, TH]), f"m_bc{sl}")
                        for ai, a in enumerate(aa):
                            it = n * len(aa) + ai
                            s2 = it % 2
                            ci = a * c.d_state + n
                            sc.wait("act", SDVE,
                                    snap_scan.get(it - 2, base_sdve))
                            sc.op("act", lambda e, a=a, s2=s2, ci=ci:
                                  e.activation(dAb[s2], dt4[a], AF.Exp,
                                               scale=A_sb[:, ci:ci + 1]),
                                  wide=True)
                            snap_dA[it] = SACT.n
                            sc.wdma("dve", f"m_bc{sl}")
                            sc.op("dve", lambda e, a=a, s2=s2, sl=sl:
                                  e.tensor_tensor(dBxb[s2], dtx4[a],
                                                  Brep[sl], ALU.mult),
                                  wide=True)
                            if th == 1:
                                sc.wait("dve", SACT, snap_dA[it])
                                sc.op("dve", lambda e, s2=s2, ci=ci:
                                      e.scalar_tensor_tensor(
                                          dBxb[s2][:, 0:1], dAb[s2][:, 0:1],
                                          carry_sb[:, ci:ci + 1],
                                          dBxb[s2][:, 0:1],
                                          ALU.mult, ALU.add))
                            sc.wait("dve", SACT, snap_dA[it])
                            sc.op("dve", lambda e, s2=s2: e.tensor_tensor_scan(
                                hbuf[s2], dAb[s2], dBxb[s2], 0.0,
                                ALU.mult, ALU.add), wide=True)
                            snap_scan[it] = SDVE.n
                            if th == 0:
                                sc.op("dve", lambda e, s2=s2, ci=ci:
                                      e.tensor_copy(carry_sb[:, ci:ci + 1],
                                                    hbuf[s2][:, TH - 1:TH]))
                            sc.wait("dve", PE, snap_ymm.get(it - 2, base_pe))
                            sc.op("dve", lambda e, s2=s2, sl=sl:
                                  e.tensor_tensor(hcb[s2], hbuf[s2],
                                                  Crep[sl], ALU.mult),
                                  wide=True)
                            sc.wait("pe", SDVE, SDVE.n)
                            for wv in range(nsw):
                                sc.op("pe", lambda e, ai=ai, s2=s2, wv=wv,
                                      n=n: e.matmul(
                                          bank(ai * nsw + wv, w=c.sw),
                                          idb_sb[:],
                                          hcb[s2][:, wv * c.sw:
                                                  (wv + 1) * c.sw],
                                          start=(n == 0),
                                          stop=(n == c.d_state - 1)),
                                      incs=((PE, 1),)
                                      if wv == nsw - 1 else ())
                            snap_ymm[it] = PE.n
                    for ai, a in enumerate(aa):
                        xb = xsl[a % 2]
                        key = f"m_x{a % 2}"
                        sc.wait("sp", SDVE, SDVE.n)
                        sc.dma("sp", xb,
                               xT_d[a * 128:(a + 1) * 128, tok0:tok0 + TH],
                               key)
                        sc.wdma("dve", key)
                        sc.wait("dve", SACT, SACT.n)
                        sc.op("dve", lambda e, xb=xb: e.tensor_tensor(
                            scr1, xb, rrep, ALU.mult), wide=True)
                        sc.op("dve", lambda e: e.tensor_tensor(
                            scr1, scr1, mrep, ALU.subtract), wide=True)
                        sc.op("dve", lambda e, a=a, xb=xb:
                              e.scalar_tensor_tensor(
                                  scr1, scr1, D_sb[:, a:a + 1], xb,
                                  ALU.mult, ALU.add), wide=True)
                        sc.wait("dve", PE, PE.n)
                        sc.op("dve", lambda e, ai=ai: e.tensor_tensor(
                            scr2m, scr1,
                            PS[:, ai * nsw * 512:ai * nsw * 512 + TH],
                            ALU.add), wide=True)
                        sc.wait("sp", SDVE, SDVE.n)
                        sc.dma("sp", y_in[c.dsl * r + 128 * a:
                                          c.dsl * r + 128 * (a + 1), :],
                               scr2m, "st_y")
                        sc.wdma("dve", "st_y")
        emit_reblocks(10 ** 9)
        sc.wdma("pool", "st_y")
        sc.wdma("pool", "conv")

        def _a2a_y(e):
            with allow_ici_all_to_all():
                return e.collective_compute(
                    "AllToAll", ALU.bypass, replica_groups=[list(range(8))],
                    ins=[y_in.tensor.ap().opt()],
                    outs=[y_out.tensor.ap().opt()])
        sc.op("pool", _a2a_y, incs=((CC1, 1),), chain=False)
        if c.debug_taps:
            sc.wait("pool", CC1, CC1.n)
            sc.dma("pool", taps["t_x2"], y_out, "st_tap")
        AR.reset()
        sc.barrier()

        def ffn_phase(x_dram, out_dram, z_dram):
            AR.mark()
            z2 = [AR.alloc(BF16, c.sblk) for _ in range(c.ndc)]
            h2 = [AR.alloc(BF16, c.sblk) for _ in range(c.nic)]
            xlat = [AR.alloc(BF16, c.sblk) for _ in range(c.nlc)]
            moe = [AR.alloc(BF16, c.sblk) for _ in range(c.nlc)]
            pan = [AR.alloc(BF16, 2048) for _ in range(5)]
            x2c = [AR.alloc(F32, c.sblk) for _ in range(2)]
            sqs = AR.alloc(F32, c.sblk)
            sqsb = AR.alloc(BF16, c.sblk)
            repr_ = AR.alloc(F32, c.sblk)
            repm = AR.alloc(F32, c.sblk)
            scr2 = AR.alloc(F32, c.sblk)
            scr3 = AR.alloc(F32, c.sblk)
            gt = AR.alloc(F32, c.sblk)
            lgt = AR.alloc(F32, c.n_exp)
            nbig = AR.alloc(F32, c.n_exp)
            m12 = AR.alloc(F32, 2)
            msc = AR.alloc(F32, max(c.sblk // 128, 1))
            msrep = AR.alloc(BF16, c.sblk)
            fev = [AR.alloc(F32, c.sblk) for _ in range(2)]
            nsub = c.sblk // 128
            pan_snap = {}
            pslot = [0]
            PQF = ("pool", "act")

            def strip_stats(src_dram, cw):
                sc.wdma("dve", "st_z3")
                for k in range(c.ndc):
                    buf = x2c[k % 2]
                    key = f"f_x{k % 2}"
                    if k >= 2:
                        sc.wait("sp", PE, PE.n)
                    sc.wait("sp", SDVE, SDVE.n)
                    sc.dma("sp", buf, src_dram[k * 128:(k + 1) * 128, cw],
                           key)
                    sc.wdma("dve", key)
                    sc.wait("dve", PE, PE.n)
                    sc.op("dve", lambda e, b=buf, k=k: e.tensor_copy(
                        z2[k], b), wide=True)
                    sc.wait("act", SDVE, SDVE.n)
                    sc.wait("act", PE, PE.n)
                    sc.op("act", lambda e, k=k: e.activation(
                        sqsb, z2[k], AF.Square), wide=True)
                    sc.wait("pe", SACT, SACT.n)
                    sc.op("pe", lambda e, k=k: e.matmul(
                        bank(6, rows=1, w=c.sblk), onesb_sb[:], z2[k],
                        start=(k == 0), stop=(k == c.ndc - 1)))
                    sc.op("pe", lambda e, k=k: e.matmul(
                        bank(7, rows=1, w=c.sblk), onesb_sb[:], sqsb,
                        start=(k == 0), stop=(k == c.ndc - 1)),
                        incs=((PE, 1),))
                ln_finalize(mu_row[:, :c.sblk], rs_row[:, :c.sblk],
                            ms_row[:, :c.sblk], c.sblk, c.d_model, 6, 7)
                replicate(rs_row[:, :c.sblk], repr_, 6)
                replicate(ms_row[:, :c.sblk], repm, 7)

            for w in range(c.nsblk):
                cw = slice(c.sblk * w, c.sblk * (w + 1))
                strip_stats(x_dram, cw)
                for k in range(c.ndc):
                    sc.wait("dve", SACT, SACT.n)
                    sc.op("dve", lambda e, k=k: e.tensor_tensor(
                        scr2, z2[k], repr_, ALU.mult), wide=True)
                    sc.op("dve", lambda e, k=k: e.tensor_tensor(
                        z2[k], scr2, repm, ALU.subtract), wide=True)
                sc.wait("pe", SDVE, SDVE.n)
                for k in range(c.ndc):
                    sc.op("pe", lambda e, k=k: e.matmul(
                        bank(0, rows=c.n_exp, w=c.sblk),
                        gwT_sb[:, k * c.n_exp:(k + 1) * c.n_exp], z2[k],
                        start=(k == 0), stop=(k == c.ndc - 1)),
                        incs=((PE, 1),) if k == c.ndc - 1 else ())
                sc.wait("act", PE, PE.n)
                sc.op("act", lambda e: e.activation(
                    gt[:c.n_exp, :], bank(0, rows=c.n_exp, w=c.sblk),
                    AF.Copy))
                sc.wait("pe", SACT, SACT.n)
                for i in range(nsub):
                    sc.op("pe", lambda e, i=i: e.transpose(
                        PS[:, 512 + i * 128:512 + i * 128 + c.n_exp],
                        gt[:c.n_exp, i * 128:(i + 1) * 128],
                        id32[:c.n_exp, :c.n_exp]),
                        incs=((PE, 1),) if i == nsub - 1 else ())
                sc.wait("dve", PE, PE.n)
                sc.op("dve", lambda e: e.memset(nbig, -1e30))
                for i in range(nsub):
                    sc.op("dve", lambda e, i=i: e.tensor_copy(
                        lgt, PS[:, 512 + i * 128:512 + i * 128 + c.n_exp]))
                    sc.op("dve", lambda e: e.reduce_max(
                        m12[:, 0:1], lgt, axis=AX.X))
                    sc.op("dve", lambda e: e.tensor_scalar(
                        eqm[:], lgt, m12[:, 0:1], None, ALU.is_equal))
                    sc.op("dve", lambda e: e.copy_predicated(
                        lgt, eqm[:], nbig))
                    sc.op("dve", lambda e: e.reduce_max(
                        m12[:, 1:2], lgt, axis=AX.X))
                    sc.op("dve", lambda e, i=i: e.tensor_tensor(
                        msc[:, i:i + 1], m12[:, 0:1], m12[:, 1:2], ALU.add))
                sc.wait("pe", SDVE, SDVE.n)
                sc.op("pe", lambda e: e.transpose(
                    PS[:nsub, 512:640], msc[:, :nsub], id32[:, :]),
                    incs=((PE, 1),))
                sc.wait("dve", PE, PE.n)
                sc.wdma("dve", "f_mrow")
                sc.op("dve", lambda e: e.tensor_copy(
                    msct[:nsub, :], PS[:nsub, 512:640]))
                sc.wait("sp", SDVE, SDVE.n)
                sc.wait("sp", PE, PE.n)
                for i in range(nsub):
                    sc.dma("sp", m_row[:, i * 128:(i + 1) * 128],
                           msct[i:i + 1, :], "f_mrow")
                sc.wdma("pe", "f_mrow")
                replicate(m_row[:], msrep, 1)
                for q_ in ("sp", "pool", "act"):
                    sc.wdma(q_, "rbk")
                evac_snap = {}

                def pair_mms(pbname, jj, rhs_list, b0, extra=False,
                             st_first=True, sp_last=True):
                    wbf, pb, kc, ncc, oct_, npj, nkq, cols = PBW[pbname]
                    for kq in range(nkq):
                        sl = pslot[0] % 5
                        q_ = PQF[pslot[0] % 2]
                        if pslot[0] >= 5:
                            sc.wait(q_, PE, pan_snap[pslot[0] - 5])
                        sc.dma(q_, pan[sl][:, :oct_ * 256],
                               pb[jj * nkq + kq, :, :], f"f_p{sl}")
                        sc.wdma("pe", f"f_p{sl}")
                        for a in range(oct_):
                            k = kq * oct_ + a
                            for gi in range(2):
                                o = (a * 2 + gi) * 128
                                st = st_first and (k == 0)
                                sp = sp_last and (k == kc - 1)
                                last = (a == oct_ - 1 and gi == 1)
                                sc.op("pe", lambda e, o=o, gi=gi, sl=sl,
                                      b0=b0, st=st, sp=sp, k=k: e.matmul(
                                          bank(b0 + gi, w=c.sblk),
                                          pan[sl][:, o:o + 128],
                                          rhs_list[k],
                                          start=st, stop=sp),
                                      incs=((PE, 1),) if last else ())
                        pan_snap[pslot[0]] = PE.n
                        pslot[0] += 1

                NPAIR = (c.nic + c.nlc) // 2
                for g in range(NPAIR):
                    b0 = (g % 2) * 2
                    if g >= 2:
                        sc.wait("pe", SACT, evac_snap[g - 2][0])
                        sc.wait("pe", SDVE, evac_snap[g - 2][1])
                    else:
                        sc.wait("pe", SACT, SACT.n)
                        sc.wait("pe", SDVE, SDVE.n)
                    if g < c.nic // 2:
                        pair_mms("up", g, z2, b0)
                        ics = [2 * g, 2 * g + 1]
                    else:
                        jj = g - c.nic // 2
                        pair_mms("fc1", jj, z2, b0)
                        ics = [c.nic + 2 * jj, c.nic + 2 * jj + 1]
                    sc.wait("act", PE, PE.n)
                    sc.wait("act", SDVE, SDVE.n)
                    for gi, ic in enumerate(ics):
                        if ic < c.nic:
                            sc.op("act", lambda e, gi=gi, b0=b0: e.activation(
                                fev[gi], bank(b0 + gi, w=c.sblk), AF.Relu), wide=True)
                            sc.wait("dve", SACT, SACT.n)
                            sc.op("dve", lambda e, ic=ic, gi=gi:
                                  e.tensor_tensor(h2[ic], fev[gi],
                                                  fev[gi], ALU.mult), wide=True)
                        else:
                            sc.op("act", lambda e, ic=ic, b0=b0, gi=gi:
                                  e.activation(xlat[ic - c.nic],
                                               bank(b0 + gi, w=c.sblk),
                                               AF.Copy), wide=True)
                    evac_snap[g] = (SACT.n, SDVE.n)

                for g in range(c.nlc // 2):
                    b0 = (g % 2) * 2
                    if g < 2:
                        sc.wait("pe", SACT, SACT.n)
                        sc.wait("pe", SDVE, SDVE.n)
                    else:
                        sc.wait("pe", SDVE, evac_snap[("e", g - 2)][0])
                    pair_mms("exp", g, xlat, b0)
                    sc.wait("dve", PE, PE.n)
                    for gi in range(2):
                        sc.op("dve", lambda e, g=g, gi=gi, b0=b0:
                              e.tensor_tensor(moe[2 * g + gi],
                                              bank(b0 + gi, w=c.sblk),
                                              msrep, ALU.mult), wide=True)
                    evac_snap[("e", g)] = (SDVE.n,)

                ocnt = [0]
                for g in range(c.ndc // 2):
                    b0 = (g % 2) * 2
                    if g < 2:
                        sc.wait("pe", SACT, SACT.n)
                        sc.wait("pe", SDVE, SDVE.n)
                    else:
                        sc.wait("pe", SACT, evac_snap[("d", g - 2)][0])
                    pair_mms("down", g, h2, b0, st_first=True, sp_last=False)
                    pair_mms("fc2", g, moe, b0, st_first=False, sp_last=True)
                    sc.wait("act", PE, PE.n)
                    for gi in range(2):
                        o = 2 * g + gi
                        sc.wdma("act", "st_fa")
                        sc.op("act", lambda e, gi=gi, b0=b0: e.activation(
                            fev[gi], bank(b0 + gi, w=c.sblk), AF.Copy), wide=True)
                        sc.wait("act", PE, PE.n)
                        sc.op("act", lambda e, gi=gi: e.activation(
                            sqs, fev[gi], AF.Square), wide=True)
                        sc.wait("pe", SACT, SACT.n)
                        st0 = (ocnt[0] == 0)
                        fin = (ocnt[0] == c.ndc - 1)
                        sc.op("pe", lambda e, gi=gi, st0=st0, fin=fin:
                              e.matmul(bank(6, rows=1, w=c.sblk),
                                       onesf_sb[:], fev[gi],
                                       start=st0, stop=fin))
                        sc.op("pe", lambda e, st0=st0, fin=fin: e.matmul(
                            bank(7, rows=1, w=c.sblk), onesf_sb[:], sqs,
                            start=st0, stop=fin), incs=((PE, 1),))
                        ocnt[0] += 1
                        sc.wait("sp", SACT, SACT.n)
                        sc.dma("sp", facc_d[o * 128:(o + 1) * 128,
                                            (w % 2) * c.sblk:
                                            (w % 2 + 1) * c.sblk],
                               fev[gi], "st_fa")
                    evac_snap[("d", g)] = (SACT.n,)
                ln_finalize(mu_row[:, :c.sblk], rs_row[:, :c.sblk],
                            ms_row[:, :c.sblk], c.sblk, c.d_model, 6, 7)
                replicate(rs_row[:, :c.sblk], repr_, 6)
                replicate(ms_row[:, :c.sblk], repm, 7)
                sc.wdma("sp", "st_fa")
                sc.wait("sp", PE, PE.n)
                for k in range(c.ndc):
                    buf = x2c[k % 2]
                    fv = fev[k % 2]
                    key = f"f_x{k % 2}"
                    sc.wait("sp", SDVE, SDVE.n)
                    sc.wait("sp", SACT, SACT.n)
                    sc.dma("sp", buf, x_dram[k * 128:(k + 1) * 128, cw], key)
                    sc.dma("sp", fv, facc_d[k * 128:(k + 1) * 128,
                                            (w % 2) * c.sblk:
                                            (w % 2 + 1) * c.sblk], key)
                    sc.wdma("dve", key)
                    sc.wait("dve", SACT, SACT.n)
                    sc.op("dve", lambda e, fv=fv: e.tensor_tensor(
                        scr2, fv, repr_, ALU.mult), wide=True)
                    sc.op("dve", lambda e: e.tensor_tensor(
                        scr2, scr2, repm, ALU.subtract), wide=True)
                    sc.wait("act", SDVE, SDVE.n)
                    sc.wdma("act", "st_out")
                    sc.op("act", lambda e, k=k: e.activation(
                        scr3, scr2, AF.Identity,
                        bias=lnb_sb[:, k:k + 1], scale=lng_sb[:, k:k + 1]), wide=True)
                    sc.wait("dve", SACT, SACT.n)
                    sc.op("dve", lambda e, buf=buf: e.tensor_tensor(
                        scr3, scr3, buf, ALU.add), wide=True)
                    sc.wait("sp", SDVE, SDVE.n)
                    sc.dma("sp", out_dram[k * 128:(k + 1) * 128, cw],
                           scr3, "st_out")
                if z_dram is not None:
                    sc.wdma("sp", "st_out")
                    sc.wdma("dve", "st_z3")
                    strip_stats(out_dram, cw)
                    for k in range(c.ndc):
                        sc.wait("dve", SACT, SACT.n)
                        sc.op("dve", lambda e, k=k: e.tensor_tensor(
                            scr2, z2[k], repr_, ALU.mult), wide=True)
                        sc.op("dve", lambda e, k=k: e.tensor_tensor(
                            z2[k], scr2, repm, ALU.subtract), wide=True)
                        sc.wait("sp", SDVE, SDVE.n)
                        sc.dma("sp", z_dram[w, :, k * c.sblk:(k + 1) * c.sblk],
                               z2[k], "st_z3")
            AR.reset()

        sc.wait("sp", CC1, 1)
        sc.wait("pool", CC1, 1)
        ffn_phase(y_out, x3_d, z3_in)
        if c.debug_taps:
            sc.wdma("pool", "st_out", "st_z3")
            sc.dma("pool", taps["t_x3"], x3_d, "st_tap")
            sc.dma("pool", taps["t_z3"], z3_in, "st_tap")
        sc.barrier()

        if c.stop_after == "p3":
            sc.barrier()
            block = es.enter_context(nc.Block())
            sc.replay(block)
            return nc, taps
        AR.mark()
        z3r = AR.alloc(BF16, c.nsblk * c.ndc * c.sblk)
        wpan = [AR.alloc(BF16, c.ndc * 128) for _ in range(4)]
        qkvev = AR.alloc(BF16, c.tpc)
        nwv = c.tpc // c.sw
        sc.wdma("sp", "st_z3")
        for wv in range(c.nsblk):
            sc.dma("sp", z3r[:, wv * c.ndc * c.sblk:(wv + 1) * c.ndc * c.sblk],
                   z3_in[wv, :, :], "a_z3")
        NCC = 3 * c.heads
        qsnap = {}
        esnap = {}
        for cc in range(NCC):
            sl = cc % 4
            b0 = (cc % 2) * 4
            if cc >= 4:
                sc.wait("pool", PE, qsnap[cc - 4])
            sc.dma("pool", wpan[sl], wqkv_p[cc, :, :], f"a_w{sl}")
            sc.wdma("pe", f"a_w{sl}")
            sc.wdma("pe", "a_z3")
            if cc >= 2:
                sc.wait("pe", SACT, esnap[cc - 2])
            for k in range(c.ndc):
                for wv in range(nwv):
                    sc.op("pe", lambda e, k=k, wv=wv, b0=b0, sl=sl: e.matmul(
                        bank(b0 + wv, w=c.sw),
                        wpan[sl][:, k * 128:(k + 1) * 128],
                        z3r[:, (wv * c.ndc + k) * c.sblk:
                            (wv * c.ndc + k) * c.sblk + c.sw],
                        start=(k == 0), stop=(k == c.ndc - 1)),
                        incs=((PE, 1),)
                        if (k == c.ndc - 1 and wv == nwv - 1) else ())
            qsnap[cc] = PE.n
            sc.wait("act", PE, PE.n)
            sc.wdma("act", "st_qkv")
            for wv in range(nwv):
                sc.op("act", lambda e, wv=wv, b0=b0: e.activation(
                    qkvev[:, wv * c.sw:(wv + 1) * c.sw],
                    bank(b0 + wv, w=c.sw), AF.Copy))
            esnap[cc] = SACT.n
            m, hg = divmod(cc, c.heads) if False else (cc // c.heads,
                                                       cc % c.heads)
            j = hg // c.hbc
            row = j * 3 * c.hbc * 128 + m * c.hbc * 128 + (hg % c.hbc) * 128
            sc.wait("sp", SACT, SACT.n)
            sc.dma("sp", qkv_in[row:row + 128, :], qkvev, "st_qkv")
        sc.wdma("pool", "st_qkv")

        def _a2a_qkv(e):
            with allow_ici_all_to_all():
                return e.collective_compute(
                    "AllToAll", ALU.bypass, replica_groups=[list(range(8))],
                    ins=[qkv_in.tensor.ap().opt()],
                    outs=[qkv_out.tensor.ap().opt()])
        sc.op("pool", _a2a_qkv, incs=((CC2, 1),), chain=False)
        AR.reset()
        sc.barrier()

        AR.mark()
        qsb = AR.alloc(BF16, c.seq)
        ksb = AR.alloc(BF16, c.seq)
        vsb = AR.alloc(BF16, c.seq)
        vstg = AR.alloc(F32, 128)
        vtT = AR.alloc(BF16, c.seq)
        ptb = [AR.alloc(BF16, 2 * c.qmac) for _ in range(2)]
        zrep = AR.alloc(F32, c.qmac)
        atev = AR.alloc(BF16, c.qmac)
        scale = 1.0 / math.sqrt(c.dq)
        sc.wait("sp", CC2, 1)
        for b in range(c.batch):
            for hhl in range(c.hbc):
                sc.wait("sp", PE, PE.n)
                sc.wait("sp", SDVE, SDVE.n)
                for half in range(2):
                    i = 2 * b + half
                    base = i * 3 * c.hbc * 128
                    sc.dma("sp", qsb[:, half * c.th:(half + 1) * c.th],
                           qkv_out[base + hhl * 128:
                                   base + (hhl + 1) * 128, :], "a_qkv")
                    sc.dma("sp", ksb[:, half * c.th:(half + 1) * c.th],
                           qkv_out[base + (c.hbc + hhl) * 128:
                                   base + (c.hbc + hhl + 1) * 128, :],
                           "a_qkv")
                    sc.dma("sp", vtT[:, half * c.th:(half + 1) * c.th],
                           qkv_out[base + (2 * c.hbc + hhl) * 128:
                                   base + (2 * c.hbc + hhl + 1) * 128, :],
                           "a_qkv")
                sc.wdma("dve", "a_qkv")
                for q in range(c.seq // 128):
                    sc.wait("dve", PE, PE.n)
                    sc.op("dve", lambda e, q=q: e.tensor_copy(
                        vstg, vtT[:, q * 128:(q + 1) * 128]))
                    sc.wait("pe", SDVE, SDVE.n)
                    sc.wait("pe", SACT, SACT.n)
                    sc.op("pe", lambda e, q=q: e.transpose(
                        PS[:, 7 * 512:7 * 512 + 128], vstg, id32[:, :]),
                        incs=((PE, 1),))
                    sc.wait("act", PE, PE.n)
                    sc.op("act", lambda e, q=q: e.activation(
                        vsb[:, q * 128:(q + 1) * 128],
                        PS[:, 7 * 512:7 * 512 + 128], AF.Copy))
                sc.wdma("pe", "a_qkv")
                pairs = []
                for qm in range(c.nqm):
                    nk = (qm + 1) * c.ktpq
                    q0 = qm * c.qmac
                    for jp in range(0, nk, 2):
                        pairs.append((qm, [jp, jp + 1], nk, q0))
                base_act4 = SACT.n
                snap_st = {}
                snap_exp = {}
                snap_mask = {}
                snap_pv = {}

                def emit_st(gi):
                    qm, js, nk, q0 = pairs[gi]
                    sA = gi % 2
                    sc.wait("pe", SACT, snap_exp.get(gi - 2, base_act4))
                    for ji, j in enumerate(js):
                        sc.op("pe", lambda e, j=j, ji=ji, sA=sA, q0=q0:
                              e.matmul(
                                  PS[:, sA * 1024 + ji * c.qmac:
                                     sA * 1024 + (ji + 1) * c.qmac],
                                  ksb[:, j * 128:(j + 1) * 128],
                                  qsb[:, q0:q0 + c.qmac],
                                  start=True, stop=True),
                              incs=((PE, 1),) if ji == len(js) - 1
                              else ())
                    snap_st[gi] = PE.n

                def emit_fin(qm, gi):
                    q0 = qm * c.qmac
                    sc.wait("dve", PE, snap_pv[gi])
                    sc.op("dve", lambda e: e.reciprocal(
                        z_row[:], bank(5, rows=1, w=c.qmac)))
                    replicate(z_row[:], zrep, 6)
                    sc.wait("dve", SACT, SACT.n)
                    sc.wdma("dve", "st_at")
                    sc.op("dve", lambda e: e.tensor_tensor(
                        atev, bank(4, w=c.qmac), zrep, ALU.mult))
                    jq = 2 * b + (1 if q0 >= c.th else 0)
                    row = jq * c.hbc * 128 + hhl * 128
                    col = q0 % c.th
                    sc.wait("sp", SDVE, SDVE.n)
                    sc.dma("sp", at_in[row:row + 128, col:col + c.qmac],
                           atev, "st_at")

                emit_st(0)
                pend = None
                for gi, (qm, js, nk, q0) in enumerate(pairs):
                    sA = gi % 2
                    sc.wait("act", PE, snap_st[gi])
                    mv = snap_mask.get(gi - 2, 0)
                    if mv:
                        sc.wait("act", SDVE, mv)
                    np_ = len(js) * c.qmac
                    sc.op("act", lambda e, sA=sA, np_=np_:
                          e.activation(ptb[sA][:, :np_],
                                       PS[:, sA * 1024:sA * 1024 + np_],
                                       AF.Exp, scale=scale))
                    snap_exp[gi] = SACT.n
                    anym = False
                    for ji, j in enumerate(js):
                        p = j - qm * c.ktpq
                        if p >= 0:
                            sc.wait("dve", SACT, snap_exp[gi])
                            sc.op("dve", lambda e, ji=ji, p=p, sA=sA:
                                  e.tensor_tensor(
                                      ptb[sA][:, ji * c.qmac:
                                              (ji + 1) * c.qmac],
                                      ptb[sA][:, ji * c.qmac:
                                              (ji + 1) * c.qmac],
                                      masks_sb[:, p * c.qmac:
                                               (p + 1) * c.qmac],
                                      ALU.mult))
                            anym = True
                    if anym:
                        snap_mask[gi] = SDVE.n
                    if gi + 1 < len(pairs):
                        emit_st(gi + 1)
                    if pend is not None:
                        emit_fin(*pend)
                        pend = None
                    sc.wait("pe", SACT, snap_exp[gi])
                    sc.wait("pe", SDVE, SDVE.n)
                    for ji, j in enumerate(js):
                        sc.op("pe", lambda e, ji=ji, j=j, nk=nk, sA=sA:
                              e.matmul(
                                  bank(4, w=c.qmac),
                                  vsb[:, j * 128:(j + 1) * 128],
                                  ptb[sA][:, ji * c.qmac:
                                          (ji + 1) * c.qmac],
                                  start=(j == 0), stop=(j == nk - 1)))
                        sc.op("pe", lambda e, ji=ji, j=j, nk=nk, sA=sA:
                              e.matmul(
                                  bank(5, rows=1, w=c.qmac),
                                  onesb_sb[:],
                                  ptb[sA][:, ji * c.qmac:
                                          (ji + 1) * c.qmac],
                                  start=(j == 0), stop=(j == nk - 1)),
                              incs=((PE, 1),) if j == nk - 1 else ())
                    snap_pv[gi] = PE.n
                    if jp_last := (js[-1] + 1 >= nk):
                        pend = (qm, gi)
                if pend is not None:
                    emit_fin(*pend)
        sc.wdma("pool", "st_at")

        def _a2a_at(e):
            with allow_ici_all_to_all():
                return e.collective_compute(
                    "AllToAll", ALU.bypass, replica_groups=[list(range(8))],
                    ins=[at_in.tensor.ap().opt()],
                    outs=[at_out.tensor.ap().opt()])
        sc.op("pool", _a2a_at, incs=((CC3, 1),), chain=False)
        if c.debug_taps:
            sc.wait("pool", CC3, CC3.n)
            sc.dma("pool", taps["t_at"], at_out, "st_tap")
        AR.reset()
        sc.barrier()

        AR.mark()
        ae = [AR.alloc(BF16, c.tpc) for _ in range(c.nec)]
        wpan2 = [AR.alloc(BF16, c.nec * 128) for _ in range(2)]
        x3c = AR.alloc(F32, c.tpc)
        x4c = AR.alloc(F32, c.tpc)
        sc.wait("sp", CC3, 1)
        for k in range(c.nec):
            sc.dma("sp", ae[k], at_out[k * 128:(k + 1) * 128, :], "c_ae")
        osnap = {}
        dsnap = {}
        for o in range(c.ndc):
            sl = o % 2
            b0 = sl * 4
            if o >= 2:
                sc.wait("pool", PE, osnap[o - 2])
            sc.dma("pool", wpan2[sl], wo_p[o, :, :], f"c_w{sl}")
            sc.wdma("pe", "c_ae")
            sc.wdma("pe", f"c_w{sl}")
            if o >= 2:
                sc.wait("pe", SDVE, dsnap[o - 2])
            for ec in range(c.nec):
                for wv in range(nwv):
                    sc.op("pe", lambda e, ec=ec, wv=wv, b0=b0, sl=sl:
                          e.matmul(bank(b0 + wv, w=c.sw),
                                   wpan2[sl][:, ec * 128:(ec + 1) * 128],
                                   ae[ec][:, wv * c.sw:(wv + 1) * c.sw],
                                   start=(ec == 0), stop=(ec == c.nec - 1)),
                          incs=((PE, 1),) if (ec == c.nec - 1
                                              and wv == nwv - 1) else ())
            osnap[o] = PE.n
            sc.wait("sp", SDVE, SDVE.n)
            sc.wdma("sp", "st_out")
            sc.dma("sp", x3c, x3_d[o * 128:(o + 1) * 128, :], "c_x3")
            sc.wait("dve", PE, PE.n)
            sc.wdma("dve", "c_x3", "st_x4")
            for wv in range(nwv):
                sc.op("dve", lambda e, wv=wv, b0=b0: e.tensor_tensor(
                    x4c[:, wv * c.sw:(wv + 1) * c.sw],
                    bank(b0 + wv, w=c.sw),
                    x3c[:, wv * c.sw:(wv + 1) * c.sw], ALU.add))
            dsnap[o] = SDVE.n
            sc.wait("sp", SDVE, SDVE.n)
            sc.dma("sp", x4_d[o * 128:(o + 1) * 128, :], x4c, "st_x4")
        if c.debug_taps:
            sc.wdma("pool", "st_x4")
            sc.dma("pool", taps["t_x4"], x4_d, "st_tap")
        AR.reset()
        sc.barrier()

        sc.wdma("sp", "st_x4")
        ffn_phase(x4_d, out_xT, None)

        for nm, s in list(sc.sems.items()):
            if nm.startswith("dma_"):
                sc.wait("pool", s)
        sc.barrier()

        block = es.enter_context(nc.Block())
        sc.replay(block)

    return nc, taps



def _make_consts(c):
    import ml_dtypes
    ident = np.eye(128, dtype=np.float32)
    masks = np.zeros((128, c.ktpq * c.qmac), dtype=np.float32)
    for p in range(c.ktpq):
        k = np.arange(128)[:, None]
        q = np.arange(c.qmac)[None, :]
        masks[:, p * c.qmac:(p + 1) * c.qmac] = (128 * p + k <= q)
    return {
        "ident": ident,
        "ones_b": np.ones((128, 1), np.float32).astype(ml_dtypes.bfloat16),
        "ones_f": np.ones((128, 1), np.float32),
        "onesr_f": np.ones((1, 128), np.float32),
        "masks": masks.astype(ml_dtypes.bfloat16),
    }


_NC_CACHE = {}


def kernel(L0_attn_a_x, L0_attn_a_A_log, L0_attn_a_D, L0_attn_a_x_proj_w,
           L0_attn_a_dt_proj_w, L0_attn_a_dt_proj_b, L0_attn_b_wq,
           L0_attn_b_wkv, L0_attn_b_wo, L0_ffn_gate_w, L0_ffn_up_w,
           L0_ffn_down_w, L0_ffn_fc1_w, L0_ffn_fc2_w, L0_ffn_expert_w,
           L0_ffn_ln_gamma, L0_ffn_ln_beta):
    from concourse.bass_utils import run_bass_kernel_spmd

    c = Cfg(d_model=4096, seq=4096, batch=4, ffn_int=8192, ffn_lat=2048,
            relax=True)
    x = np.asarray(L0_attn_a_x, dtype=np.float32)
    xT = np.ascontiguousarray(x.reshape(c.tokens, c.d_model).T)

    f32 = lambda a: np.ascontiguousarray(np.asarray(a, dtype=np.float32))
    wq = f32(L0_attn_b_wq)
    wkv = f32(L0_attn_b_wkv)
    wq_u = np.concatenate(
        [wq[:, i * c.dq:i * c.dq + c.dn] for i in range(c.heads)], axis=1)
    wk_u = np.concatenate(
        [wkv[:, i * c.dn:(i + 1) * c.dn] for i in range(c.heads)], axis=1)
    wv_u = np.concatenate(
        [wkv[:, c.heads * c.dn + i * c.dv:c.heads * c.dn + (i + 1) * c.dv]
         for i in range(c.heads)], axis=1)
    wqkv_u = np.ascontiguousarray(np.concatenate([wq_u, wk_u, wv_u], axis=1))
    consts = _make_consts(c)

    A_log = f32(L0_attn_a_A_log)
    D = f32(L0_attn_a_D)
    dtw = f32(L0_attn_a_dt_proj_w)
    dtb = f32(L0_attn_a_dt_proj_b)
    shared = {
        "xpw": f32(L0_attn_a_x_proj_w),
        "wqkv_p": np.ascontiguousarray(
            wqkv_u.reshape(c.ndc, 128, 3 * c.heads, 128)
            .transpose(2, 1, 0, 3).reshape(3 * c.heads, 128, c.ndc * 128)),
        "wo_p": np.ascontiguousarray(
            f32(L0_attn_b_wo).reshape(c.nec, 128, c.ndc, 128)
            .transpose(2, 1, 0, 3).reshape(c.ndc, 128, c.nec * 128)),
        "gate_wT": np.ascontiguousarray(f32(L0_ffn_gate_w).T),
        "up_w": f32(L0_ffn_up_w),
        "down_w": f32(L0_ffn_down_w),
        "fc1_w": f32(L0_ffn_fc1_w),
        "fc2_w": f32(L0_ffn_fc2_w),
        "exp_w": f32(L0_ffn_expert_w),
        "ln_g": np.ascontiguousarray(
            f32(L0_ffn_ln_gamma).reshape(c.ndc, 128).T),
        "ln_b": np.ascontiguousarray(
            f32(L0_ffn_ln_beta).reshape(c.ndc, 128).T),
    }
    shared.update(consts)

    maps = []
    for core in range(8):
        sl = slice(c.dsl * core, c.dsl * (core + 1))
        m = dict(shared)
        m["xT_t"] = np.ascontiguousarray(
            xT[:, c.tpc * core:c.tpc * (core + 1)])
        m["xT_d"] = np.ascontiguousarray(xT[sl, :])
        m["A_log_sl"] = np.ascontiguousarray(
            A_log[sl].reshape(c.ndt, 128, c.d_state)
            .transpose(1, 0, 2).reshape(128, c.ndt * c.d_state))
        m["D_sl"] = np.ascontiguousarray(D[sl].reshape(c.ndt, 128).T)
        m["dtw_sl"] = np.ascontiguousarray(
            dtw[sl].reshape(c.ndt, 128, 128)
            .transpose(1, 0, 2).reshape(128, c.ndt * 128))
        m["dtb_sl"] = np.ascontiguousarray(dtb[sl].reshape(c.ndt, 128).T)
        maps.append(m)

    key = (c.d_model, c.seq)
    if key not in _NC_CACHE:
        _NC_CACHE[key] = build(c)
    nc, _ = _NC_CACHE[key]
    res = run_bass_kernel_spmd(nc, maps, core_ids=list(range(8)))
    outT = np.concatenate(
        [np.asarray(r["out_xT"]) for r in res.results], axis=1)
    return np.ascontiguousarray(outT.T).reshape(c.batch, c.seq, c.d_model)
